# version 05
# baseline (speedup 1.0000x reference)
import numpy as np
import concourse.bass as bass
import concourse.mybir as mybir
from concourse.bass_utils import run_bass_kernel_spmd
from concourse.alu_op_type import AluOpType as Op

F32 = mybir.dt.float32
BF16 = mybir.dt.bfloat16
AF = mybir.ActivationFunctionType
AX = mybir.AxisListType.X

T = 2048
D = 1024
H = 16
DH = 64
NCH = 8
CA = 512
CF = 1024
DEPTH = 6
NIC = 22
WIN = 256

TAP = None
LAYERS = DEPTH
DO_OUTER = True
SECOND = True

ENGS = ("tensor", "vector", "scalar", "sync", "gpsimd")
ARN = 36864


class Bld:
    def __init__(self, nc):
        self.nc = nc
        self.q = {e: [] for e in ENGS}
        self.cnt = {}
        self.sems = {}
        self.lastwait = {}
        self.prev_mm_row = None
        self.spacer = None
        self.nspacer = 0
        self.ninstr = 0

    def sem(self, name, h):
        self.sems[name] = h
        self.cnt[name] = 0

    def bump(self, name, k=1):
        self.cnt[name] += k
        return self.cnt[name]

    def val(self, name):
        return self.cnt[name]

    def op(self, eng, f, waits=(), incs=()):
        s0 = {"vector": "V", "scalar": "S"}.get(eng)
        if s0 is not None and self.cnt.get(s0, 0) > 0:
            waits = [(s0, self.cnt[s0])] + list(waits)
        wl = []
        for s, v in waits:
            if v <= 0:
                continue
            key = (eng, s)
            if self.lastwait.get(key, -1) >= v:
                continue
            self.lastwait[key] = v
            wl.append((s, v))
        for s, k in incs:
            self.bump(s, k)
        sems = self.sems
        self.ninstr += 1 + len(wl)

        def body(e, f=f, wl=tuple(wl), incs=tuple(incs)):
            for s, v in wl:
                e.wait_ge(sems[s], v)
            inst = f(e)
            for s, k in incs:
                inst.then_inc(sems[s], k)
        self.q[eng].append(body)

    def mm(self, out, lhsT, rhs, start, stop, kbase=0, ksize=128,
           waits=(), incs=(), tp=None, sgc=False):
        rows = 128 if ksize > 64 else (64 if ksize > 32 else 32)
        cur = (rows, kbase) if rows < 128 else None
        if cur is not None and self.prev_mm_row is not None and self.prev_mm_row != cur:
            so, sl, sr = self.spacer
            self.nspacer += 1
            import traceback
            fr = traceback.extract_stack()[-3]
            print("SPACER", self.prev_mm_row, "->", cur, "at", fr.lineno)
            self.ninstr += 1
            self.q["tensor"].append(
                lambda t, so=so, sl=sl, sr=sr: t.matmul(
                    so, sl, sr, start=False, stop=True, skip_group_check=True))
        self.prev_mm_row = cur
        self.op("tensor",
                lambda t, out=out, lhsT=lhsT, rhs=rhs, start=start, stop=stop, tp=tp,
                sgc=sgc: t.matmul(out, lhsT, rhs, start=start, stop=stop,
                                  tile_position=tp, skip_group_check=sgc),
                waits=waits, incs=incs)

    def finalize(self, blk):
        for e in ENGS:
            ops = self.q[e]
            if not ops:
                continue

            def body(eng, ops=ops):
                for f in ops:
                    f(eng)
            getattr(blk, e)(body)


def _arena_offsets():
    AO = {}
    off = 0
    for name, n in (("qtok", 4096), ("ktok", 4096), ("vtok", 6144), ("qfm", 4096),
                    ("kfm", 6144), ("pnbuf", 4096), ("ptbuf", 2048),
                    ("sqs", 2048), ("rott", 2048), ("abuf", 2048)):
        AO[name] = off
        off += n
    assert off <= ARN, off
    AO["p"] = AO["qtok"]
    AO["gact"] = AO["pnbuf"]
    AO["aout"] = AO["qtok"]
    return AO


AO = _arena_offsets()


def host_tables():
    t = np.arange(T, dtype=np.float64)
    j = np.arange(32, dtype=np.float64)
    inv_freq = 1.0 / (10000.0 ** (2.0 * j / DH))
    freqs = t[:, None] * inv_freq[None, :]
    base_scale = (2.0 * j + 0.4 * DH) / (1.4 * DH)
    power = (t - T // 2) / (WIN // 2)
    sc = base_scale[None, :] ** power[:, None]
    cos = np.cos(freqs)
    sin = np.sin(freqs)

    def tok(arr):
        return np.ascontiguousarray(
            arr.reshape(16, 128, 64).transpose(1, 0, 2)).astype(np.float32)

    cq = np.concatenate([cos * sc, cos * sc], axis=1)
    sq = np.concatenate([-sin * sc, sin * sc], axis=1)
    inv = 1.0 / sc
    ck = np.concatenate([cos * inv, cos * inv], axis=1)
    sk = np.concatenate([-sin * inv, sin * inv], axis=1)
    tabs = np.stack([tok(x) for x in (cq, sq, ck, sk)])

    LT = (np.arange(128)[:, None] <= np.arange(128)[None, :]).astype(np.float32)
    BIG = 1e9
    Rs = []
    for w0 in (False, True):
        for isub in (0, 1):
            i = 128 * isub + np.arange(128)[:, None]
            jj = np.arange(512)[None, :]
            d = i + 256 - jj
            valid = (d >= 0) & (d <= 256)
            if w0:
                valid &= (jj >= 256)
            M = np.where(valid, 0.0, -BIG).astype(np.float32)
            R = np.empty_like(M)
            R[0] = M[0]
            R[1:] = M[1:] - M[:-1]
            Rs.append(R)
    maskR = np.stack(Rs)
    UT = (np.arange(128)[:, None] <= np.arange(128)[None, :]).astype(np.float32)
    return tabs, LT, maskR, UT


def apv(tensor, pbase, npart, freeoff, dims):
    epp = 1
    for s in tensor.shape[1:]:
        epp *= s
    return bass.AP(tensor, pbase * epp + freeoff, [[epp, npart]] + [list(d) for d in dims])


_AR = [None]


def aap(name, pbase, npart, extra, dims):
    return apv(_AR[0], pbase, npart, AO[name] + extra, dims)


def build_program(verbose=False):
    nc = bass.Bass("TRN2", target_bir_lowering=False)
    E = {"nc": nc}
    E["dx"] = nc.dram_tensor("x", [D, T], F32, kind="ExternalInput").ap()
    E["dwqkv"] = nc.dram_tensor("wqkv", [DEPTH, NCH, 128, 3 * D], F32, kind="ExternalInput").ap()
    E["dwo"] = nc.dram_tensor("wo", [DEPTH, NCH, 128, D], F32, kind="ExternalInput").ap()
    E["dw1"] = nc.dram_tensor("w1", [DEPTH, 2, NIC, NCH, 128, 128], F32, kind="ExternalInput").ap()
    E["dw2"] = nc.dram_tensor("w2", [DEPTH, NIC, 128, D], F32, kind="ExternalInput").ap()
    E["dqs"] = nc.dram_tensor("qsr", [DEPTH, 128], F32, kind="ExternalInput").ap()
    E["dks"] = nc.dram_tensor("ksr", [DEPTH, 128], F32, kind="ExternalInput").ap()
    E["dgb"] = nc.dram_tensor("gb", [DEPTH, 2, D], F32, kind="ExternalInput").ap()
    E["db1"] = nc.dram_tensor("b1p", [DEPTH, 128, 2, NIC], F32, kind="ExternalInput").ap()
    E["db2"] = nc.dram_tensor("b2c", [DEPTH, 128, NCH], F32, kind="ExternalInput").ap()
    E["dowq"] = nc.dram_tensor("owq", [NCH, 128, D], F32, kind="ExternalInput").ap()
    E["dowk"] = nc.dram_tensor("owk", [NCH, 128, D], F32, kind="ExternalInput").ap()
    E["dowv"] = nc.dram_tensor("owv", [NCH, 128, D], F32, kind="ExternalInput").ap()
    E["dowo"] = nc.dram_tensor("owo", [NCH, 128, D], F32, kind="ExternalInput").ap()
    E["dowb"] = nc.dram_tensor("owb", [NCH, 128, H], F32, kind="ExternalInput").ap()
    E["dobb"] = nc.dram_tensor("obb", [1, H], F32, kind="ExternalInput").ap()
    E["dtabs"] = nc.dram_tensor("tabs", [4, 128, 16, 64], F32, kind="ExternalInput").ap()
    E["dlt"] = nc.dram_tensor("mlt", [128, 128], F32, kind="ExternalInput").ap()
    E["dmr"] = nc.dram_tensor("maskR", [4, 128, 512], F32, kind="ExternalInput").ap()
    E["dut"] = nc.dram_tensor("mut", [128, 128], F32, kind="ExternalInput").ap()
    E["do"] = nc.dram_tensor("o", [D, T], F32, kind="ExternalOutput").ap()
    E["dpark"] = nc.dram_tensor("xpark", [NCH, 128, T], F32).ap()

    ctx = []

    def sb(name, shape, dt):
        h = nc.sbuf_tensor(name, shape, dt)
        ctx.append(h)
        E[name] = h.__enter__()

    sb("u", [128, NCH, T], F32)
    sb("xs", [128, NCH, CF], BF16)
    sb("arena", [128, ARN], BF16)
    sb("tabf", [128, 4, 1024], BF16)
    sb("wring", [128, 8192], BF16)
    sb("rowf", [128, 2048], F32)
    sb("rowbf", [128, 2048], BF16)
    sb("qsb", [128, 256], BF16)
    sb("onesN", [1, 512], BF16)
    sb("onesrow", [1, 128], BF16)
    sb("tmpf", [128, 2, 512], F32)
    sb("colb", [128, 256], F32)
    sb("onec", [128, 1], BF16)
    sb("zeroc", [128, 1], BF16)
    sb("epsc", [128, 1], F32)
    sb("bndc", [128, 16], F32)
    sb("eps18", [128, 1], F32)
    sb("ltc", [128, 128], BF16)
    sb("mrc", [128, 4, 512], BF16)
    sb("utc", [128, 128], BF16)
    sb("gbsb", [1, 2 * D], BF16)
    sb("qsrow", [1, 384], BF16)
    sb("b1sb", [128, 2 * NIC], F32)
    sb("b2sb", [128, NCH], F32)
    sb("stf", [128, 8, 64], F32)
    sb("stb", [128, 8, 64], BF16)
    sb("bbrow", [1, H], BF16)
    _AR[0] = E["arena"]

    pbh = [nc.psum_tensor(f"pb{i}", [128, 512], F32) for i in range(8)]
    for h in pbh:
        ctx.append(h)
    E["pb"] = [h.__enter__() for h in pbh]

    b = Bld(nc)
    with nc.Block() as blk:
        for s in ("W", "K", "O", "T", "V", "S", "G"):
            h = nc.semaphore(s)
            ctx.append(h)
            b.sem(s, h.__enter__())
        b.spacer = (E["pb"][6][0:1, 508:509], E["onec"][:], E["onec"][:])
        emit_all(b, E)
        b.finalize(blk)

    if verbose:
        print(f"instr~{b.ninstr} spacers={b.nspacer} " +
              " ".join(f"{k}={v}" for k, v in b.cnt.items()))
    assert b.nspacer <= 8, f"unexpected spacers: {b.nspacer}"
    for h in reversed(ctx):
        h.__exit__(None, None, None)
    return nc



def emit_all(b, E):
    pb = E["pb"]
    u = E["u"]
    for c in range(NCH):
        b.op("gpsimd", lambda g, c=c: g.dma_start(u[:, c, :], E["dx"][c * 128:(c + 1) * 128, :]),
             incs=[("W", 16)])
    b.op("gpsimd", lambda g: g.dma_start(E["ltc"][:], E["dlt"]), incs=[("W", 16)])
    b.op("gpsimd", lambda g: g.dma_start(
        E["mrc"][:], E["dmr"].rearrange("a p n -> p a n")), incs=[("W", 16)])
    b.op("gpsimd", lambda g: g.dma_start(E["utc"][:], E["dut"]), incs=[("W", 16)])
    b.op("gpsimd", lambda g: g.dma_start(E["bbrow"][:], E["dobb"]), incs=[("W", 16)])
    b.op("gpsimd", lambda g: g.memset(E["onec"][:], 1.0))
    b.op("gpsimd", lambda g: g.memset(E["zeroc"][:], 0.0))
    b.op("gpsimd", lambda g: g.memset(E["epsc"][:], 1e-5))
    b.op("gpsimd", lambda g: g.memset(E["eps18"][:], 1e-18))
    b.op("gpsimd", lambda g: g.memset(E["onesrow"][:], 1.0))
    b.op("gpsimd", lambda g: g.memset(E["onesN"][:], 1.0))
    b.op("gpsimd", lambda g: g.memset(E["stf"][:], 0.0))
    b.op("gpsimd", lambda g: g.memset(E["stb"][:], 0.0), incs=[("G", 1)])
    st = {"winit": b.val("W"), "ginit": b.val("G"), "wring_free_t": 0, "xs_free_t": 0}
    st["u_w"] = st["winit"]

    park_and_ln(b, E, st)
    if TAP == "ln1":
        emit_output(b, E, st)
        return
    if emit_ffn6(b, E, st):
        if TAP in ("l0a", "l0f"):
            emit_output(b, E, st)
        return
    if TAP == "ffn6":
        emit_output(b, E, st)
        return
    unpark_merge(b, E, st)
    if TAP == "x1":
        emit_output(b, E, st)
        return
    if DO_OUTER:
        emit_deltanet(b, E, st)
    if st.get("tapped"):
        return
    if TAP == "x2" or not SECOND:
        emit_output(b, E, st)
        return
    park_and_ln(b, E, st)
    if emit_ffn6(b, E, st):
        emit_output(b, E, st)
        return
    unpark_merge(b, E, st)
    emit_output(b, E, st)


def barrier_waits(b):
    return [("T", b.val("T")), ("V", b.val("V")), ("S", b.val("S")),
            ("W", b.val("W")), ("K", b.val("K")), ("O", b.val("O")),
            ("G", b.val("G"))]



def emit_boundary_snapshot(b, E, st):
    u = E["u"]
    for bi, t in enumerate((511, 1023, 1535)):
        for c in range(4, NCH):
            dst = E["bndc"][:, bi * 4 + c - 4:bi * 4 + c - 3]
            b.op("vector", lambda v, dst=dst, c=c, t=t: v.tensor_copy(
                dst, u[:, c, t:t + 1]), waits=[("W", st["u_w"])], incs=[("V", 1)])
    st["v_bnd"] = b.val("V")


def emit_ln_slice(b, E, st, s0, shifted, gb, out_mode, park_o=None):
    pb = E["pb"]
    u = E["u"]
    xs = E["xs"]
    x0 = s0 % CF
    par = (s0 // 512) % 2
    n = 512
    v_entry = b.val("V")

    casts = []
    for c in range(NCH):
        dst = xs[:, c, x0:x0 + n]
        sh = shifted and c >= 4
        xw = [("T", st["xs_free_t"]), ("W", st["u_w"])]
        if sh and s0 == 0:
            b.op("vector", lambda v, dst=dst, c=c: v.tensor_copy(
                dst[:, 1:n], u[:, c, 0:n - 1]), waits=xw, incs=[("V", 1)])
            b.op("vector", lambda v, dst=dst: v.tensor_copy(dst[:, 0:1], E["zeroc"][:]),
                 waits=[("G", st["ginit"])], incs=[("V", 1)])
        elif sh:
            b.op("vector", lambda v, dst=dst, c=c, s0=s0: v.tensor_copy(
                dst[:, 1:n], u[:, c, s0:s0 + n - 1]), waits=xw, incs=[("V", 1)])
            bcol = E["bndc"][:, (s0 // 512 - 1) * 4 + c - 4:(s0 // 512 - 1) * 4 + c - 3]
            b.op("vector", lambda v, dst=dst, bcol=bcol: v.tensor_copy(
                dst[:, 0:1], bcol), incs=[("V", 1)])
        else:
            b.op("vector", lambda v, dst=dst, c=c, s0=s0: v.tensor_copy(
                dst, u[:, c, s0:s0 + n]), waits=xw, incs=[("V", 1)])
        casts.append(b.val("V"))

    for c in range(NCH):
        b.mm(pb[6][0:1, 0:n], E["onec"][:], xs[:, c, x0:x0 + n],
             start=(c == 0), stop=(c == NCH - 1),
             waits=[("V", casts[c])] + ([("V", v_entry), ("G", st["ginit"])]
                                        if c == 0 else []),
             incs=([("T", 1)] if c == NCH - 1 else []))
    t_sx = b.val("T")
    sq_cons = st.get("lnsq_cons", [0, 0])
    for c in range(NCH):
        slot = c % 2
        sq = aap("sqs", 0, 128, slot * 1024, [[1, n]])
        b.op("vector", lambda v, sq=sq, c=c, x0=x0: v.tensor_tensor(
            sq, xs[:, c, x0:x0 + n], xs[:, c, x0:x0 + n], Op.mult),
            waits=[("T", sq_cons[slot])], incs=[("V", 1)])
        vv = b.val("V")
        b.mm(pb[7][0:1, 0:n], E["onec"][:], sq, start=(c == 0), stop=(c == NCH - 1),
             waits=[("V", vv)] + ([("V", v_entry)] if c == 0 else []),
             incs=[("T", 1)])
        sq_cons[slot] = b.val("T")
    st["lnsq_cons"] = sq_cons
    t_sx2 = b.val("T")

    rf = E["rowf"]
    ro = par * 512
    murow = rf[0:1, 0:n]
    mu2 = rf[0:1, 512:512 + n]
    var = rf[0:1, 1024:1024 + n]
    lnr = rf[0:1, 1536:1536 + n]
    inv = 1.0 / float(D)
    b.op("vector", lambda v: v.tensor_scalar(murow, pb[6][0:1, 0:n], inv, None, Op.mult),
         waits=[("T", t_sx), ("S", st.get("ln_rows_s", 0))], incs=[("V", 1)])
    b.op("vector", lambda v: v.tensor_tensor(mu2, murow, murow, Op.mult), incs=[("V", 1)])
    b.op("vector", lambda v: v.scalar_tensor_tensor(var, pb[7][0:1, 0:n], inv, mu2,
                                                    Op.mult, Op.subtract),
         waits=[("T", t_sx2)], incs=[("V", 1)])
    v_var = b.val("V")
    b.op("scalar", lambda sc: sc.activation(lnr, var, AF.Ln, bias=E["epsc"][0:1, :],
                                            scale=1.0),
         waits=[("V", v_var), ("G", st["ginit"])], incs=[("S", 1)])
    rstdf = mu2
    b.op("scalar", lambda sc: sc.activation(rstdf, lnr, AF.Exp, bias=0.0, scale=-0.5),
         incs=[("S", 1)])
    s_rstd = b.val("S")
    st["ln_rows_s"] = s_rstd
    rbf_cons = st.get("rowbf_cons", [0])
    rbf = E["rowbf"]
    rstd_hi = rbf[0:1, 0:n]
    rstd_lo = rbf[0:1, 512:512 + n]
    nm_hi = rbf[0:1, 1024:1024 + n]
    nm_lo = rbf[0:1, 1536:1536 + n]
    b.op("vector", lambda v: v.tensor_copy(rstd_hi, rstdf),
         waits=[("S", s_rstd), ("T", rbf_cons[0])], incs=[("V", 1)])
    b.op("vector", lambda v: v.tensor_tensor(rstd_lo, rstdf, rstd_hi, Op.subtract),
         incs=[("V", 1)])
    nmursf = rf[0:1, 1536:1536 + n]
    b.op("vector", lambda v: v.scalar_tensor_tensor(nmursf, murow, -1.0, rstdf,
                                                    Op.mult, Op.mult),
         waits=[("S", s_rstd)], incs=[("V", 1)])
    b.op("vector", lambda v: v.tensor_copy(nm_hi, nmursf), incs=[("V", 1)])
    b.op("vector", lambda v: v.tensor_tensor(nm_lo, nmursf, nm_hi, Op.subtract),
         incs=[("V", 1)])
    v_rows = b.val("V")

    if gb is None:
        b.mm(pb[4][:, 0:n], E["onesrow"][:], rstd_hi, start=True, stop=False,
             ksize=1, waits=[("V", v_rows), ("V", v_entry)])
        b.mm(pb[4][:, 0:n], E["onesrow"][:], rstd_lo, start=False, stop=True,
             ksize=1, incs=[("T", 1)])
        b.mm(pb[5][:, 0:n], E["onesrow"][:], nm_hi, start=True, stop=False, ksize=1)
        b.mm(pb[5][:, 0:n], E["onesrow"][:], nm_lo, start=False, stop=True,
             ksize=1, incs=[("T", 1)])

    applies = []
    for c in range(NCH):
        if gb is not None:
            glhs = E["gbsb"][0:1, c * 128:(c + 1) * 128]
            blhs = E["gbsb"][0:1, D + c * 128:D + (c + 1) * 128]
            bw = [("V", v_rows), ("W", st.get("w_params", 0)),
                  ("V", v_entry if c == 0 else 0)]
            if applies:
                bw.append(("V", applies[-1]))
            b.mm(pb[4][:, 0:n], glhs, rstd_hi, start=True, stop=False, ksize=1,
                 waits=bw)
            b.mm(pb[4][:, 0:n], glhs, rstd_lo, start=False, stop=True, ksize=1,
                 incs=[("T", 1)])
            b.mm(pb[5][:, 0:n], glhs, nm_hi, start=True, stop=False, ksize=1)
            b.mm(pb[5][:, 0:n], glhs, nm_lo, start=False, stop=False, ksize=1)
            b.mm(pb[5][:, 0:n], blhs, E["onesN"][:, 0:n], start=False, stop=True,
                 ksize=1, incs=[("T", 1)])
        t_bc = b.val("T")
        slot = c % 2
        tf = E["tmpf"][:, slot, :]
        sh = shifted and c >= 4
        src1 = xs[:, c, x0:x0 + n] if sh else u[:, c, s0:s0 + n]
        b.op("vector", lambda v, tf=tf, src1=src1: v.tensor_tensor(
            tf, src1, pb[4][:, 0:n], Op.mult),
            waits=[("T", t_bc), ("W", st["u_w"]), ("T", t_sx), ("T", t_sx2)],
            incs=[("V", 1)])
        if out_mode == "u":
            dst = u[:, c, s0:s0 + n]
            pw = [("O", park_o[c])] if park_o else []
        else:
            dst = xs[:, c, x0:x0 + n]
            pw = []
        b.op("vector", lambda v, dst=dst, tf=tf: v.tensor_tensor(
            dst, tf, pb[5][:, 0:n], Op.add),
            waits=[("T", t_sx), ("T", t_sx2)] + pw, incs=[("V", 1)])
        applies.append(b.val("V"))
    rbf_cons[0] = b.val("T")
    st["rowbf_cons"] = rbf_cons
    return b.val("V")


def park_and_ln(b, E, st):
    u = E["u"]
    park_o = []
    bw = barrier_waits(b)
    for c in range(NCH):
        b.op("gpsimd", lambda g, c=c: g.dma_start(E["dpark"][c], u[:, c, :]),
             waits=bw, incs=[("O", 16)])
    park_o = [b.val("O")] * NCH
    for sl in range(4):
        emit_ln_slice(b, E, st, sl * 512, shifted=False, gb=None, out_mode="u",
                      park_o=park_o)
    st["xs_free_t"] = 0


def unpark_merge(b, E, st):
    u = E["u"]
    bw = barrier_waits(b)
    for c in range(NCH):
        b.op("vector", lambda v, c=c: v.tensor_scalar(
            u[:, c, :], u[:, c, :], 0.5, None, Op.mult),
            waits=bw, incs=[("V", 1)])
        vv = b.val("V")
        b.op("gpsimd", lambda g, c=c: g.dma_start(
            u[:, c, :], E["dpark"][c], accum_op=Op.add),
            waits=bw + [("V", vv)], incs=[("W", 16)])
    st["u_w"] = b.val("W")


def emit_output(b, E, st):
    u = E["u"]
    bw = barrier_waits(b)
    for c in range(NCH):
        b.op("sync", lambda s, c=c: s.dma_start(
            E["do"][c * 128:(c + 1) * 128, :], u[:, c, :]), waits=bw, incs=[("K", 16)])
    fin = b.val("K")
    b.op("sync", lambda s: None, waits=[("K", fin)])



def dbg_dump(b, E, regions):
    bw = barrier_waits(b)
    for ap, r0, nr, ncol in regions:
        b.op("gpsimd", lambda g, ap=ap, r0=r0, nr=nr, ncol=ncol: g.dma_start(
            E["do"][r0:r0 + nr, 0:ncol], ap), waits=bw, incs=[("W", 16)])
    fin = b.val("W")
    b.op("sync", lambda s: None, waits=[("W", fin)])


def emit_ffn6(b, E, st):
    for li in range(LAYERS):
        emit_layer_params(b, E, st, li)
        if st.get("tapped"):
            return True
        if emit_attn_sublayer(b, E, st, li):
            return True
        if TAP == "l0a" and li == 0:
            return True
        emit_ffn_sublayer(b, E, st, li)
        if TAP == "l0f" and li == 0:
            return True
    return False


def emit_layer_params(b, E, st, li):
    pb = E["pb"]
    bw = [("T", b.val("T")), ("V", b.val("V")), ("S", b.val("S"))]
    b.op("gpsimd", lambda g: g.dma_start(
        E["gbsb"][:], E["dgb"][li:li + 1].rearrange("o s n -> o (s n)")),
        waits=bw, incs=[("W", 16)])
    b.op("gpsimd", lambda g: g.dma_start(
        E["b1sb"][:].rearrange("p (s i) -> p s i", s=2), E["db1"][li]), incs=[("W", 16)])
    b.op("gpsimd", lambda g: g.dma_start(E["b2sb"][:], E["db2"][li]), incs=[("W", 16)])
    b.op("gpsimd", lambda g: g.dma_start(E["qsrow"][0:1, 256:384], E["dqs"][li:li + 1, :]),
         incs=[("W", 16)])
    b.op("gpsimd", lambda g: g.dma_start(E["qsrow"][0:1, 128:256], E["dks"][li:li + 1, :]),
         incs=[("W", 16)])
    wv = b.val("W")
    st["w_params"] = wv
    qsr = E["qsrow"]
    b.op("vector", lambda v: v.tensor_scalar(qsr[0:1, 0:128], qsr[0:1, 256:384],
                                             8.0, None, Op.mult),
         waits=bw + [("W", wv)], incs=[("V", 1)])
    v_rows = b.val("V")
    for i, src0 in enumerate((0, 64, 128, 192)):
        b.mm(pb[6][:, 0:64], E["onesrow"][:], qsr[0:1, src0:src0 + 64],
             start=True, stop=True, ksize=1,
             waits=[("V", b.val("V"))], incs=[("T", 1)])
        tv = b.val("T")
        b.op("vector", lambda v, i=i: v.tensor_copy(E["qsb"][:, i * 64:(i + 1) * 64],
                                                    pb[6][:, 0:64]),
             waits=[("T", tv)], incs=[("V", 1)])
    v_qsb = b.val("V")
    tb = apv(E["wring"], 0, 128, 0, [[1024, 4], [1, 1024]])
    b.op("gpsimd", lambda g, tb=tb: g.dma_start(
        tb, E["dtabs"].rearrange("a p t d -> p a (t d)")),
        waits=[("T", st["wring_free_t"])], incs=[("W", 16)])
    w_tb = b.val("W")
    for i in range(4):
        bc = apv(E["qsb"], 0, 128, i * 64, [[0, 16], [1, 64]])
        tbi = apv(E["wring"], 0, 128, i * 1024, [[1, 1024]])
        b.op("vector", lambda v, i=i, bc=bc, tbi=tbi: v.tensor_tensor(
            E["tabf"][:, i, :], tbi, bc, Op.mult),
            waits=[("V", v_qsb), ("W", w_tb)], incs=[("V", 1)])
    st["v_tabs"] = b.val("V")
    st["tabs_wring_v"] = b.val("V")
    if TAP == "dtab" and li == 0:
        dbg_dump(b, E, [(E["qsrow"][0:1, 0:384], 0, 1, 384),
                        (E["qsb"][:, 0:256], 1, 128, 256),
                        (apv(E["tabf"], 0, 128, 0, [[1, 2048]]), 129, 128, 2048),
                        (apv(E["tabf"], 0, 128, 2048, [[1, 2048]]), 257, 128, 2048)])
        st["tapped"] = True


def emit_attn_sublayer(b, E, st, li):
    pb = E["pb"]
    u = E["u"]
    xs = E["xs"]
    wr = E["wring"]
    emit_boundary_snapshot(b, E, st)

    wslot_cons = [st["wring_free_t"], st["wring_free_t"]]
    tabv = st.get("tabs_wring_v", 0)
    pbank_free = st.get("pbank_free", [[] for _ in range(8)])
    ring_s = st.get("attn_ring_s", [0, 0, 0, 0])
    ring_v = st.get("attn_ring_v", [0, 0, 0, 0])
    pn_cons = st.get("pn_cons", [0, 0])
    pt_cons = st.get("pt_cons", [0, 0])
    sbank_cons = st.get("sbank_cons", [[], []])
    sidx0 = st.get("attn_sidx", 0)

    for ch in range(4):
        s0 = ch * CA
        casts = []
        for c in range(NCH):
            dst = xs[:, c, 0:CA]
            sh = c >= 4
            xw = [("T", st["xs_free_t"]), ("W", st["u_w"])]
            if sh and s0 == 0:
                b.op("vector", lambda v, dst=dst, c=c: v.tensor_copy(
                    dst[:, 1:CA], u[:, c, 0:CA - 1]), waits=xw, incs=[("V", 1)])
                b.op("vector", lambda v, dst=dst: v.tensor_copy(dst[:, 0:1],
                     E["zeroc"][:]), waits=[("G", st["ginit"])], incs=[("V", 1)])
            elif sh:
                b.op("vector", lambda v, dst=dst, c=c, s0=s0: v.tensor_copy(
                    dst[:, 1:CA], u[:, c, s0:s0 + CA - 1]), waits=xw, incs=[("V", 1)])
                bcol = E["bndc"][:, (s0 // 512 - 1) * 4 + c - 4:(s0 // 512 - 1) * 4 + c - 3]
                b.op("vector", lambda v, dst=dst, bcol=bcol: v.tensor_copy(
                    dst[:, 0:1], bcol), incs=[("V", 1)])
            else:
                src = u[:, c, s0:s0 + CA]
                b.op("vector", lambda v, dst=dst, src=src: v.tensor_copy(dst, src),
                     waits=xw, incs=[("V", 1)])
            casts.append(b.val("V"))

        hw = [("T", st.get("attn_prev_t", 0))]
        if ch > 0:
            b.op("vector", lambda v: v.tensor_copy(
                aap("kfm", 0, 128, 0, [[768, 8], [1, 256]]),
                aap("kfm", 0, 128, 512, [[768, 8], [1, 256]])),
                waits=hw, incs=[("V", 1)])
            b.op("vector", lambda v: v.tensor_copy(
                aap("vtok", 0, 128, 0, [[1, 2048]]),
                aap("vtok", 0, 128, 4096, [[1, 2048]])),
                waits=hw, incs=[("V", 1)])
        v_halo = b.val("V")

        evac_v = {}
        for cg in range(6):
            slot = cg % 2
            wsl = apv(wr, 0, 128, slot * 4096, [[512, 8], [1, 512]])
            src = E["dwqkv"][li].rearrange("kc k n -> k kc n")[:, :, cg * 512:(cg + 1) * 512]
            b.op("gpsimd", lambda g, wsl=wsl, src=src: g.dma_start(wsl, src),
                 waits=[("T", wslot_cons[slot]), ("V", tabv)], incs=[("W", 16)])
            wv = b.val("W")
            for tt in range(4):
                bank = pb[tt % 2]
                for kc in range(NCH):
                    b.mm(bank[:, 0:512],
                         xs[:, kc, tt * 128:(tt + 1) * 128],
                         apv(wr, 0, 128, slot * 4096 + kc * 512, [[1, 512]]),
                         start=(kc == 0), stop=(kc == NCH - 1),
                         waits=[("W", wv), ("V", casts[kc])] +
                               (pbank_free[tt % 2] if kc == 0 else []),
                         incs=([("T", 1)] if kc == NCH - 1 else []))
                tv = b.val("T")
                if cg < 2:
                    dst = aap("qtok", 0, 128, tt * 1024 + cg * 512, [[1, 512]])
                elif cg < 4:
                    dst = aap("ktok", 0, 128, tt * 1024 + (cg - 2) * 512, [[1, 512]])
                else:
                    dst = aap("vtok", 0, 128, (2 + tt) * 1024 + (cg - 4) * 512, [[1, 512]])
                wt = [("T", tv), ("V", v_halo)]
                if cg < 2:
                    wt += [("T", st.get("aout_free_t", 0)), ("K", st.get("qtok_free_k", 0))]
                elif cg < 4:
                    wt += [("K", st.get("ktok_free_k", 0))]
                else:
                    wt += [("T", st.get("attn_prev_t", 0))]
                b.op("vector", lambda v, dst=dst, bank=bank: v.tensor_copy(
                    dst, bank[:, 0:512]), waits=wt, incs=[("V", 1)])
                pbank_free[tt % 2] = [("V", b.val("V"))]
                evac_v[(cg, tt)] = b.val("V")
            wslot_cons[slot] = b.val("T")
        v_vall = max(evac_v[(cg, tt)] for cg in (4, 5) for tt in range(4))
        if ch == 0 and TAP == "dq":
            dbg_dump(b, E, [(aap("qtok", 0, 128, tt * 1024, [[1, 1024]]),
                             tt * 128, 128, 1024) for tt in range(4)])
            return True
        if ch == 0 and TAP == "dk0":
            dbg_dump(b, E, [(aap("ktok", 0, 128, tt * 1024, [[1, 1024]]),
                             tt * 128, 128, 1024) for tt in range(4)])
            return True
        if ch == 0 and TAP == "dxs":
            dbg_dump(b, E, [(xs[:, c, 0:CA], c * 128, 128, 512) for c in range(NCH)])
            return True

        v_tabs = st.get("v_tabs", 0)
        sq_cons = st.get("asq_cons", [0, 0])
        rot_cons = st.get("rot_cons", [0, 0])
        q_done = {}
        k_done = {}
        for which, reg, ctab, stab, rncol0 in (("q", "qtok", 0, 1, 0),
                                               ("k", "ktok", 2, 3, 64)):
            for tt in range(4):
                ttg = ch * 4 + tt
                g0 = 0 if which == "q" else 2
                ready = [("V", max(evac_v[(g0, tt)], evac_v[(g0 + 1, tt)]))]
                slot = tt % 2
                sq = aap("sqs", 0, 128, slot * 1024, [[1, 1024]])
                tile = aap(reg, 0, 128, tt * 1024, [[1, 1024]])
                b.op("vector", lambda v, sq=sq, tile=tile: v.tensor_tensor(
                    sq, tile, tile, Op.mult),
                    waits=ready + [("V", sq_cons[slot])], incs=[("V", 1)])
                nrm = apv(E["colb"], 0, 128, rncol0 + tt * 16, [[1, 16]])
                sq3 = aap("sqs", 0, 128, slot * 1024, [[64, 16], [1, 64]])
                b.op("vector", lambda v, nrm=nrm, sq3=sq3: v.reduce_sum(nrm, sq3, axis=AX),
                     incs=[("V", 1)])
                v_n = b.val("V")
                b.op("scalar", lambda sc, nrm=nrm: sc.activation(
                    nrm, nrm, AF.Ln, bias=E["eps18"][:, 0:1], scale=1.0),
                    waits=[("V", v_n), ("G", st["ginit"])], incs=[("S", 1)])
                b.op("scalar", lambda sc, nrm=nrm: sc.activation(
                    nrm, nrm, AF.Exp, bias=0.0, scale=-0.5), incs=[("S", 1)])
                s_rn = b.val("S")
                rsl = tt % 2
                tmp = aap("rott", 0, 128, rsl * 1024, [[1, 1024]])
                rotv = aap(reg, 0, 128, tt * 1024 + 32, [[64, 16], [-32, 2], [1, 32]])
                stv = apv(E["tabf"], 0, 128, stab * 1024 + ttg * 64,
                          [[0, 16], [32, 2], [1, 32]])
                ctv = apv(E["tabf"], 0, 128, ctab * 1024 + ttg * 64, [[0, 16], [1, 64]])
                b.op("vector", lambda v, tmp=tmp, rotv=rotv, stv=stv: v.tensor_tensor(
                    tmp, rotv, stv, Op.mult),
                    waits=[("V", v_tabs), ("V", rot_cons[rsl])], incs=[("V", 1)])
                xr = sq
                b.op("vector", lambda v, xr=xr, tile=tile, ctv=ctv: v.tensor_tensor(
                    xr, tile, ctv, Op.mult), incs=[("V", 1)])
                kf0 = aap("rott", 0, 128, (rsl ^ 1) * 1024, [[1, 1024]])
                b.op("vector", lambda v, kf0=kf0, xr=xr, tmp=tmp: v.tensor_tensor(
                    kf0, xr, tmp, Op.add), incs=[("V", 1)])
                rnb = apv(E["colb"], 0, 128, rncol0 + tt * 16, [[1, 16], [0, 64]])
                b.op("vector", lambda v, tile=tile, kf0=kf0, rnb=rnb: v.tensor_tensor(
                    tile, kf0, rnb, Op.mult),
                    waits=[("S", s_rn)], incs=[("V", 1)])
                if which == "q":
                    q_done[tt] = b.val("V")
                else:
                    k_done[tt] = b.val("V")
                sq_cons[slot] = b.val("V")
                rot_cons[rsl] = b.val("V")
        st["asq_cons"] = sq_cons
        st["rot_cons"] = rot_cons
        if ch == 0 and TAP == "dnrm":
            dbg_dump(b, E, [(apv(E["colb"], 0, 128, 0, [[1, 256]]), 0, 128, 256),
                            (aap("sqs", 0, 128, 0, [[1, 1024]]), 128, 128, 1024),
                            (aap("sqs", 0, 128, 1024, [[1, 1024]]), 256, 128, 1024)])
            return True
        if ch == 0 and TAP == "dqr":
            dbg_dump(b, E, [(aap("qtok", 0, 128, tt * 1024, [[1, 1024]]),
                             tt * 128, 128, 1024) for tt in range(4)])
            return True
        if ch == 0 and TAP == "dkr":
            dbg_dump(b, E, [(aap("ktok", 0, 128, tt * 1024, [[1, 1024]]),
                             tt * 128, 128, 1024) for tt in range(4)] +
                     [(apv(E["colb"], 0, 128, 0, [[1, 256]]), 512, 128, 256)])
            return True

        fmw = [("T", st.get("attn_prev_t", 0))]
        qxp = {}
        kxp = {}
        for tt in range(4):
            src = aap("qtok", 0, 128, tt * 1024, [[1, 1024]])
            dst = aap("qfm", 0, 128, tt * 128, [[512, 8], [1, 128]])
            b.op("sync", lambda s, dst=dst, src=src: s.dma_start(dst, src, transpose=True),
                 waits=[("V", q_done[tt])] + fmw, incs=[("K", 16)])
            qxp[tt] = b.val("K")
        for tt in range(4):
            src = aap("ktok", 0, 128, tt * 1024, [[1, 1024]])
            dst = aap("kfm", 0, 128, 256 + tt * 128, [[768, 8], [1, 128]])
            b.op("sync", lambda s, dst=dst, src=src: s.dma_start(dst, src, transpose=True),
                 waits=[("V", k_done[tt]), ("V", v_halo)] + fmw, incs=[("K", 16)])
            kxp[tt] = b.val("K")
        k_allxp = b.val("K")
        st["qtok_free_k"] = max(qxp.values())
        st["ktok_free_k"] = k_allxp
        if ch == 0 and TAP == "dqfm":
            dbg_dump(b, E, [(aap("qfm", 0, 128, 0, [[1, 2048]]), 0, 128, 2048),
                            (aap("qfm", 0, 128, 2048, [[1, 2048]]), 128, 128, 2048),
                            (aap("kfm", 0, 128, 0, [[1, 2048]]), 256, 128, 2048),
                            (aap("kfm", 0, 128, 2048, [[1, 2048]]), 384, 128, 2048)])
            return True

        sidx = sidx0
        aout_v = []
        obank_cons = st.get("obank_cons", [[], []])
        for wl in range(2):
            wg = 2 * ch + wl
            w0 = (wg == 0)
            kc0 = 256 if w0 else wl * 256
            N = 256 if w0 else 512
            nm = 2 if w0 else 4
            for p in range(8):
                obank = pb[4 + (p % 2)]
                for hh in range(2):
                    h = 2 * p + hh
                    pvw = []
                    for isub in range(2):
                        sb_i = sidx % 2
                        sbank = pb[2 + sb_i]
                        lhs = aap("qfm", hh * 64, 64,
                                  (h // 2) * 512 + wl * 256 + isub * 128, [[1, 128]])
                        rhs = aap("kfm", hh * 64, 64, (h // 2) * 768 + kc0, [[1, N]])
                        b.mm(sbank[:, 0:N], lhs, rhs, start=True, stop=False,
                             kbase=hh * 64, ksize=64,
                             waits=[("K", k_allxp)] + sbank_cons[sb_i])
                        vi = (2 + isub) if w0 else isub
                        mrhs = E["mrc"][:, vi, 256:512] if w0 else E["mrc"][:, vi, 0:512]
                        b.mm(sbank[:, 0:N], E["ltc"][:], mrhs, start=False, stop=True,
                             waits=[("W", st["winit"])], incs=[("T", 1)])
                        t_s = b.val("T")
                        ri = sidx % 4
                        negm = apv(E["colb"], 0, 128, 128 + ri * 4, [[1, 1]])
                        b.op("vector", lambda v, negm=negm, sbank=sbank, N=N: v.reduce_max(
                            negm, sbank[:, 0:N], axis=AX, negate=True),
                            waits=[("T", t_s), ("S", ring_s[ri])],
                            incs=[("V", 1)])
                        v_m = b.val("V")
                        lcol = apv(E["colb"], 0, 128, 130 + ri * 4, [[1, 1]])
                        pdst = aap("pnbuf", 0, 128, hh * 2048 + isub * 512, [[1, N]])
                        b.op("scalar", lambda sc, pdst=pdst, sbank=sbank, negm=negm,
                             lcol=lcol, N=N: sc.activation(
                                 pdst, sbank[:, 0:N], AF.Exp, bias=negm, scale=1.0,
                                 accum_out=lcol),
                             waits=[("V", v_m)], incs=[("S", 1)])
                        s_exp = b.val("S")
                        sbank_cons[sb_i] = [("S", s_exp)]
                        rl = apv(E["colb"], 0, 128, 131 + ri * 4, [[1, 1]])
                        b.op("vector", lambda v, rl=rl, lcol=lcol: v.reciprocal(rl, lcol),
                             waits=[("S", s_exp)], incs=[("V", 1)])
                        v_rl = b.val("V")
                        pnd = aap("abuf", 0, 128, hh * 1024 + isub * 512, [[1, N]])
                        b.op("scalar", lambda sc, pnd=pnd, pdst=pdst, rl=rl: sc.activation(
                            pnd, pdst, AF.Copy, bias=0.0, scale=rl),
                            waits=[("V", v_rl), ("K", pn_cons[hh])], incs=[("S", 1)])
                        pvw.append(("S", b.val("S")))
                        ring_s[ri] = b.val("S")
                        ring_v[ri] = v_rl
                        sidx += 1
                    psrc = aap("abuf", 0, 128, hh * 1024, [[1, 1024]])
                    ptd = aap("ptbuf", 0, 128, hh * 1024, [[128, 8], [1, 128]])
                    b.op("sync", lambda s, ptd=ptd, psrc=psrc: s.dma_start(
                        ptd, psrc, transpose=True),
                        waits=pvw + [("T", pt_cons[hh])], incs=[("K", 16)])
                    k_pt = b.val("K")
                    pn_cons[hh] = k_pt
                    for jm in range(nm):
                        vti = wl * 2 + jm + (2 if w0 else 0)
                        vsl = aap("vtok", 0, 128, vti * 1024 + h * 64, [[1, 64]])
                        prhs = aap("ptbuf", 0, 128, hh * 1024 + jm * 128,
                                   [[512, 2], [1, 128]])
                        ow = [("K", k_pt), ("V", v_vall)]
                        if jm == 0:
                            ow += obank_cons[p % 2]
                        b.mm(obank[hh * 64:hh * 64 + 64, 0:256], vsl, prhs,
                             start=(jm == 0), stop=(jm == nm - 1), tp=(0, hh * 64),
                             waits=ow, incs=([("T", 1)] if jm == nm - 1 else []),
                             sgc=True)
                    pt_cons[hh] = b.val("T")
                t_pv = b.val("T")
                adst = aap("aout", 0, 128, p * 512 + wl * 256, [[1, 256]])
                b.op("vector", lambda v, adst=adst, obank=obank: v.tensor_copy(
                    adst, obank[:, 0:256]),
                    waits=[("T", t_pv), ("K", st["qtok_free_k"])], incs=[("V", 1)])
                obank_cons[p % 2] = [("V", b.val("V"))]
                aout_v.append(b.val("V"))
        st["obank_cons"] = obank_cons
        st["attn_prev_t"] = b.val("T")
        if ch == 0 and TAP == "dao":
            dbg_dump(b, E, [(aap("aout", 0, 128, 0, [[1, 2048]]), 0, 128, 2048),
                            (aap("aout", 0, 128, 2048, [[1, 2048]]), 128, 128, 2048)])
            return True
        if ch == 0 and TAP == "dpn":
            dbg_dump(b, E, [(aap("abuf", 0, 128, 0, [[1, 2048]]), 0, 128, 2048),
                            (aap("ptbuf", 0, 128, 0, [[1, 2048]]), 128, 128, 2048)])
            return True

        v_aout = max(aout_v)
        for oc in range(NCH):
            slot = oc % 2
            wsl = apv(wr, 0, 128, slot * 4096, [[128, 8], [1, 128]])
            src = E["dwo"][li].rearrange("kc k n -> k kc n")[:, :, oc * 128:(oc + 1) * 128]
            b.op("gpsimd", lambda g, wsl=wsl, src=src: g.dma_start(wsl, src),
                 waits=[("T", wslot_cons[slot])], incs=[("W", 16)])
            wv = b.val("W")
            bank = pb[oc % 2]
            for kc in range(NCH):
                b.mm(bank[:, 0:512],
                     apv(wr, 0, 128, slot * 4096 + kc * 128, [[1, 128]]),
                     aap("aout", 0, 128, kc * 512, [[1, 512]]),
                     start=(kc == 0), stop=(kc == NCH - 1),
                     waits=[("W", wv), ("V", v_aout)] +
                           (pbank_free[oc % 2] if kc == 0 else []),
                     incs=([("T", 1)] if kc == NCH - 1 else []))
            tv = b.val("T")
            b.op("vector", lambda v, oc=oc, bank=bank, s0=s0: v.tensor_tensor(
                u[:, oc, s0:s0 + CA], u[:, oc, s0:s0 + CA], bank[:, 0:512], Op.add),
                waits=[("T", tv), ("W", st["u_w"])], incs=[("V", 1)])
            pbank_free[oc % 2] = [("V", b.val("V"))]
            wslot_cons[slot] = b.val("T")
        st["aout_free_t"] = b.val("T")
        st["xs_free_t"] = b.val("T")
        st["attn_prev_t"] = b.val("T")
    st["pbank_free"] = pbank_free
    st["attn_ring_s"] = ring_s
    st["attn_ring_v"] = ring_v
    st["pn_cons"] = pn_cons
    st["pt_cons"] = pt_cons
    st["sbank_cons"] = sbank_cons
    st["attn_sidx"] = sidx
    st["wring_free_t"] = b.val("T")
    return False


def emit_ffn_sublayer(b, E, st, li):
    pb = E["pb"]
    u = E["u"]
    xs = E["xs"]
    wr = E["wring"]
    emit_boundary_snapshot(b, E, st)
    pbank_free = st.get("pbank_free", [[] for _ in range(8)])
    w1slot_cons = [st["wring_free_t"]] * 4
    w2slot_cons = [st["wring_free_t"]] * 2
    gact_cons = st.get("gact_cons", [0, 0])

    for ch in range(2):
        c0 = ch * CF
        xn_v = []
        for tsl in range(2):
            s0 = c0 + tsl * 512
            vd = emit_ln_slice(b, E, st, s0, shifted=True, gb=True, out_mode="xs")
            xn_v.append(vd)
        v_xn = max(xn_v)
        if ch == 1 and TAP == "dxn" and li == 0:
            dbg_dump(b, E, [(xs[:, c, 0:CF], c * 128, 128, 1024) for c in range(NCH)])
            st["tapped"] = True
            return

        p_t = {}
        for ic in range(NIC):
            for s in range(2):
                slot = s * 2 + ic % 2
                wsl = apv(wr, 0, 128, slot * 1024, [[128, 8], [1, 128]])
                src = E["dw1"][li, s, ic].rearrange("kc k m -> k kc m")
                b.op("gpsimd", lambda g, wsl=wsl, src=src: g.dma_start(wsl, src),
                     waits=[("T", w1slot_cons[slot])], incs=[("W", 16)])
            wv = b.val("W")
            for tsl in range(2):
                bkA = pb[tsl * 2]
                bkG = pb[tsl * 2 + 1]
                for s in range(2):
                    bank = bkA if s == 0 else bkG
                    slot = s * 2 + ic % 2
                    for kc in range(NCH):
                        b.mm(bank[:, 0:512],
                             apv(wr, 0, 128, slot * 1024 + kc * 128, [[1, 128]]),
                             xs[:, kc, tsl * 512:(tsl + 1) * 512],
                             start=(kc == 0), stop=(kc == NCH - 1),
                             waits=[("W", wv), ("V", v_xn)] +
                                   (pbank_free[tsl * 2 + s] if kc == 0 else []),
                             incs=([("T", 1)] if kc == NCH - 1 else []))
                t_ag = b.val("T")
                if tsl == 1:
                    w1slot_cons[ic % 2] = t_ag
                    w1slot_cons[2 + ic % 2] = t_ag
                gsl = (ic * 2 + tsl) % 2
                gact = aap("gact", 0, 128, gsl * 512, [[1, 512]])
                b1g = apv(E["b1sb"], 0, 128, NIC + ic, [[1, 1]])
                b1a = apv(E["b1sb"], 0, 128, ic, [[1, 1]])
                b.op("scalar", lambda sc, gact=gact, bkG=bkG, b1g=b1g: sc.activation(
                    gact, bkG[:, 0:512], AF.Gelu, bias=b1g, scale=1.0),
                    waits=[("T", t_ag), ("S", gact_cons[gsl]),
                           ("W", st.get("w_params", 0)),
                           ("K", st.get("pn_cons", [0, 0])[gsl])], incs=[("S", 1)])
                s_g = b.val("S")
                gact_cons[gsl] = s_g
                pdst = aap("p", 0, 128, ic * 1024 + tsl * 512, [[1, 512]])
                b.op("vector", lambda v, pdst=pdst, bkA=bkA, b1a=b1a, gact=gact:
                     v.scalar_tensor_tensor(pdst, bkA[:, 0:512], b1a, gact,
                                            Op.add, Op.mult),
                     waits=[("S", s_g), ("T", st.get("p_free_t", 0)),
                            ("K", st.get("qtok_free_k", 0))], incs=[("V", 1)])
                pv = b.val("V")
                p_t[(ic, tsl)] = pv
                pbank_free[tsl * 2] = [("V", pv)]
                pbank_free[tsl * 2 + 1] = [("S", s_g)]
        st["gact_cons"] = gact_cons

        for tsl in range(2):
            v_pall = max(p_t[(ic, tsl)] for ic in range(NIC))
            evw = []
            for ic in range(NIC):
                slot = ic % 2
                wsl = apv(wr, 0, 128, 4096 + slot * 1024, [[1, 1024]])
                src = E["dw2"][li, ic]
                b.op("gpsimd", lambda g, wsl=wsl, src=src: g.dma_start(wsl, src),
                     waits=[("T", w2slot_cons[slot])], incs=[("W", 16)])
                wv = b.val("W")
                for oc in range(NCH):
                    ww = [("W", wv), ("V", p_t[(ic, tsl)])]
                    if ic == 0:
                        ww += [("V", v_pall)] + pbank_free[oc]
                    b.mm(pb[oc][:, 0:512],
                         apv(wr, 0, 128, 4096 + slot * 1024 + oc * 128, [[1, 128]]),
                         aap("p", 0, 128, ic * 1024 + tsl * 512, [[1, 512]]),
                         start=(ic == 0), stop=(ic == NIC - 1),
                         waits=ww,
                         incs=([("T", 1)] if oc == NCH - 1 else []))
                w2slot_cons[slot] = b.val("T")
            t_w2 = b.val("T")
            for oc in range(NCH):
                b2c = apv(E["b2sb"], 0, 128, oc, [[1, 1]])
                dst = u[:, oc, c0 + tsl * 512:c0 + (tsl + 1) * 512]
                b.op("vector", lambda v, dst=dst, oc=oc, b2c=b2c: v.scalar_tensor_tensor(
                    dst, pb[oc][:, 0:512], b2c, dst, Op.add, Op.add),
                    waits=[("T", t_w2), ("W", st["u_w"])], incs=[("V", 1)])
                pbank_free[oc] = [("V", b.val("V"))]
        st["p_free_t"] = b.val("T")
        st["xs_free_t"] = b.val("T")
    st["pbank_free"] = pbank_free
    st["wring_free_t"] = b.val("T")


def emit_deltanet(b, E, st):
    pb = E["pb"]
    old_spacer = b.spacer
    b.spacer = (pb[2][0:1, 400:401], E["onec"][:], E["onec"][:])
    u = E["u"]
    xs = E["xs"]
    wr = E["wring"]
    wslot_cons = [st["wring_free_t"]] * 2
    pbank_free = st.get("pbank_free", [[] for _ in range(8)])
    abuf_cons = [0, 0, 0, 0]
    dn_sq = [0, 0]
    st["xs_free_t"] = st.get("xs_free_t", 0)

    for ch in range(4):
        s0 = ch * 512
        x0 = (ch % 2) * 512
        v_xo = emit_ln_slice(b, E, st, s0, shifted=False, gb=None, out_mode="xs")
        obank_cons = [[("V", v_xo)], [("V", v_xo)]]

        evac_v = {}
        for pi, (dw, reg) in enumerate((("dowq", "qtok"), ("dowk", "ktok"),
                                        ("dowv", "vtok"))):
            for cg in range(2):
                slot = (pi * 2 + cg) % 2
                wsl = apv(wr, 0, 128, slot * 4096, [[512, 8], [1, 512]])
                src = E[dw].rearrange("kc k n -> k kc n")[:, :, cg * 512:(cg + 1) * 512]
                b.op("gpsimd", lambda g, wsl=wsl, src=src: g.dma_start(wsl, src),
                     waits=[("T", wslot_cons[slot])], incs=[("W", 16)])
                wv = b.val("W")
                for tt in range(4):
                    bank = pb[tt % 2]
                    for kc in range(NCH):
                        b.mm(bank[:, 0:512],
                             xs[:, kc, x0 + tt * 128:x0 + (tt + 1) * 128],
                             apv(wr, 0, 128, slot * 4096 + kc * 512, [[1, 512]]),
                             start=(kc == 0), stop=(kc == NCH - 1),
                             waits=[("W", wv), ("V", v_xo)] +
                                   (pbank_free[tt % 2] if kc == 0 else []),
                             incs=([("T", 1)] if kc == NCH - 1 else []))
                    tv = b.val("T")
                    dst = aap(reg, 0, 128, tt * 1024 + cg * 512, [[1, 512]])
                    b.op("vector", lambda v, dst=dst, bank=bank: v.tensor_copy(
                        dst, bank[:, 0:512]),
                        waits=[("T", tv), ("K", st.get("dn_fm_k", 0)),
                               ("T", st.get("dn_scan_t", 0))], incs=[("V", 1)])
                    pbank_free[tt % 2] = [("V", b.val("V"))]
                    evac_v[(pi, cg, tt)] = b.val("V")
                wslot_cons[slot] = b.val("T")
        wslb = apv(E["qsb"], 0, 128, 0, [[16, 8], [1, 16]])
        b.op("gpsimd", lambda g, wslb=wslb: g.dma_start(
            wslb, E["dowb"].rearrange("kc k n -> k kc n")),
            waits=[("T", st["wring_free_t"]), ("V", st.get("dn_qsb_v", 0))],
            incs=[("W", 16)])
        wvb = b.val("W")
        s_betas = {}
        for tt in range(4):
            bank = pb[2]
            for kc in range(NCH):
                b.mm(bank[:, 0:16], xs[:, kc, x0 + tt * 128:x0 + (tt + 1) * 128],
                     apv(E["qsb"], 0, 128, kc * 16, [[1, 16]]),
                     start=(kc == 0), stop=False,
                     waits=[("W", wvb), ("V", v_xo),
                            ("S", s_betas.get(tt - 1, 0))] +
                           (st.get("betabank_free", []) if kc == 0 else []))
            b.mm(bank[:, 0:16], E["onesrow"][:], E["bbrow"][:], start=False, stop=True,
                 ksize=1, waits=[("W", st["winit"])], incs=[("T", 1)])
            tv = b.val("T")
            bcol = apv(E["colb"], 0, 128, 64 + tt * 16, [[1, 16]])
            b.op("scalar", lambda sc, bcol=bcol, bank=bank: sc.activation(
                bcol, bank[:, 0:16], AF.Sigmoid, bias=0.0, scale=1.0),
                waits=[("T", tv)], incs=[("S", 1)])
            s_betas[tt] = b.val("S")
        st["betabank_free"] = [("S", b.val("S"))]

        qk_v = {}
        for which, reg, rn0, pi in (("q", "qtok", 0, 0), ("k", "ktok", 128, 1)):
            for tt in range(4):
                ready = [("V", max(evac_v[(pi, 0, tt)], evac_v[(pi, 1, tt)]))]
                slot = tt % 2
                sq = aap("sqs", 0, 128, slot * 1024, [[1, 1024]])
                tile = aap(reg, 0, 128, tt * 1024, [[1, 1024]])
                b.op("vector", lambda v, sq=sq, tile=tile: v.tensor_tensor(
                    sq, tile, tile, Op.mult),
                    waits=ready + [("V", dn_sq[slot])], incs=[("V", 1)])
                nrm = apv(E["colb"], 0, 128, rn0 + tt * 16, [[1, 16]])
                sq3 = aap("sqs", 0, 128, slot * 1024, [[64, 16], [1, 64]])
                b.op("vector", lambda v, nrm=nrm, sq3=sq3: v.reduce_sum(nrm, sq3, axis=AX),
                     incs=[("V", 1)])
                v_n = b.val("V")
                b.op("scalar", lambda sc, nrm=nrm: sc.activation(
                    nrm, nrm, AF.Ln, bias=E["eps18"][:, 0:1], scale=1.0),
                    waits=[("V", v_n), ("G", st["ginit"])], incs=[("S", 1)])
                b.op("scalar", lambda sc, nrm=nrm: sc.activation(
                    nrm, nrm, AF.Exp, bias=0.0, scale=-0.5), incs=[("S", 1)])
                s_rn = b.val("S")
                if which == "q":
                    rnb = apv(E["colb"], 0, 128, tt * 16, [[1, 16], [0, 64]])
                    b.op("vector", lambda v, tile=tile, rnb=rnb: v.tensor_tensor(
                        tile, tile, rnb, Op.mult),
                        waits=[("S", s_rn)], incs=[("V", 1)])
                dn_sq[slot] = b.val("V")
                qk_v[(which, tt)] = (b.val("V"), s_rn)
        v_brn = {}
        for tt in range(4):
            bc = apv(E["colb"], 0, 128, 64 + tt * 16, [[1, 16]])
            rc = apv(E["colb"], 0, 128, 128 + tt * 16, [[1, 16]])
            dc = apv(E["colb"], 0, 128, 192 + tt * 16, [[1, 16]])
            b.op("vector", lambda v, dc=dc, bc=bc, rc=rc: v.tensor_tensor(
                dc, bc, rc, Op.mult),
                waits=[("S", s_betas[tt]), ("S", qk_v[("k", tt)][1])], incs=[("V", 1)])
            v_brn[tt] = b.val("V")

        qxp = {}
        kxp = {}
        for which, reg, dreg, stp in (("q", "qtok", "qfm", 512), ("k", "ktok", "kfm", 768)):
            for tt in range(4):
                src = aap(reg, 0, 128, tt * 1024, [[1, 1024]])
                dst = aap(dreg, 0, 128, tt * 128, [[stp, 8], [1, 128]])
                b.op("sync", lambda s, dst=dst, src=src: s.dma_start(
                    dst, src, transpose=True),
                    waits=[("V", qk_v[(which, tt)][0]), ("T", st.get("dn_scan_t", 0))],
                    incs=[("K", 16)])
                (qxp if which == "q" else kxp)[tt] = b.val("K")
        kb_v = {}
        for tt in range(4):
            tile = aap("ktok", 0, 128, tt * 1024, [[1, 1024]])
            brnb = apv(E["colb"], 0, 128, 192 + tt * 16, [[1, 16], [0, 64]])
            b.op("vector", lambda v, tile=tile, brnb=brnb: v.tensor_tensor(
                tile, tile, brnb, Op.mult),
                waits=[("K", kxp[tt]), ("V", v_brn[tt])], incs=[("V", 1)])
            kb_v[tt] = b.val("V")

        dn_sb = st.get("dn_sb", [[], []])
        dfree = st.get("dn_dfree", [0, 0])
        for blk_i in range(4):
            dbank = pb[5 + (blk_i % 2)]
            for p in range(8):
                obank = pb[4] if p % 2 == 0 else pb[7]
                for hh in range(2):
                    h = 2 * p + hh
                    sbank = pb[2] if hh == 0 else pb[3]
                    lhs = aap("kfm", hh * 64, 64, (h // 2) * 768 + blk_i * 128, [[1, 128]])
                    rhs = aap("qfm", hh * 64, 64, (h // 2) * 512 + blk_i * 128, [[1, 128]])
                    b.mm(sbank[:, 0:128], lhs, rhs, start=True, stop=True,
                         kbase=hh * 64, ksize=64,
                         waits=[("K", kxp[blk_i]), ("K", qxp[blk_i]),
                                ("S", s_betas[3])] + dn_sb[hh],
                         incs=[("T", 1)])
                    t_sa = b.val("T")
                    asl = (p * 2 + hh) % 4
                    A = aap("abuf", 0, 128, asl * 512, [[1, 128]])
                    brncol = apv(E["colb"], 0, 128, 192 + blk_i * 16 + h, [[1, 1]])
                    b.op("vector", lambda v, A=A, sbank=sbank, brncol=brncol:
                         v.scalar_tensor_tensor(A, sbank[:, 0:128], brncol, E["utc"][:],
                                                Op.mult, Op.mult),
                         waits=[("T", t_sa), ("V", v_brn[blk_i]),
                                ("T", abuf_cons[asl]), ("W", st["winit"])],
                         incs=[("V", 1)])
                    v_a = b.val("V")
                    dn_sb[hh] = [("V", v_a)]
                    vsl = aap("vtok", 0, 128, blk_i * 1024 + h * 64, [[1, 64]])
                    ow = [("V", v_a), ("V", evac_v[(2, 0, blk_i)]),
                          ("V", evac_v[(2, 1, blk_i)])]
                    if hh == 0:
                        ow += obank_cons[p % 2]
                    b.mm(obank[hh * 64:hh * 64 + 64, 0:128], vsl, A,
                         start=True, stop=False, tp=(0, hh * 64), waits=ow, sgc=True)
                    stsl = apv(E["stb"], hh * 64, 64, p * 64, [[1, 64]])
                    b.mm(obank[hh * 64:hh * 64 + 64, 0:128], stsl,
                         aap("qfm", hh * 64, 64, (h // 2) * 512 + blk_i * 128, [[1, 128]]),
                         start=False, stop=True, kbase=hh * 64, ksize=64,
                         tp=(hh * 64, hh * 64),
                         waits=[("V", st.get("dn_stb_v", 0)), ("G", st["ginit"])],
                         incs=[("T", 1)], sgc=True)
                    abuf_cons[asl] = b.val("T")
                    b.mm(dbank[hh * 64:hh * 64 + 64, p * 64:(p + 1) * 64],
                         aap("ktok", 0, 128, blk_i * 1024 + h * 64, [[1, 64]]),
                         vsl, start=True, stop=True, tp=(0, hh * 64),
                         waits=[("V", kb_v[blk_i]), ("V", dfree[blk_i % 2])],
                         incs=[("T", 1)], sgc=True)
                t_pair = b.val("T")
                adst = aap("aout", 0, 128, p * 512 + blk_i * 128, [[1, 128]])
                b.op("vector", lambda v, adst=adst, obank=obank: v.tensor_copy(
                    adst, obank[:, 0:128]),
                    waits=[("T", t_pair), ("K", max(qxp.values()))], incs=[("V", 1)])
                obank_cons[p % 2] = [("V", b.val("V"))]
            t_deltas = b.val("T")
            for p in range(8):
                for hh in range(2):
                    dsl = dbank[hh * 64:hh * 64 + 64, p * 64:(p + 1) * 64]
                    fsl = apv(E["stf"], hh * 64, 64, p * 64, [[1, 64]])
                    b.op("vector", lambda v, fsl=fsl, dsl=dsl: v.tensor_tensor(
                        fsl, fsl, dsl, Op.add),
                        waits=[("T", t_deltas), ("G", st["ginit"])], incs=[("V", 1)])
            for p in range(8):
                fsl = apv(E["stf"], 0, 128, p * 64, [[1, 64]])
                bsl = apv(E["stb"], 0, 128, p * 64, [[1, 64]])
                b.op("vector", lambda v, fsl=fsl, bsl=bsl: v.tensor_copy(bsl, fsl),
                     incs=[("V", 1)])
            st["dn_stb_v"] = b.val("V")
            dfree[blk_i % 2] = b.val("V")
        st["dn_sb"] = dn_sb
        st["dn_dfree"] = dfree
        st["dn_scan_t"] = b.val("T")
        v_aout = b.val("V")

        for oc in range(NCH):
            slot = oc % 2
            wsl = apv(wr, 0, 128, slot * 4096, [[128, 8], [1, 128]])
            src = E["dowo"].rearrange("kc k n -> k kc n")[:, :, oc * 128:(oc + 1) * 128]
            b.op("gpsimd", lambda g, wsl=wsl, src=src: g.dma_start(wsl, src),
                 waits=[("T", wslot_cons[slot])], incs=[("W", 16)])
            wv = b.val("W")
            bank = pb[oc % 2]
            for kc in range(NCH):
                b.mm(bank[:, 0:512],
                     apv(wr, 0, 128, slot * 4096 + kc * 128, [[1, 128]]),
                     aap("aout", 0, 128, kc * 512, [[1, 512]]),
                     start=(kc == 0), stop=(kc == NCH - 1),
                     waits=[("W", wv), ("V", v_aout)] +
                           (pbank_free[oc % 2] if kc == 0 else []),
                     incs=([("T", 1)] if kc == NCH - 1 else []))
            tv = b.val("T")
            b.op("vector", lambda v, oc=oc, bank=bank, s0=s0: v.tensor_tensor(
                u[:, oc, s0:s0 + 512], u[:, oc, s0:s0 + 512], bank[:, 0:512], Op.add),
                waits=[("T", tv), ("W", st["u_w"])], incs=[("V", 1)])
            pbank_free[oc % 2] = [("V", b.val("V"))]
            wslot_cons[slot] = b.val("T")
        st["xs_free_t"] = b.val("T")
        st["dn_fm_k"] = b.val("K")
        if ch == 0 and TAP == "ddn2":
            dbg_dump(b, E, [(xs[:, c, 0:512], c * 128, 128, 512) for c in range(NCH)])
            st["tapped"] = True
            return
        if ch == 0 and TAP == "ddn":
            dbg_dump(b, E, [(aap("aout", 0, 128, 0, [[1, 2048]]), 0, 128, 2048),
                            (aap("aout", 0, 128, 2048, [[1, 2048]]), 128, 128, 2048),
                            (apv(E["stf"], 0, 128, 0, [[1, 512]]), 256, 128, 512),
                            (apv(E["colb"], 0, 128, 0, [[1, 256]]), 384, 128, 256),
                            (apv(E["qsb"], 0, 128, 0, [[1, 128]]), 512, 128, 128),
                            (xs[:, 0, 0:512], 640, 128, 512),
                            (xs[:, 7, 0:512], 768, 128, 512)])
            st["tapped"] = True
            return
    st["pbank_free"] = pbank_free
    st["wring_free_t"] = b.val("T")
    b.spacer = old_spacer



def _stage_weights(inputs):
    w = {}
    w["wqkv"] = np.ascontiguousarray(
        np.asarray(inputs["L0_ffn_Wqkv"], np.float32).reshape(DEPTH, NCH, 128, 3 * D))
    w["wo"] = np.ascontiguousarray(
        np.asarray(inputs["L0_ffn_Wo"], np.float32).reshape(DEPTH, NCH, 128, D))
    W1 = np.asarray(inputs["L0_ffn_W1"], np.float32)
    W1p = np.zeros((DEPTH, 2, NIC * 128, D), np.float32)
    W1p[:, 0, :2730] = W1[:, :, :2730].transpose(0, 2, 1)
    W1p[:, 1, :2730] = W1[:, :, 2730:].transpose(0, 2, 1)
    W1p = W1p.reshape(DEPTH, 2, NIC, 128, NCH, 128).transpose(0, 1, 2, 4, 5, 3)
    w["w1"] = np.ascontiguousarray(W1p)
    W2 = np.asarray(inputs["L0_ffn_W2"], np.float32)
    W2p = np.zeros((DEPTH, NIC * 128, D), np.float32)
    W2p[:, :2730] = W2
    w["w2"] = np.ascontiguousarray(W2p.reshape(DEPTH, NIC, 128, D))
    qsr = np.asarray(inputs["L0_ffn_q_scale"], np.float32)
    ksr = np.asarray(inputs["L0_ffn_k_scale"], np.float32)
    w["qsr"] = np.ascontiguousarray(
        np.concatenate([qsr, qsr[:, 32:], qsr[:, :32]], axis=1))
    w["ksr"] = np.ascontiguousarray(
        np.concatenate([ksr, ksr[:, 32:], ksr[:, :32]], axis=1))
    w["gb"] = np.ascontiguousarray(
        np.stack([inputs["L0_ffn_ln_g"], inputs["L0_ffn_ln_b"]], axis=1).astype(np.float32))
    b1 = np.asarray(inputs["L0_ffn_b1"], np.float32)
    b1p = np.zeros((DEPTH, 2, NIC * 128), np.float32)
    b1p[:, 0, :2730] = b1[:, :2730]
    b1p[:, 1, :2730] = b1[:, 2730:]
    w["b1p"] = np.ascontiguousarray(
        b1p.reshape(DEPTH, 2, NIC, 128).transpose(0, 3, 1, 2))
    w["b2c"] = np.ascontiguousarray(
        np.asarray(inputs["L0_ffn_b2"], np.float32).reshape(DEPTH, NCH, 128)
        .transpose(0, 2, 1))
    for nm, key in (("owq", "L0_attn_Wq"), ("owk", "L0_attn_Wk"),
                    ("owv", "L0_attn_Wv"), ("owo", "L0_attn_Wo")):
        w[nm] = np.ascontiguousarray(
            np.asarray(inputs[key], np.float32).reshape(NCH, 128, D))
    w["owb"] = np.ascontiguousarray(
        np.asarray(inputs["L0_attn_Wb"], np.float32).reshape(NCH, 128, H))
    w["obb"] = np.asarray(inputs["L0_attn_bb"], np.float32).reshape(1, H)
    tabs, LT, maskR, UT = host_tables()
    w["tabs"] = tabs
    w["mlt"] = LT
    w["maskR"] = maskR
    w["mut"] = UT
    return w


def kernel(**inputs):
    x = np.asarray(inputs["L0_attn_x"], np.float32)
    ncores = 8
    w = _stage_weights(inputs)
    nc = build_program()
    in_maps = []
    for c in range(ncores):
        m = dict(w)
        m["x"] = np.ascontiguousarray(x[c].T)
        in_maps.append(m)
    res = run_bass_kernel_spmd(nc, in_maps, core_ids=list(range(ncores)))
    out = np.stack([np.asarray(r["o"]).T for r in res.results], axis=0)
    return out.astype(np.float32)


if __name__ == "__main__":
    build_program(verbose=True)
    print("built ok")


# version 06
# speedup vs baseline: 1.0489x; 1.0489x over previous
import numpy as np
import concourse.bass as bass
import concourse.mybir as mybir
from concourse.bass_utils import run_bass_kernel_spmd
from concourse.alu_op_type import AluOpType as Op

F32 = mybir.dt.float32
BF16 = mybir.dt.bfloat16
AF = mybir.ActivationFunctionType
AX = mybir.AxisListType.X

T = 2048
D = 1024
H = 16
DH = 64
NCH = 8
CA = 512
CF = 1024
DEPTH = 6
NIC = 22
WIN = 256

TAP = None
LAYERS = DEPTH
DO_OUTER = True
SECOND = True

ENGS = ("tensor", "vector", "scalar", "sync", "gpsimd")
ARN = 36864


class Bld:
    def __init__(self, nc):
        self.nc = nc
        self.q = {e: [] for e in ENGS}
        self.cnt = {}
        self.sems = {}
        self.lastwait = {}
        self.prev_mm_row = None
        self.spacer = None
        self.nspacer = 0
        self.ninstr = 0

    def sem(self, name, h):
        self.sems[name] = h
        self.cnt[name] = 0

    def bump(self, name, k=1):
        self.cnt[name] += k
        return self.cnt[name]

    def val(self, name):
        return self.cnt[name]

    def op(self, eng, f, waits=(), incs=()):
        wl = []
        for s, v in waits:
            if v <= 0:
                continue
            key = (eng, s)
            if self.lastwait.get(key, -1) >= v:
                continue
            self.lastwait[key] = v
            wl.append((s, v))
        for s, k in incs:
            self.bump(s, k)
        sems = self.sems
        self.ninstr += 1 + len(wl)

        def body(e, f=f, wl=tuple(wl), incs=tuple(incs)):
            for s, v in wl:
                e.wait_ge(sems[s], v)
            inst = f(e)
            for s, k in incs:
                inst.then_inc(sems[s], k)
        self.q[eng].append(body)

    def mm(self, out, lhsT, rhs, start, stop, kbase=0, ksize=128,
           waits=(), incs=(), tp=None, sgc=False):
        rows = 128 if ksize > 64 else (64 if ksize > 32 else 32)
        cur = (rows, kbase) if rows < 128 else None
        if cur is not None and self.prev_mm_row is not None and self.prev_mm_row != cur:
            so, sl, sr = self.spacer
            self.nspacer += 1
            import traceback
            fr = traceback.extract_stack()[-3]
            print("SPACER", self.prev_mm_row, "->", cur, "at", fr.lineno)
            self.ninstr += 1
            self.q["tensor"].append(
                lambda t, so=so, sl=sl, sr=sr: t.matmul(
                    so, sl, sr, start=False, stop=True, skip_group_check=True))
        self.prev_mm_row = cur
        self.op("tensor",
                lambda t, out=out, lhsT=lhsT, rhs=rhs, start=start, stop=stop, tp=tp,
                sgc=sgc: t.matmul(out, lhsT, rhs, start=start, stop=stop,
                                  tile_position=tp, skip_group_check=sgc),
                waits=waits, incs=incs)

    def finalize(self, blk):
        for e in ENGS:
            ops = self.q[e]
            if not ops:
                continue

            def body(eng, ops=ops):
                for f in ops:
                    f(eng)
            getattr(blk, e)(body)


def _arena_offsets():
    AO = {}
    off = 0
    for name, n in (("qtok", 4096), ("ktok", 4096), ("vtok", 6144), ("qfm", 4096),
                    ("kfm", 6144), ("pnbuf", 4096), ("ptbuf", 2048),
                    ("sqs", 2048), ("rott", 2048), ("abuf", 2048)):
        AO[name] = off
        off += n
    assert off <= ARN, off
    AO["p"] = AO["qtok"]
    AO["gact"] = AO["pnbuf"]
    AO["aout"] = AO["qtok"]
    return AO


AO = _arena_offsets()


def host_tables():
    t = np.arange(T, dtype=np.float64)
    j = np.arange(32, dtype=np.float64)
    inv_freq = 1.0 / (10000.0 ** (2.0 * j / DH))
    freqs = t[:, None] * inv_freq[None, :]
    base_scale = (2.0 * j + 0.4 * DH) / (1.4 * DH)
    power = (t - T // 2) / (WIN // 2)
    sc = base_scale[None, :] ** power[:, None]
    cos = np.cos(freqs)
    sin = np.sin(freqs)

    def tok(arr):
        return np.ascontiguousarray(
            arr.reshape(16, 128, 64).transpose(1, 0, 2)).astype(np.float32)

    cq = np.concatenate([cos * sc, cos * sc], axis=1)
    sq = np.concatenate([-sin * sc, sin * sc], axis=1)
    inv = 1.0 / sc
    ck = np.concatenate([cos * inv, cos * inv], axis=1)
    sk = np.concatenate([-sin * inv, sin * inv], axis=1)
    tabs = np.stack([tok(x) for x in (cq, sq, ck, sk)])

    LT = (np.arange(128)[:, None] <= np.arange(128)[None, :]).astype(np.float32)
    BIG = 1e9
    Rs = []
    for w0 in (False, True):
        for isub in (0, 1):
            i = 128 * isub + np.arange(128)[:, None]
            jj = np.arange(512)[None, :]
            d = i + 256 - jj
            valid = (d >= 0) & (d <= 256)
            if w0:
                valid &= (jj >= 256)
            M = np.where(valid, 0.0, -BIG).astype(np.float32)
            R = np.empty_like(M)
            R[0] = M[0]
            R[1:] = M[1:] - M[:-1]
            Rs.append(R)
    maskR = np.stack(Rs)
    UT = (np.arange(128)[:, None] <= np.arange(128)[None, :]).astype(np.float32)
    return tabs, LT, maskR, UT


def apv(tensor, pbase, npart, freeoff, dims):
    epp = 1
    for s in tensor.shape[1:]:
        epp *= s
    return bass.AP(tensor, pbase * epp + freeoff, [[epp, npart]] + [list(d) for d in dims])


_AR = [None]


def aap(name, pbase, npart, extra, dims):
    return apv(_AR[0], pbase, npart, AO[name] + extra, dims)


def build_program(verbose=False):
    nc = bass.Bass("TRN2", target_bir_lowering=False)
    E = {"nc": nc}
    E["dx"] = nc.dram_tensor("x", [D, T], F32, kind="ExternalInput").ap()
    E["dwqkv"] = nc.dram_tensor("wqkv", [DEPTH, NCH, 128, 3 * D], F32, kind="ExternalInput").ap()
    E["dwo"] = nc.dram_tensor("wo", [DEPTH, NCH, 128, D], F32, kind="ExternalInput").ap()
    E["dw1"] = nc.dram_tensor("w1", [DEPTH, 2, NIC, NCH, 128, 128], F32, kind="ExternalInput").ap()
    E["dw2"] = nc.dram_tensor("w2", [DEPTH, NIC, 128, D], F32, kind="ExternalInput").ap()
    E["dqs"] = nc.dram_tensor("qsr", [DEPTH, 128], F32, kind="ExternalInput").ap()
    E["dks"] = nc.dram_tensor("ksr", [DEPTH, 128], F32, kind="ExternalInput").ap()
    E["dgb"] = nc.dram_tensor("gb", [DEPTH, 2, D], F32, kind="ExternalInput").ap()
    E["db1"] = nc.dram_tensor("b1p", [DEPTH, 128, 2, NIC], F32, kind="ExternalInput").ap()
    E["db2"] = nc.dram_tensor("b2c", [DEPTH, 128, NCH], F32, kind="ExternalInput").ap()
    E["dowq"] = nc.dram_tensor("owq", [NCH, 128, D], F32, kind="ExternalInput").ap()
    E["dowk"] = nc.dram_tensor("owk", [NCH, 128, D], F32, kind="ExternalInput").ap()
    E["dowv"] = nc.dram_tensor("owv", [NCH, 128, D], F32, kind="ExternalInput").ap()
    E["dowo"] = nc.dram_tensor("owo", [NCH, 128, D], F32, kind="ExternalInput").ap()
    E["dowb"] = nc.dram_tensor("owb", [NCH, 128, H], F32, kind="ExternalInput").ap()
    E["dobb"] = nc.dram_tensor("obb", [1, H], F32, kind="ExternalInput").ap()
    E["dtabs"] = nc.dram_tensor("tabs", [4, 128, 16, 64], F32, kind="ExternalInput").ap()
    E["dlt"] = nc.dram_tensor("mlt", [128, 128], F32, kind="ExternalInput").ap()
    E["dmr"] = nc.dram_tensor("maskR", [4, 128, 512], F32, kind="ExternalInput").ap()
    E["dut"] = nc.dram_tensor("mut", [128, 128], F32, kind="ExternalInput").ap()
    E["do"] = nc.dram_tensor("o", [D, T], F32, kind="ExternalOutput").ap()
    E["dpark"] = nc.dram_tensor("xpark", [NCH, 128, T], F32).ap()

    ctx = []

    def sb(name, shape, dt):
        h = nc.sbuf_tensor(name, shape, dt)
        ctx.append(h)
        E[name] = h.__enter__()

    sb("u", [128, NCH, T], F32)
    sb("xs", [128, NCH, CF], BF16)
    sb("arena", [128, ARN], BF16)
    sb("tabf", [128, 4, 1024], BF16)
    sb("wring", [128, 8192], BF16)
    sb("rowf", [128, 2048], F32)
    sb("rowbf", [128, 2048], BF16)
    sb("qsb", [128, 256], BF16)
    sb("onesN", [1, 512], BF16)
    sb("onesrow", [1, 128], BF16)
    sb("tmpf", [128, 2, 512], F32)
    sb("colb", [128, 256], F32)
    sb("onec", [128, 1], BF16)
    sb("zeroc", [128, 1], BF16)
    sb("epsc", [128, 1], F32)
    sb("bndc", [128, 16], F32)
    sb("eps18", [128, 1], F32)
    sb("ltc", [128, 128], BF16)
    sb("mrc", [128, 4, 512], BF16)
    sb("utc", [128, 128], BF16)
    sb("gbsb", [1, 2 * D], BF16)
    sb("qsrow", [1, 384], BF16)
    sb("b1sb", [128, 2 * NIC], F32)
    sb("b2sb", [128, NCH], F32)
    sb("stf", [128, 8, 64], F32)
    sb("stb", [128, 8, 64], BF16)
    sb("bbrow", [1, H], BF16)
    _AR[0] = E["arena"]

    pbh = [nc.psum_tensor(f"pb{i}", [128, 512], F32) for i in range(8)]
    for h in pbh:
        ctx.append(h)
    E["pb"] = [h.__enter__() for h in pbh]

    b = Bld(nc)
    with nc.Block() as blk:
        for s in ("W", "K", "O", "T", "V", "S", "G"):
            h = nc.semaphore(s)
            ctx.append(h)
            b.sem(s, h.__enter__())
        b.spacer = (E["pb"][6][0:1, 508:509], E["onec"][:], E["onec"][:])
        emit_all(b, E)
        b.finalize(blk)

    if verbose:
        print(f"instr~{b.ninstr} spacers={b.nspacer} " +
              " ".join(f"{k}={v}" for k, v in b.cnt.items()))
    assert b.nspacer <= 8, f"unexpected spacers: {b.nspacer}"
    for h in reversed(ctx):
        h.__exit__(None, None, None)
    return nc



def emit_all(b, E):
    pb = E["pb"]
    u = E["u"]
    for c in range(NCH):
        b.op("gpsimd", lambda g, c=c: g.dma_start(u[:, c, :], E["dx"][c * 128:(c + 1) * 128, :]),
             incs=[("W", 16)])
    b.op("gpsimd", lambda g: g.dma_start(E["ltc"][:], E["dlt"]), incs=[("W", 16)])
    b.op("gpsimd", lambda g: g.dma_start(
        E["mrc"][:], E["dmr"].rearrange("a p n -> p a n")), incs=[("W", 16)])
    b.op("gpsimd", lambda g: g.dma_start(E["utc"][:], E["dut"]), incs=[("W", 16)])
    b.op("gpsimd", lambda g: g.dma_start(E["bbrow"][:], E["dobb"]), incs=[("W", 16)])
    b.op("gpsimd", lambda g: g.memset(E["onec"][:], 1.0))
    b.op("gpsimd", lambda g: g.memset(E["zeroc"][:], 0.0))
    b.op("gpsimd", lambda g: g.memset(E["epsc"][:], 1e-5))
    b.op("gpsimd", lambda g: g.memset(E["eps18"][:], 1e-18))
    b.op("gpsimd", lambda g: g.memset(E["onesrow"][:], 1.0))
    b.op("gpsimd", lambda g: g.memset(E["onesN"][:], 1.0))
    b.op("gpsimd", lambda g: g.memset(E["stf"][:], 0.0))
    b.op("gpsimd", lambda g: g.memset(E["stb"][:], 0.0), incs=[("G", 1)])
    st = {"winit": b.val("W"), "ginit": b.val("G"), "wring_free_t": 0, "xs_free_t": 0}
    st["u_w"] = st["winit"]

    park_and_ln(b, E, st)
    if TAP == "ln1":
        emit_output(b, E, st)
        return
    if emit_ffn6(b, E, st):
        if TAP in ("l0a", "l0f"):
            emit_output(b, E, st)
        return
    if TAP == "ffn6":
        emit_output(b, E, st)
        return
    unpark_merge(b, E, st)
    if TAP == "x1":
        emit_output(b, E, st)
        return
    if DO_OUTER:
        emit_deltanet(b, E, st)
    if st.get("tapped"):
        return
    if TAP == "x2" or not SECOND:
        emit_output(b, E, st)
        return
    park_and_ln(b, E, st)
    if emit_ffn6(b, E, st):
        emit_output(b, E, st)
        return
    unpark_merge(b, E, st)
    emit_output(b, E, st)


def barrier_waits(b):
    return [("T", b.val("T")), ("V", b.val("V")), ("S", b.val("S")),
            ("W", b.val("W")), ("K", b.val("K")), ("O", b.val("O")),
            ("G", b.val("G"))]



def emit_boundary_snapshot(b, E, st):
    u = E["u"]
    for bi, t in enumerate((511, 1023, 1535)):
        for c in range(4, NCH):
            dst = E["bndc"][:, bi * 4 + c - 4:bi * 4 + c - 3]
            b.op("vector", lambda v, dst=dst, c=c, t=t: v.tensor_copy(
                dst, u[:, c, t:t + 1]), waits=[("W", st["u_w"])], incs=[("V", 1)])
    st["v_bnd"] = b.val("V")


def emit_ln_slice(b, E, st, s0, shifted, gb, out_mode, park_o=None):
    pb = E["pb"]
    u = E["u"]
    xs = E["xs"]
    x0 = s0 % CF
    par = (s0 // 512) % 2
    n = 512
    v_entry = b.val("V")

    casts = []
    for c in range(NCH):
        dst = xs[:, c, x0:x0 + n]
        sh = shifted and c >= 4
        xw = [("T", st["xs_free_t"]), ("W", st["u_w"])]
        if sh and s0 == 0:
            b.op("vector", lambda v, dst=dst, c=c: v.tensor_copy(
                dst[:, 1:n], u[:, c, 0:n - 1]), waits=xw, incs=[("V", 1)])
            b.op("vector", lambda v, dst=dst: v.tensor_copy(dst[:, 0:1], E["zeroc"][:]),
                 waits=[("G", st["ginit"])], incs=[("V", 1)])
        elif sh:
            b.op("vector", lambda v, dst=dst, c=c, s0=s0: v.tensor_copy(
                dst[:, 1:n], u[:, c, s0:s0 + n - 1]), waits=xw, incs=[("V", 1)])
            bcol = E["bndc"][:, (s0 // 512 - 1) * 4 + c - 4:(s0 // 512 - 1) * 4 + c - 3]
            b.op("vector", lambda v, dst=dst, bcol=bcol: v.tensor_copy(
                dst[:, 0:1], bcol), incs=[("V", 1)])
        else:
            b.op("vector", lambda v, dst=dst, c=c, s0=s0: v.tensor_copy(
                dst, u[:, c, s0:s0 + n]), waits=xw, incs=[("V", 1)])
        casts.append(b.val("V"))

    for c in range(NCH):
        b.mm(pb[6][0:1, 0:n], E["onec"][:], xs[:, c, x0:x0 + n],
             start=(c == 0), stop=(c == NCH - 1),
             waits=[("V", casts[c])] + ([("V", v_entry), ("G", st["ginit"])]
                                        if c == 0 else []),
             incs=([("T", 1)] if c == NCH - 1 else []))
    t_sx = b.val("T")
    sq_cons = st.get("lnsq_cons", [0, 0])
    for c in range(NCH):
        slot = c % 2
        sq = aap("sqs", 0, 128, slot * 1024, [[1, n]])
        b.op("vector", lambda v, sq=sq, c=c, x0=x0: v.tensor_tensor(
            sq, xs[:, c, x0:x0 + n], xs[:, c, x0:x0 + n], Op.mult),
            waits=[("T", sq_cons[slot])], incs=[("V", 1)])
        vv = b.val("V")
        b.mm(pb[7][0:1, 0:n], E["onec"][:], sq, start=(c == 0), stop=(c == NCH - 1),
             waits=[("V", vv)] + ([("V", v_entry)] if c == 0 else []),
             incs=[("T", 1)])
        sq_cons[slot] = b.val("T")
    st["lnsq_cons"] = sq_cons
    t_sx2 = b.val("T")

    rf = E["rowf"]
    ro = par * 512
    murow = rf[0:1, 0:n]
    mu2 = rf[0:1, 512:512 + n]
    var = rf[0:1, 1024:1024 + n]
    lnr = rf[0:1, 1536:1536 + n]
    inv = 1.0 / float(D)
    b.op("vector", lambda v: v.tensor_scalar(murow, pb[6][0:1, 0:n], inv, None, Op.mult),
         waits=[("T", t_sx), ("S", st.get("ln_rows_s", 0))], incs=[("V", 1)])
    b.op("vector", lambda v: v.tensor_tensor(mu2, murow, murow, Op.mult), incs=[("V", 1)])
    b.op("vector", lambda v: v.scalar_tensor_tensor(var, pb[7][0:1, 0:n], inv, mu2,
                                                    Op.mult, Op.subtract),
         waits=[("T", t_sx2)], incs=[("V", 1)])
    v_var = b.val("V")
    b.op("scalar", lambda sc: sc.activation(lnr, var, AF.Ln, bias=E["epsc"][0:1, :],
                                            scale=1.0),
         waits=[("V", v_var), ("G", st["ginit"])], incs=[("S", 1)])
    rstdf = mu2
    b.op("scalar", lambda sc: sc.activation(rstdf, lnr, AF.Exp, bias=0.0, scale=-0.5),
         incs=[("S", 1)])
    s_rstd = b.val("S")
    st["ln_rows_s"] = s_rstd
    rbf_cons = st.get("rowbf_cons", [0])
    rbf = E["rowbf"]
    rstd_hi = rbf[0:1, 0:n]
    rstd_lo = rbf[0:1, 512:512 + n]
    nm_hi = rbf[0:1, 1024:1024 + n]
    nm_lo = rbf[0:1, 1536:1536 + n]
    b.op("vector", lambda v: v.tensor_copy(rstd_hi, rstdf),
         waits=[("S", s_rstd), ("T", rbf_cons[0])], incs=[("V", 1)])
    b.op("vector", lambda v: v.tensor_tensor(rstd_lo, rstdf, rstd_hi, Op.subtract),
         incs=[("V", 1)])
    nmursf = rf[0:1, 1536:1536 + n]
    b.op("vector", lambda v: v.scalar_tensor_tensor(nmursf, murow, -1.0, rstdf,
                                                    Op.mult, Op.mult),
         waits=[("S", s_rstd)], incs=[("V", 1)])
    b.op("vector", lambda v: v.tensor_copy(nm_hi, nmursf), incs=[("V", 1)])
    b.op("vector", lambda v: v.tensor_tensor(nm_lo, nmursf, nm_hi, Op.subtract),
         incs=[("V", 1)])
    v_rows = b.val("V")

    if gb is None:
        b.mm(pb[4][:, 0:n], E["onesrow"][:], rstd_hi, start=True, stop=False,
             ksize=1, waits=[("V", v_rows), ("V", v_entry)])
        b.mm(pb[4][:, 0:n], E["onesrow"][:], rstd_lo, start=False, stop=True,
             ksize=1, incs=[("T", 1)])
        b.mm(pb[5][:, 0:n], E["onesrow"][:], nm_hi, start=True, stop=False, ksize=1)
        b.mm(pb[5][:, 0:n], E["onesrow"][:], nm_lo, start=False, stop=True,
             ksize=1, incs=[("T", 1)])

    applies = []
    for c in range(NCH):
        if gb is not None:
            glhs = E["gbsb"][0:1, c * 128:(c + 1) * 128]
            blhs = E["gbsb"][0:1, D + c * 128:D + (c + 1) * 128]
            bw = [("V", v_rows), ("W", st.get("w_params", 0)),
                  ("V", v_entry if c == 0 else 0)]
            if applies:
                bw.append(("V", applies[-1]))
            b.mm(pb[4][:, 0:n], glhs, rstd_hi, start=True, stop=False, ksize=1,
                 waits=bw)
            b.mm(pb[4][:, 0:n], glhs, rstd_lo, start=False, stop=True, ksize=1,
                 incs=[("T", 1)])
            b.mm(pb[5][:, 0:n], glhs, nm_hi, start=True, stop=False, ksize=1)
            b.mm(pb[5][:, 0:n], glhs, nm_lo, start=False, stop=False, ksize=1)
            b.mm(pb[5][:, 0:n], blhs, E["onesN"][:, 0:n], start=False, stop=True,
                 ksize=1, incs=[("T", 1)])
        t_bc = b.val("T")
        slot = c % 2
        tf = E["tmpf"][:, slot, :]
        sh = shifted and c >= 4
        src1 = xs[:, c, x0:x0 + n] if sh else u[:, c, s0:s0 + n]
        b.op("vector", lambda v, tf=tf, src1=src1: v.tensor_tensor(
            tf, src1, pb[4][:, 0:n], Op.mult),
            waits=[("T", t_bc), ("W", st["u_w"]), ("T", t_sx), ("T", t_sx2)],
            incs=[("V", 1)])
        if out_mode == "u":
            dst = u[:, c, s0:s0 + n]
            pw = [("O", park_o[c])] if park_o else []
        else:
            dst = xs[:, c, x0:x0 + n]
            pw = []
        b.op("vector", lambda v, dst=dst, tf=tf: v.tensor_tensor(
            dst, tf, pb[5][:, 0:n], Op.add),
            waits=[("T", t_sx), ("T", t_sx2)] + pw, incs=[("V", 1)])
        applies.append(b.val("V"))
    rbf_cons[0] = b.val("T")
    st["rowbf_cons"] = rbf_cons
    return b.val("V")


def park_and_ln(b, E, st):
    u = E["u"]
    park_o = []
    bw = barrier_waits(b)
    for c in range(NCH):
        b.op("gpsimd", lambda g, c=c: g.dma_start(E["dpark"][c], u[:, c, :]),
             waits=bw, incs=[("O", 16)])
    park_o = [b.val("O")] * NCH
    for sl in range(4):
        emit_ln_slice(b, E, st, sl * 512, shifted=False, gb=None, out_mode="u",
                      park_o=park_o)
    st["xs_free_t"] = 0


def unpark_merge(b, E, st):
    u = E["u"]
    bw = barrier_waits(b)
    for c in range(NCH):
        b.op("vector", lambda v, c=c: v.tensor_scalar(
            u[:, c, :], u[:, c, :], 0.5, None, Op.mult),
            waits=bw, incs=[("V", 1)])
        vv = b.val("V")
        b.op("gpsimd", lambda g, c=c: g.dma_start(
            u[:, c, :], E["dpark"][c], accum_op=Op.add),
            waits=bw + [("V", vv)], incs=[("W", 16)])
    st["u_w"] = b.val("W")


def emit_output(b, E, st):
    u = E["u"]
    bw = barrier_waits(b)
    for c in range(NCH):
        b.op("sync", lambda s, c=c: s.dma_start(
            E["do"][c * 128:(c + 1) * 128, :], u[:, c, :]), waits=bw, incs=[("K", 16)])
    fin = b.val("K")
    b.op("sync", lambda s: None, waits=[("K", fin)])



def dbg_dump(b, E, regions):
    bw = barrier_waits(b)
    for ap, r0, nr, ncol in regions:
        b.op("gpsimd", lambda g, ap=ap, r0=r0, nr=nr, ncol=ncol: g.dma_start(
            E["do"][r0:r0 + nr, 0:ncol], ap), waits=bw, incs=[("W", 16)])
    fin = b.val("W")
    b.op("sync", lambda s: None, waits=[("W", fin)])


def emit_ffn6(b, E, st):
    for li in range(LAYERS):
        emit_layer_params(b, E, st, li)
        if st.get("tapped"):
            return True
        if emit_attn_sublayer(b, E, st, li):
            return True
        if TAP == "l0a" and li == 0:
            return True
        emit_ffn_sublayer(b, E, st, li)
        if TAP == "l0f" and li == 0:
            return True
    return False


def emit_layer_params(b, E, st, li):
    pb = E["pb"]
    bw = [("T", b.val("T")), ("V", b.val("V")), ("S", b.val("S"))]
    b.op("gpsimd", lambda g: g.dma_start(
        E["gbsb"][:], E["dgb"][li:li + 1].rearrange("o s n -> o (s n)")),
        waits=bw, incs=[("W", 16)])
    b.op("gpsimd", lambda g: g.dma_start(
        E["b1sb"][:].rearrange("p (s i) -> p s i", s=2), E["db1"][li]), incs=[("W", 16)])
    b.op("gpsimd", lambda g: g.dma_start(E["b2sb"][:], E["db2"][li]), incs=[("W", 16)])
    b.op("gpsimd", lambda g: g.dma_start(E["qsrow"][0:1, 256:384], E["dqs"][li:li + 1, :]),
         incs=[("W", 16)])
    b.op("gpsimd", lambda g: g.dma_start(E["qsrow"][0:1, 128:256], E["dks"][li:li + 1, :]),
         incs=[("W", 16)])
    wv = b.val("W")
    st["w_params"] = wv
    qsr = E["qsrow"]
    b.op("vector", lambda v: v.tensor_scalar(qsr[0:1, 0:128], qsr[0:1, 256:384],
                                             8.0, None, Op.mult),
         waits=bw + [("W", wv)], incs=[("V", 1)])
    v_rows = b.val("V")
    for i, src0 in enumerate((0, 64, 128, 192)):
        b.mm(pb[6][:, 0:64], E["onesrow"][:], qsr[0:1, src0:src0 + 64],
             start=True, stop=True, ksize=1,
             waits=[("V", b.val("V"))], incs=[("T", 1)])
        tv = b.val("T")
        b.op("vector", lambda v, i=i: v.tensor_copy(E["qsb"][:, i * 64:(i + 1) * 64],
                                                    pb[6][:, 0:64]),
             waits=[("T", tv)], incs=[("V", 1)])
    v_qsb = b.val("V")
    tb = apv(E["wring"], 0, 128, 0, [[1024, 4], [1, 1024]])
    b.op("gpsimd", lambda g, tb=tb: g.dma_start(
        tb, E["dtabs"].rearrange("a p t d -> p a (t d)")),
        waits=[("T", st["wring_free_t"])], incs=[("W", 16)])
    w_tb = b.val("W")
    for i in range(4):
        bc = apv(E["qsb"], 0, 128, i * 64, [[0, 16], [1, 64]])
        tbi = apv(E["wring"], 0, 128, i * 1024, [[1, 1024]])
        b.op("vector", lambda v, i=i, bc=bc, tbi=tbi: v.tensor_tensor(
            E["tabf"][:, i, :], tbi, bc, Op.mult),
            waits=[("V", v_qsb), ("W", w_tb)], incs=[("V", 1)])
    st["v_tabs"] = b.val("V")
    st["tabs_wring_v"] = b.val("V")
    if TAP == "dtab" and li == 0:
        dbg_dump(b, E, [(E["qsrow"][0:1, 0:384], 0, 1, 384),
                        (E["qsb"][:, 0:256], 1, 128, 256),
                        (apv(E["tabf"], 0, 128, 0, [[1, 2048]]), 129, 128, 2048),
                        (apv(E["tabf"], 0, 128, 2048, [[1, 2048]]), 257, 128, 2048)])
        st["tapped"] = True


def emit_attn_sublayer(b, E, st, li):
    pb = E["pb"]
    u = E["u"]
    xs = E["xs"]
    wr = E["wring"]
    emit_boundary_snapshot(b, E, st)

    wslot_cons = [st["wring_free_t"], st["wring_free_t"]]
    tabv = st.get("tabs_wring_v", 0)
    pbank_free = st.get("pbank_free", [[] for _ in range(8)])
    ring_s = st.get("attn_ring_s", [0, 0, 0, 0])
    ring_v = st.get("attn_ring_v", [0, 0, 0, 0])
    pn_cons = st.get("pn_cons", [0, 0])
    pt_cons = st.get("pt_cons", [0, 0])
    sbank_cons = st.get("sbank_cons", [[], []])
    sidx0 = st.get("attn_sidx", 0)

    for ch in range(4):
        s0 = ch * CA
        casts = []
        for c in range(NCH):
            dst = xs[:, c, 0:CA]
            sh = c >= 4
            xw = [("T", st["xs_free_t"]), ("W", st["u_w"])]
            if sh and s0 == 0:
                b.op("vector", lambda v, dst=dst, c=c: v.tensor_copy(
                    dst[:, 1:CA], u[:, c, 0:CA - 1]), waits=xw, incs=[("V", 1)])
                b.op("vector", lambda v, dst=dst: v.tensor_copy(dst[:, 0:1],
                     E["zeroc"][:]), waits=[("G", st["ginit"])], incs=[("V", 1)])
            elif sh:
                b.op("vector", lambda v, dst=dst, c=c, s0=s0: v.tensor_copy(
                    dst[:, 1:CA], u[:, c, s0:s0 + CA - 1]), waits=xw, incs=[("V", 1)])
                bcol = E["bndc"][:, (s0 // 512 - 1) * 4 + c - 4:(s0 // 512 - 1) * 4 + c - 3]
                b.op("vector", lambda v, dst=dst, bcol=bcol: v.tensor_copy(
                    dst[:, 0:1], bcol), incs=[("V", 1)])
            else:
                src = u[:, c, s0:s0 + CA]
                b.op("vector", lambda v, dst=dst, src=src: v.tensor_copy(dst, src),
                     waits=xw, incs=[("V", 1)])
            casts.append(b.val("V"))

        hw = [("T", st.get("attn_prev_t", 0))]
        if ch > 0:
            b.op("vector", lambda v: v.tensor_copy(
                aap("kfm", 0, 128, 0, [[768, 8], [1, 256]]),
                aap("kfm", 0, 128, 512, [[768, 8], [1, 256]])),
                waits=hw, incs=[("V", 1)])
            b.op("vector", lambda v: v.tensor_copy(
                aap("vtok", 0, 128, 0, [[1, 2048]]),
                aap("vtok", 0, 128, 4096, [[1, 2048]])),
                waits=hw, incs=[("V", 1)])
        v_halo = b.val("V")

        evac_v = {}
        for cg in range(6):
            slot = cg % 2
            wsl = apv(wr, 0, 128, slot * 4096, [[512, 8], [1, 512]])
            src = E["dwqkv"][li].rearrange("kc k n -> k kc n")[:, :, cg * 512:(cg + 1) * 512]
            b.op("gpsimd", lambda g, wsl=wsl, src=src: g.dma_start(wsl, src),
                 waits=[("T", wslot_cons[slot]), ("V", tabv)], incs=[("W", 16)])
            wv = b.val("W")
            for tt in range(4):
                bank = pb[tt % 2]
                for kc in range(NCH):
                    b.mm(bank[:, 0:512],
                         xs[:, kc, tt * 128:(tt + 1) * 128],
                         apv(wr, 0, 128, slot * 4096 + kc * 512, [[1, 512]]),
                         start=(kc == 0), stop=(kc == NCH - 1),
                         waits=[("W", wv), ("V", casts[kc])] +
                               (pbank_free[tt % 2] if kc == 0 else []),
                         incs=([("T", 1)] if kc == NCH - 1 else []))
                tv = b.val("T")
                if cg < 2:
                    dst = aap("qtok", 0, 128, tt * 1024 + cg * 512, [[1, 512]])
                elif cg < 4:
                    dst = aap("ktok", 0, 128, tt * 1024 + (cg - 2) * 512, [[1, 512]])
                else:
                    dst = aap("vtok", 0, 128, (2 + tt) * 1024 + (cg - 4) * 512, [[1, 512]])
                wt = [("T", tv), ("V", v_halo)]
                if cg < 2:
                    wt += [("T", st.get("aout_free_t", 0)), ("K", st.get("qtok_free_k", 0))]
                elif cg < 4:
                    wt += [("K", st.get("ktok_free_k", 0))]
                else:
                    wt += [("T", st.get("attn_prev_t", 0))]
                b.op("vector", lambda v, dst=dst, bank=bank: v.tensor_copy(
                    dst, bank[:, 0:512]), waits=wt, incs=[("V", 1)])
                pbank_free[tt % 2] = [("V", b.val("V"))]
                evac_v[(cg, tt)] = b.val("V")
            wslot_cons[slot] = b.val("T")
        v_vall = max(evac_v[(cg, tt)] for cg in (4, 5) for tt in range(4))
        if ch == 0 and TAP == "dq":
            dbg_dump(b, E, [(aap("qtok", 0, 128, tt * 1024, [[1, 1024]]),
                             tt * 128, 128, 1024) for tt in range(4)])
            return True
        if ch == 0 and TAP == "dk0":
            dbg_dump(b, E, [(aap("ktok", 0, 128, tt * 1024, [[1, 1024]]),
                             tt * 128, 128, 1024) for tt in range(4)])
            return True
        if ch == 0 and TAP == "dxs":
            dbg_dump(b, E, [(xs[:, c, 0:CA], c * 128, 128, 512) for c in range(NCH)])
            return True

        v_tabs = st.get("v_tabs", 0)
        sq_cons = st.get("asq_cons", [0, 0])
        rot_cons = st.get("rot_cons", [0, 0])
        q_done = {}
        k_done = {}
        for which, reg, ctab, stab, rncol0 in (("q", "qtok", 0, 1, 0),
                                               ("k", "ktok", 2, 3, 64)):
            for tt in range(4):
                ttg = ch * 4 + tt
                g0 = 0 if which == "q" else 2
                ready = [("V", max(evac_v[(g0, tt)], evac_v[(g0 + 1, tt)]))]
                slot = tt % 2
                sq = aap("sqs", 0, 128, slot * 1024, [[1, 1024]])
                tile = aap(reg, 0, 128, tt * 1024, [[1, 1024]])
                b.op("vector", lambda v, sq=sq, tile=tile: v.tensor_tensor(
                    sq, tile, tile, Op.mult),
                    waits=ready + [("V", sq_cons[slot])], incs=[("V", 1)])
                nrm = apv(E["colb"], 0, 128, rncol0 + tt * 16, [[1, 16]])
                sq3 = aap("sqs", 0, 128, slot * 1024, [[64, 16], [1, 64]])
                b.op("vector", lambda v, nrm=nrm, sq3=sq3: v.reduce_sum(nrm, sq3, axis=AX),
                     incs=[("V", 1)])
                v_n = b.val("V")
                b.op("scalar", lambda sc, nrm=nrm: sc.activation(
                    nrm, nrm, AF.Ln, bias=E["eps18"][:, 0:1], scale=1.0),
                    waits=[("V", v_n), ("G", st["ginit"])], incs=[("S", 1)])
                b.op("scalar", lambda sc, nrm=nrm: sc.activation(
                    nrm, nrm, AF.Exp, bias=0.0, scale=-0.5), incs=[("S", 1)])
                s_rn = b.val("S")
                rsl = tt % 2
                tmp = aap("rott", 0, 128, rsl * 1024, [[1, 1024]])
                rotv = aap(reg, 0, 128, tt * 1024 + 32, [[64, 16], [-32, 2], [1, 32]])
                stv = apv(E["tabf"], 0, 128, stab * 1024 + ttg * 64,
                          [[0, 16], [32, 2], [1, 32]])
                ctv = apv(E["tabf"], 0, 128, ctab * 1024 + ttg * 64, [[0, 16], [1, 64]])
                b.op("vector", lambda v, tmp=tmp, rotv=rotv, stv=stv: v.tensor_tensor(
                    tmp, rotv, stv, Op.mult),
                    waits=[("V", v_tabs), ("V", rot_cons[rsl])], incs=[("V", 1)])
                xr = sq
                b.op("vector", lambda v, xr=xr, tile=tile, ctv=ctv: v.tensor_tensor(
                    xr, tile, ctv, Op.mult), incs=[("V", 1)])
                kf0 = aap("rott", 0, 128, (rsl ^ 1) * 1024, [[1, 1024]])
                b.op("vector", lambda v, kf0=kf0, xr=xr, tmp=tmp: v.tensor_tensor(
                    kf0, xr, tmp, Op.add), incs=[("V", 1)])
                rnb = apv(E["colb"], 0, 128, rncol0 + tt * 16, [[1, 16], [0, 64]])
                b.op("vector", lambda v, tile=tile, kf0=kf0, rnb=rnb: v.tensor_tensor(
                    tile, kf0, rnb, Op.mult),
                    waits=[("S", s_rn)], incs=[("V", 1)])
                if which == "q":
                    q_done[tt] = b.val("V")
                else:
                    k_done[tt] = b.val("V")
                sq_cons[slot] = b.val("V")
                rot_cons[rsl] = b.val("V")
        st["asq_cons"] = sq_cons
        st["rot_cons"] = rot_cons
        if ch == 0 and TAP == "dnrm":
            dbg_dump(b, E, [(apv(E["colb"], 0, 128, 0, [[1, 256]]), 0, 128, 256),
                            (aap("sqs", 0, 128, 0, [[1, 1024]]), 128, 128, 1024),
                            (aap("sqs", 0, 128, 1024, [[1, 1024]]), 256, 128, 1024)])
            return True
        if ch == 0 and TAP == "dqr":
            dbg_dump(b, E, [(aap("qtok", 0, 128, tt * 1024, [[1, 1024]]),
                             tt * 128, 128, 1024) for tt in range(4)])
            return True
        if ch == 0 and TAP == "dkr":
            dbg_dump(b, E, [(aap("ktok", 0, 128, tt * 1024, [[1, 1024]]),
                             tt * 128, 128, 1024) for tt in range(4)] +
                     [(apv(E["colb"], 0, 128, 0, [[1, 256]]), 512, 128, 256)])
            return True

        fmw = [("T", st.get("attn_prev_t", 0))]
        qxp = {}
        kxp = {}
        for tt in range(4):
            src = aap("qtok", 0, 128, tt * 1024, [[1, 1024]])
            dst = aap("qfm", 0, 128, tt * 128, [[512, 8], [1, 128]])
            b.op("sync", lambda s, dst=dst, src=src: s.dma_start(dst, src, transpose=True),
                 waits=[("V", q_done[tt])] + fmw, incs=[("K", 16)])
            qxp[tt] = b.val("K")
        for tt in range(4):
            src = aap("ktok", 0, 128, tt * 1024, [[1, 1024]])
            dst = aap("kfm", 0, 128, 256 + tt * 128, [[768, 8], [1, 128]])
            b.op("sync", lambda s, dst=dst, src=src: s.dma_start(dst, src, transpose=True),
                 waits=[("V", k_done[tt]), ("V", v_halo)] + fmw, incs=[("K", 16)])
            kxp[tt] = b.val("K")
        k_allxp = b.val("K")
        st["qtok_free_k"] = max(qxp.values())
        st["ktok_free_k"] = k_allxp
        if ch == 0 and TAP == "dqfm":
            dbg_dump(b, E, [(aap("qfm", 0, 128, 0, [[1, 2048]]), 0, 128, 2048),
                            (aap("qfm", 0, 128, 2048, [[1, 2048]]), 128, 128, 2048),
                            (aap("kfm", 0, 128, 0, [[1, 2048]]), 256, 128, 2048),
                            (aap("kfm", 0, 128, 2048, [[1, 2048]]), 384, 128, 2048)])
            return True

        sidx = sidx0
        aout_v = []
        obank_cons = st.get("obank_cons", [[], []])
        for wl in range(2):
            wg = 2 * ch + wl
            w0 = (wg == 0)
            kc0 = 256 if w0 else wl * 256
            N = 256 if w0 else 512
            nm = 2 if w0 else 4
            for p in range(8):
                obank = pb[4 + (p % 2)]
                for hh in range(2):
                    h = 2 * p + hh
                    pvw = []
                    for isub in range(2):
                        sb_i = sidx % 2
                        sbank = pb[2 + sb_i]
                        lhs = aap("qfm", hh * 64, 64,
                                  (h // 2) * 512 + wl * 256 + isub * 128, [[1, 128]])
                        rhs = aap("kfm", hh * 64, 64, (h // 2) * 768 + kc0, [[1, N]])
                        b.mm(sbank[:, 0:N], lhs, rhs, start=True, stop=False,
                             kbase=hh * 64, ksize=64,
                             waits=[("K", k_allxp)] + sbank_cons[sb_i])
                        vi = (2 + isub) if w0 else isub
                        mrhs = E["mrc"][:, vi, 256:512] if w0 else E["mrc"][:, vi, 0:512]
                        b.mm(sbank[:, 0:N], E["ltc"][:], mrhs, start=False, stop=True,
                             waits=[("W", st["winit"])], incs=[("T", 1)])
                        t_s = b.val("T")
                        ri = sidx % 4
                        negm = apv(E["colb"], 0, 128, 128 + ri * 4, [[1, 1]])
                        b.op("vector", lambda v, negm=negm, sbank=sbank, N=N: v.reduce_max(
                            negm, sbank[:, 0:N], axis=AX, negate=True),
                            waits=[("T", t_s), ("S", ring_s[ri])],
                            incs=[("V", 1)])
                        v_m = b.val("V")
                        lcol = apv(E["colb"], 0, 128, 130 + ri * 4, [[1, 1]])
                        pdst = aap("pnbuf", 0, 128, hh * 2048 + isub * 512, [[1, N]])
                        b.op("scalar", lambda sc, pdst=pdst, sbank=sbank, negm=negm,
                             lcol=lcol, N=N: sc.activation(
                                 pdst, sbank[:, 0:N], AF.Exp, bias=negm, scale=1.0,
                                 accum_out=lcol),
                             waits=[("V", v_m)], incs=[("S", 1)])
                        s_exp = b.val("S")
                        sbank_cons[sb_i] = [("S", s_exp)]
                        rl = apv(E["colb"], 0, 128, 131 + ri * 4, [[1, 1]])
                        b.op("vector", lambda v, rl=rl, lcol=lcol: v.reciprocal(rl, lcol),
                             waits=[("S", s_exp)], incs=[("V", 1)])
                        v_rl = b.val("V")
                        pnd = aap("abuf", 0, 128, hh * 1024 + isub * 512, [[1, N]])
                        b.op("scalar", lambda sc, pnd=pnd, pdst=pdst, rl=rl: sc.activation(
                            pnd, pdst, AF.Copy, bias=0.0, scale=rl),
                            waits=[("V", v_rl), ("K", pn_cons[hh])], incs=[("S", 1)])
                        pvw.append(("S", b.val("S")))
                        ring_s[ri] = b.val("S")
                        ring_v[ri] = v_rl
                        sidx += 1
                    psrc = aap("abuf", 0, 128, hh * 1024, [[1, 1024]])
                    ptd = aap("ptbuf", 0, 128, hh * 1024, [[128, 8], [1, 128]])
                    b.op("sync", lambda s, ptd=ptd, psrc=psrc: s.dma_start(
                        ptd, psrc, transpose=True),
                        waits=pvw + [("T", pt_cons[hh])], incs=[("K", 16)])
                    k_pt = b.val("K")
                    pn_cons[hh] = k_pt
                    for jm in range(nm):
                        vti = wl * 2 + jm + (2 if w0 else 0)
                        vsl = aap("vtok", 0, 128, vti * 1024 + h * 64, [[1, 64]])
                        prhs = aap("ptbuf", 0, 128, hh * 1024 + jm * 128,
                                   [[512, 2], [1, 128]])
                        ow = [("K", k_pt), ("V", v_vall)]
                        if jm == 0:
                            ow += obank_cons[p % 2]
                        b.mm(obank[hh * 64:hh * 64 + 64, 0:256], vsl, prhs,
                             start=(jm == 0), stop=(jm == nm - 1), tp=(0, hh * 64),
                             waits=ow, incs=([("T", 1)] if jm == nm - 1 else []),
                             sgc=True)
                    pt_cons[hh] = b.val("T")
                t_pv = b.val("T")
                adst = aap("aout", 0, 128, p * 512 + wl * 256, [[1, 256]])
                b.op("vector", lambda v, adst=adst, obank=obank: v.tensor_copy(
                    adst, obank[:, 0:256]),
                    waits=[("T", t_pv), ("K", st["qtok_free_k"])], incs=[("V", 1)])
                obank_cons[p % 2] = [("V", b.val("V"))]
                aout_v.append(b.val("V"))
        st["obank_cons"] = obank_cons
        st["attn_prev_t"] = b.val("T")
        if ch == 0 and TAP == "dao":
            dbg_dump(b, E, [(aap("aout", 0, 128, 0, [[1, 2048]]), 0, 128, 2048),
                            (aap("aout", 0, 128, 2048, [[1, 2048]]), 128, 128, 2048)])
            return True
        if ch == 0 and TAP == "dpn":
            dbg_dump(b, E, [(aap("abuf", 0, 128, 0, [[1, 2048]]), 0, 128, 2048),
                            (aap("ptbuf", 0, 128, 0, [[1, 2048]]), 128, 128, 2048)])
            return True

        v_aout = max(aout_v)
        for oc in range(NCH):
            slot = oc % 2
            wsl = apv(wr, 0, 128, slot * 4096, [[128, 8], [1, 128]])
            src = E["dwo"][li].rearrange("kc k n -> k kc n")[:, :, oc * 128:(oc + 1) * 128]
            b.op("gpsimd", lambda g, wsl=wsl, src=src: g.dma_start(wsl, src),
                 waits=[("T", wslot_cons[slot])], incs=[("W", 16)])
            wv = b.val("W")
            bank = pb[oc % 2]
            for kc in range(NCH):
                b.mm(bank[:, 0:512],
                     apv(wr, 0, 128, slot * 4096 + kc * 128, [[1, 128]]),
                     aap("aout", 0, 128, kc * 512, [[1, 512]]),
                     start=(kc == 0), stop=(kc == NCH - 1),
                     waits=[("W", wv), ("V", v_aout)] +
                           (pbank_free[oc % 2] if kc == 0 else []),
                     incs=([("T", 1)] if kc == NCH - 1 else []))
            tv = b.val("T")
            b.op("vector", lambda v, oc=oc, bank=bank, s0=s0: v.tensor_tensor(
                u[:, oc, s0:s0 + CA], u[:, oc, s0:s0 + CA], bank[:, 0:512], Op.add),
                waits=[("T", tv), ("W", st["u_w"])], incs=[("V", 1)])
            pbank_free[oc % 2] = [("V", b.val("V"))]
            wslot_cons[slot] = b.val("T")
        st["aout_free_t"] = b.val("T")
        st["xs_free_t"] = b.val("T")
        st["attn_prev_t"] = b.val("T")
    st["pbank_free"] = pbank_free
    st["attn_ring_s"] = ring_s
    st["attn_ring_v"] = ring_v
    st["pn_cons"] = pn_cons
    st["pt_cons"] = pt_cons
    st["sbank_cons"] = sbank_cons
    st["attn_sidx"] = sidx
    st["wring_free_t"] = b.val("T")
    return False


def emit_ffn_sublayer(b, E, st, li):
    pb = E["pb"]
    u = E["u"]
    xs = E["xs"]
    wr = E["wring"]
    emit_boundary_snapshot(b, E, st)
    pbank_free = st.get("pbank_free", [[] for _ in range(8)])
    w1slot_cons = [st["wring_free_t"]] * 4
    w2slot_cons = [st["wring_free_t"]] * 2
    gact_cons = st.get("gact_cons", [0, 0])

    for ch in range(2):
        c0 = ch * CF
        xn_v = []
        for tsl in range(2):
            s0 = c0 + tsl * 512
            vd = emit_ln_slice(b, E, st, s0, shifted=True, gb=True, out_mode="xs")
            xn_v.append(vd)
        v_xn = max(xn_v)
        if ch == 1 and TAP == "dxn" and li == 0:
            dbg_dump(b, E, [(xs[:, c, 0:CF], c * 128, 128, 1024) for c in range(NCH)])
            st["tapped"] = True
            return

        p_t = {}
        for ic in range(NIC):
            for s in range(2):
                slot = s * 2 + ic % 2
                wsl = apv(wr, 0, 128, slot * 1024, [[128, 8], [1, 128]])
                src = E["dw1"][li, s, ic].rearrange("kc k m -> k kc m")
                b.op("gpsimd", lambda g, wsl=wsl, src=src: g.dma_start(wsl, src),
                     waits=[("T", w1slot_cons[slot])], incs=[("W", 16)])
            wv = b.val("W")
            for tsl in range(2):
                bkA = pb[tsl * 2]
                bkG = pb[tsl * 2 + 1]
                for s in range(2):
                    bank = bkA if s == 0 else bkG
                    slot = s * 2 + ic % 2
                    for kc in range(NCH):
                        b.mm(bank[:, 0:512],
                             apv(wr, 0, 128, slot * 1024 + kc * 128, [[1, 128]]),
                             xs[:, kc, tsl * 512:(tsl + 1) * 512],
                             start=(kc == 0), stop=(kc == NCH - 1),
                             waits=[("W", wv), ("V", v_xn)] +
                                   (pbank_free[tsl * 2 + s] if kc == 0 else []),
                             incs=([("T", 1)] if kc == NCH - 1 else []))
                t_ag = b.val("T")
                if tsl == 1:
                    w1slot_cons[ic % 2] = t_ag
                    w1slot_cons[2 + ic % 2] = t_ag
                gsl = (ic * 2 + tsl) % 2
                gact = aap("gact", 0, 128, gsl * 512, [[1, 512]])
                b1g = apv(E["b1sb"], 0, 128, NIC + ic, [[1, 1]])
                b1a = apv(E["b1sb"], 0, 128, ic, [[1, 1]])
                b.op("scalar", lambda sc, gact=gact, bkG=bkG, b1g=b1g: sc.activation(
                    gact, bkG[:, 0:512], AF.Gelu, bias=b1g, scale=1.0),
                    waits=[("T", t_ag), ("S", gact_cons[gsl]),
                           ("W", st.get("w_params", 0)),
                           ("K", st.get("pn_cons", [0, 0])[gsl])], incs=[("S", 1)])
                s_g = b.val("S")
                gact_cons[gsl] = s_g
                pdst = aap("p", 0, 128, ic * 1024 + tsl * 512, [[1, 512]])
                b.op("vector", lambda v, pdst=pdst, bkA=bkA, b1a=b1a, gact=gact:
                     v.scalar_tensor_tensor(pdst, bkA[:, 0:512], b1a, gact,
                                            Op.add, Op.mult),
                     waits=[("S", s_g), ("T", st.get("p_free_t", 0)),
                            ("K", st.get("qtok_free_k", 0))], incs=[("V", 1)])
                pv = b.val("V")
                p_t[(ic, tsl)] = pv
                pbank_free[tsl * 2] = [("V", pv)]
                pbank_free[tsl * 2 + 1] = [("S", s_g)]
        st["gact_cons"] = gact_cons

        for tsl in range(2):
            v_pall = max(p_t[(ic, tsl)] for ic in range(NIC))
            evw = []
            for ic in range(NIC):
                slot = ic % 2
                wsl = apv(wr, 0, 128, 4096 + slot * 1024, [[1, 1024]])
                src = E["dw2"][li, ic]
                b.op("gpsimd", lambda g, wsl=wsl, src=src: g.dma_start(wsl, src),
                     waits=[("T", w2slot_cons[slot])], incs=[("W", 16)])
                wv = b.val("W")
                for oc in range(NCH):
                    ww = [("W", wv), ("V", p_t[(ic, tsl)])]
                    if ic == 0:
                        ww += [("V", v_pall)] + pbank_free[oc]
                    b.mm(pb[oc][:, 0:512],
                         apv(wr, 0, 128, 4096 + slot * 1024 + oc * 128, [[1, 128]]),
                         aap("p", 0, 128, ic * 1024 + tsl * 512, [[1, 512]]),
                         start=(ic == 0), stop=(ic == NIC - 1),
                         waits=ww,
                         incs=([("T", 1)] if oc == NCH - 1 else []))
                w2slot_cons[slot] = b.val("T")
            t_w2 = b.val("T")
            for oc in range(NCH):
                b2c = apv(E["b2sb"], 0, 128, oc, [[1, 1]])
                dst = u[:, oc, c0 + tsl * 512:c0 + (tsl + 1) * 512]
                b.op("vector", lambda v, dst=dst, oc=oc, b2c=b2c: v.scalar_tensor_tensor(
                    dst, pb[oc][:, 0:512], b2c, dst, Op.add, Op.add),
                    waits=[("T", t_w2), ("W", st["u_w"])], incs=[("V", 1)])
                pbank_free[oc] = [("V", b.val("V"))]
        st["p_free_t"] = b.val("T")
        st["xs_free_t"] = b.val("T")
    st["pbank_free"] = pbank_free
    st["wring_free_t"] = b.val("T")


def emit_deltanet(b, E, st):
    pb = E["pb"]
    old_spacer = b.spacer
    b.spacer = (pb[2][0:1, 400:401], E["onec"][:], E["onec"][:])
    u = E["u"]
    xs = E["xs"]
    wr = E["wring"]
    wslot_cons = [st["wring_free_t"]] * 2
    pbank_free = st.get("pbank_free", [[] for _ in range(8)])
    abuf_cons = [0, 0, 0, 0]
    dn_sq = [0, 0]
    st["xs_free_t"] = st.get("xs_free_t", 0)

    for ch in range(4):
        s0 = ch * 512
        x0 = (ch % 2) * 512
        v_xo = emit_ln_slice(b, E, st, s0, shifted=False, gb=None, out_mode="xs")
        obank_cons = [[("V", v_xo)], [("V", v_xo)]]

        evac_v = {}
        for pi, (dw, reg) in enumerate((("dowq", "qtok"), ("dowk", "ktok"),
                                        ("dowv", "vtok"))):
            for cg in range(2):
                slot = (pi * 2 + cg) % 2
                wsl = apv(wr, 0, 128, slot * 4096, [[512, 8], [1, 512]])
                src = E[dw].rearrange("kc k n -> k kc n")[:, :, cg * 512:(cg + 1) * 512]
                b.op("gpsimd", lambda g, wsl=wsl, src=src: g.dma_start(wsl, src),
                     waits=[("T", wslot_cons[slot])], incs=[("W", 16)])
                wv = b.val("W")
                for tt in range(4):
                    bank = pb[tt % 2]
                    for kc in range(NCH):
                        b.mm(bank[:, 0:512],
                             xs[:, kc, x0 + tt * 128:x0 + (tt + 1) * 128],
                             apv(wr, 0, 128, slot * 4096 + kc * 512, [[1, 512]]),
                             start=(kc == 0), stop=(kc == NCH - 1),
                             waits=[("W", wv), ("V", v_xo)] +
                                   (pbank_free[tt % 2] if kc == 0 else []),
                             incs=([("T", 1)] if kc == NCH - 1 else []))
                    tv = b.val("T")
                    dst = aap(reg, 0, 128, tt * 1024 + cg * 512, [[1, 512]])
                    b.op("vector", lambda v, dst=dst, bank=bank: v.tensor_copy(
                        dst, bank[:, 0:512]),
                        waits=[("T", tv), ("K", st.get("dn_fm_k", 0)),
                               ("T", st.get("dn_scan_t", 0))], incs=[("V", 1)])
                    pbank_free[tt % 2] = [("V", b.val("V"))]
                    evac_v[(pi, cg, tt)] = b.val("V")
                wslot_cons[slot] = b.val("T")
        wslb = apv(E["qsb"], 0, 128, 0, [[16, 8], [1, 16]])
        b.op("gpsimd", lambda g, wslb=wslb: g.dma_start(
            wslb, E["dowb"].rearrange("kc k n -> k kc n")),
            waits=[("T", st["wring_free_t"]), ("V", st.get("dn_qsb_v", 0))],
            incs=[("W", 16)])
        wvb = b.val("W")
        s_betas = {}
        for tt in range(4):
            bank = pb[2]
            for kc in range(NCH):
                b.mm(bank[:, 0:16], xs[:, kc, x0 + tt * 128:x0 + (tt + 1) * 128],
                     apv(E["qsb"], 0, 128, kc * 16, [[1, 16]]),
                     start=(kc == 0), stop=False,
                     waits=[("W", wvb), ("V", v_xo),
                            ("S", s_betas.get(tt - 1, 0))] +
                           (st.get("betabank_free", []) if kc == 0 else []))
            b.mm(bank[:, 0:16], E["onesrow"][:], E["bbrow"][:], start=False, stop=True,
                 ksize=1, waits=[("W", st["winit"])], incs=[("T", 1)])
            tv = b.val("T")
            bcol = apv(E["colb"], 0, 128, 64 + tt * 16, [[1, 16]])
            b.op("scalar", lambda sc, bcol=bcol, bank=bank: sc.activation(
                bcol, bank[:, 0:16], AF.Sigmoid, bias=0.0, scale=1.0),
                waits=[("T", tv)], incs=[("S", 1)])
            s_betas[tt] = b.val("S")
        st["betabank_free"] = [("S", b.val("S"))]

        qk_v = {}
        for which, reg, rn0, pi in (("q", "qtok", 0, 0), ("k", "ktok", 128, 1)):
            for tt in range(4):
                ready = [("V", max(evac_v[(pi, 0, tt)], evac_v[(pi, 1, tt)]))]
                slot = tt % 2
                sq = aap("sqs", 0, 128, slot * 1024, [[1, 1024]])
                tile = aap(reg, 0, 128, tt * 1024, [[1, 1024]])
                b.op("vector", lambda v, sq=sq, tile=tile: v.tensor_tensor(
                    sq, tile, tile, Op.mult),
                    waits=ready + [("V", dn_sq[slot])], incs=[("V", 1)])
                nrm = apv(E["colb"], 0, 128, rn0 + tt * 16, [[1, 16]])
                sq3 = aap("sqs", 0, 128, slot * 1024, [[64, 16], [1, 64]])
                b.op("vector", lambda v, nrm=nrm, sq3=sq3: v.reduce_sum(nrm, sq3, axis=AX),
                     incs=[("V", 1)])
                v_n = b.val("V")
                b.op("scalar", lambda sc, nrm=nrm: sc.activation(
                    nrm, nrm, AF.Ln, bias=E["eps18"][:, 0:1], scale=1.0),
                    waits=[("V", v_n), ("G", st["ginit"])], incs=[("S", 1)])
                b.op("scalar", lambda sc, nrm=nrm: sc.activation(
                    nrm, nrm, AF.Exp, bias=0.0, scale=-0.5), incs=[("S", 1)])
                s_rn = b.val("S")
                if which == "q":
                    rnb = apv(E["colb"], 0, 128, tt * 16, [[1, 16], [0, 64]])
                    b.op("vector", lambda v, tile=tile, rnb=rnb: v.tensor_tensor(
                        tile, tile, rnb, Op.mult),
                        waits=[("S", s_rn)], incs=[("V", 1)])
                dn_sq[slot] = b.val("V")
                qk_v[(which, tt)] = (b.val("V"), s_rn)
        v_brn = {}
        for tt in range(4):
            bc = apv(E["colb"], 0, 128, 64 + tt * 16, [[1, 16]])
            rc = apv(E["colb"], 0, 128, 128 + tt * 16, [[1, 16]])
            dc = apv(E["colb"], 0, 128, 192 + tt * 16, [[1, 16]])
            b.op("vector", lambda v, dc=dc, bc=bc, rc=rc: v.tensor_tensor(
                dc, bc, rc, Op.mult),
                waits=[("S", s_betas[tt]), ("S", qk_v[("k", tt)][1])], incs=[("V", 1)])
            v_brn[tt] = b.val("V")

        qxp = {}
        kxp = {}
        for which, reg, dreg, stp in (("q", "qtok", "qfm", 512), ("k", "ktok", "kfm", 768)):
            for tt in range(4):
                src = aap(reg, 0, 128, tt * 1024, [[1, 1024]])
                dst = aap(dreg, 0, 128, tt * 128, [[stp, 8], [1, 128]])
                b.op("sync", lambda s, dst=dst, src=src: s.dma_start(
                    dst, src, transpose=True),
                    waits=[("V", qk_v[(which, tt)][0]), ("T", st.get("dn_scan_t", 0))],
                    incs=[("K", 16)])
                (qxp if which == "q" else kxp)[tt] = b.val("K")
        kb_v = {}
        for tt in range(4):
            tile = aap("ktok", 0, 128, tt * 1024, [[1, 1024]])
            brnb = apv(E["colb"], 0, 128, 192 + tt * 16, [[1, 16], [0, 64]])
            b.op("vector", lambda v, tile=tile, brnb=brnb: v.tensor_tensor(
                tile, tile, brnb, Op.mult),
                waits=[("K", kxp[tt]), ("V", v_brn[tt])], incs=[("V", 1)])
            kb_v[tt] = b.val("V")

        dn_sb = st.get("dn_sb", [[], []])
        dfree = st.get("dn_dfree", [0, 0])
        for blk_i in range(4):
            dbank = pb[5 + (blk_i % 2)]
            for p in range(8):
                obank = pb[4] if p % 2 == 0 else pb[7]
                for hh in range(2):
                    h = 2 * p + hh
                    sbank = pb[2] if hh == 0 else pb[3]
                    lhs = aap("kfm", hh * 64, 64, (h // 2) * 768 + blk_i * 128, [[1, 128]])
                    rhs = aap("qfm", hh * 64, 64, (h // 2) * 512 + blk_i * 128, [[1, 128]])
                    b.mm(sbank[:, 0:128], lhs, rhs, start=True, stop=True,
                         kbase=hh * 64, ksize=64,
                         waits=[("K", kxp[blk_i]), ("K", qxp[blk_i]),
                                ("S", s_betas[3])] + dn_sb[hh],
                         incs=[("T", 1)])
                    t_sa = b.val("T")
                    asl = (p * 2 + hh) % 4
                    A = aap("abuf", 0, 128, asl * 512, [[1, 128]])
                    brncol = apv(E["colb"], 0, 128, 192 + blk_i * 16 + h, [[1, 1]])
                    b.op("vector", lambda v, A=A, sbank=sbank, brncol=brncol:
                         v.scalar_tensor_tensor(A, sbank[:, 0:128], brncol, E["utc"][:],
                                                Op.mult, Op.mult),
                         waits=[("T", t_sa), ("V", v_brn[blk_i]),
                                ("T", abuf_cons[asl]), ("W", st["winit"])],
                         incs=[("V", 1)])
                    v_a = b.val("V")
                    dn_sb[hh] = [("V", v_a)]
                    vsl = aap("vtok", 0, 128, blk_i * 1024 + h * 64, [[1, 64]])
                    ow = [("V", v_a), ("V", evac_v[(2, 0, blk_i)]),
                          ("V", evac_v[(2, 1, blk_i)])]
                    if hh == 0:
                        ow += obank_cons[p % 2]
                    b.mm(obank[hh * 64:hh * 64 + 64, 0:128], vsl, A,
                         start=True, stop=False, tp=(0, hh * 64), waits=ow, sgc=True)
                    stsl = apv(E["stb"], hh * 64, 64, p * 64, [[1, 64]])
                    b.mm(obank[hh * 64:hh * 64 + 64, 0:128], stsl,
                         aap("qfm", hh * 64, 64, (h // 2) * 512 + blk_i * 128, [[1, 128]]),
                         start=False, stop=True, kbase=hh * 64, ksize=64,
                         tp=(hh * 64, hh * 64),
                         waits=[("V", st.get("dn_stb_v", 0)), ("G", st["ginit"])],
                         incs=[("T", 1)], sgc=True)
                    abuf_cons[asl] = b.val("T")
                    b.mm(dbank[hh * 64:hh * 64 + 64, p * 64:(p + 1) * 64],
                         aap("ktok", 0, 128, blk_i * 1024 + h * 64, [[1, 64]]),
                         vsl, start=True, stop=True, tp=(0, hh * 64),
                         waits=[("V", kb_v[blk_i]), ("V", dfree[blk_i % 2])],
                         incs=[("T", 1)], sgc=True)
                t_pair = b.val("T")
                adst = aap("aout", 0, 128, p * 512 + blk_i * 128, [[1, 128]])
                b.op("vector", lambda v, adst=adst, obank=obank: v.tensor_copy(
                    adst, obank[:, 0:128]),
                    waits=[("T", t_pair), ("K", max(qxp.values()))], incs=[("V", 1)])
                obank_cons[p % 2] = [("V", b.val("V"))]
            t_deltas = b.val("T")
            for p in range(8):
                for hh in range(2):
                    dsl = dbank[hh * 64:hh * 64 + 64, p * 64:(p + 1) * 64]
                    fsl = apv(E["stf"], hh * 64, 64, p * 64, [[1, 64]])
                    b.op("vector", lambda v, fsl=fsl, dsl=dsl: v.tensor_tensor(
                        fsl, fsl, dsl, Op.add),
                        waits=[("T", t_deltas), ("G", st["ginit"])], incs=[("V", 1)])
            for p in range(8):
                fsl = apv(E["stf"], 0, 128, p * 64, [[1, 64]])
                bsl = apv(E["stb"], 0, 128, p * 64, [[1, 64]])
                b.op("vector", lambda v, fsl=fsl, bsl=bsl: v.tensor_copy(bsl, fsl),
                     incs=[("V", 1)])
            st["dn_stb_v"] = b.val("V")
            dfree[blk_i % 2] = b.val("V")
        st["dn_sb"] = dn_sb
        st["dn_dfree"] = dfree
        st["dn_scan_t"] = b.val("T")
        v_aout = b.val("V")

        for oc in range(NCH):
            slot = oc % 2
            wsl = apv(wr, 0, 128, slot * 4096, [[128, 8], [1, 128]])
            src = E["dowo"].rearrange("kc k n -> k kc n")[:, :, oc * 128:(oc + 1) * 128]
            b.op("gpsimd", lambda g, wsl=wsl, src=src: g.dma_start(wsl, src),
                 waits=[("T", wslot_cons[slot])], incs=[("W", 16)])
            wv = b.val("W")
            bank = pb[oc % 2]
            for kc in range(NCH):
                b.mm(bank[:, 0:512],
                     apv(wr, 0, 128, slot * 4096 + kc * 128, [[1, 128]]),
                     aap("aout", 0, 128, kc * 512, [[1, 512]]),
                     start=(kc == 0), stop=(kc == NCH - 1),
                     waits=[("W", wv), ("V", v_aout)] +
                           (pbank_free[oc % 2] if kc == 0 else []),
                     incs=([("T", 1)] if kc == NCH - 1 else []))
            tv = b.val("T")
            b.op("vector", lambda v, oc=oc, bank=bank, s0=s0: v.tensor_tensor(
                u[:, oc, s0:s0 + 512], u[:, oc, s0:s0 + 512], bank[:, 0:512], Op.add),
                waits=[("T", tv), ("W", st["u_w"])], incs=[("V", 1)])
            pbank_free[oc % 2] = [("V", b.val("V"))]
            wslot_cons[slot] = b.val("T")
        st["xs_free_t"] = b.val("T")
        st["dn_fm_k"] = b.val("K")
        if ch == 0 and TAP == "ddn2":
            dbg_dump(b, E, [(xs[:, c, 0:512], c * 128, 128, 512) for c in range(NCH)])
            st["tapped"] = True
            return
        if ch == 0 and TAP == "ddn":
            dbg_dump(b, E, [(aap("aout", 0, 128, 0, [[1, 2048]]), 0, 128, 2048),
                            (aap("aout", 0, 128, 2048, [[1, 2048]]), 128, 128, 2048),
                            (apv(E["stf"], 0, 128, 0, [[1, 512]]), 256, 128, 512),
                            (apv(E["colb"], 0, 128, 0, [[1, 256]]), 384, 128, 256),
                            (apv(E["qsb"], 0, 128, 0, [[1, 128]]), 512, 128, 128),
                            (xs[:, 0, 0:512], 640, 128, 512),
                            (xs[:, 7, 0:512], 768, 128, 512)])
            st["tapped"] = True
            return
    st["pbank_free"] = pbank_free
    st["wring_free_t"] = b.val("T")
    b.spacer = old_spacer



def _stage_weights(inputs):
    w = {}
    w["wqkv"] = np.ascontiguousarray(
        np.asarray(inputs["L0_ffn_Wqkv"], np.float32).reshape(DEPTH, NCH, 128, 3 * D))
    w["wo"] = np.ascontiguousarray(
        np.asarray(inputs["L0_ffn_Wo"], np.float32).reshape(DEPTH, NCH, 128, D))
    W1 = np.asarray(inputs["L0_ffn_W1"], np.float32)
    W1p = np.zeros((DEPTH, 2, NIC * 128, D), np.float32)
    W1p[:, 0, :2730] = W1[:, :, :2730].transpose(0, 2, 1)
    W1p[:, 1, :2730] = W1[:, :, 2730:].transpose(0, 2, 1)
    W1p = W1p.reshape(DEPTH, 2, NIC, 128, NCH, 128).transpose(0, 1, 2, 4, 5, 3)
    w["w1"] = np.ascontiguousarray(W1p)
    W2 = np.asarray(inputs["L0_ffn_W2"], np.float32)
    W2p = np.zeros((DEPTH, NIC * 128, D), np.float32)
    W2p[:, :2730] = W2
    w["w2"] = np.ascontiguousarray(W2p.reshape(DEPTH, NIC, 128, D))
    qsr = np.asarray(inputs["L0_ffn_q_scale"], np.float32)
    ksr = np.asarray(inputs["L0_ffn_k_scale"], np.float32)
    w["qsr"] = np.ascontiguousarray(
        np.concatenate([qsr, qsr[:, 32:], qsr[:, :32]], axis=1))
    w["ksr"] = np.ascontiguousarray(
        np.concatenate([ksr, ksr[:, 32:], ksr[:, :32]], axis=1))
    w["gb"] = np.ascontiguousarray(
        np.stack([inputs["L0_ffn_ln_g"], inputs["L0_ffn_ln_b"]], axis=1).astype(np.float32))
    b1 = np.asarray(inputs["L0_ffn_b1"], np.float32)
    b1p = np.zeros((DEPTH, 2, NIC * 128), np.float32)
    b1p[:, 0, :2730] = b1[:, :2730]
    b1p[:, 1, :2730] = b1[:, 2730:]
    w["b1p"] = np.ascontiguousarray(
        b1p.reshape(DEPTH, 2, NIC, 128).transpose(0, 3, 1, 2))
    w["b2c"] = np.ascontiguousarray(
        np.asarray(inputs["L0_ffn_b2"], np.float32).reshape(DEPTH, NCH, 128)
        .transpose(0, 2, 1))
    for nm, key in (("owq", "L0_attn_Wq"), ("owk", "L0_attn_Wk"),
                    ("owv", "L0_attn_Wv"), ("owo", "L0_attn_Wo")):
        w[nm] = np.ascontiguousarray(
            np.asarray(inputs[key], np.float32).reshape(NCH, 128, D))
    w["owb"] = np.ascontiguousarray(
        np.asarray(inputs["L0_attn_Wb"], np.float32).reshape(NCH, 128, H))
    w["obb"] = np.asarray(inputs["L0_attn_bb"], np.float32).reshape(1, H)
    tabs, LT, maskR, UT = host_tables()
    w["tabs"] = tabs
    w["mlt"] = LT
    w["maskR"] = maskR
    w["mut"] = UT
    return w


def kernel(**inputs):
    x = np.asarray(inputs["L0_attn_x"], np.float32)
    ncores = 8
    w = _stage_weights(inputs)
    nc = build_program()
    in_maps = []
    for c in range(ncores):
        m = dict(w)
        m["x"] = np.ascontiguousarray(x[c].T)
        in_maps.append(m)
    res = run_bass_kernel_spmd(nc, in_maps, core_ids=list(range(ncores)))
    out = np.stack([np.asarray(r["o"]).T for r in res.results], axis=0)
    return out.astype(np.float32)


if __name__ == "__main__":
    build_program(verbose=True)
    print("built ok")


# version 07 (final)
# speedup vs baseline: 1.0898x; 1.0390x over previous
import numpy as np
import concourse.bass as bass
import concourse.mybir as mybir
from concourse.bass_utils import run_bass_kernel_spmd
from concourse.alu_op_type import AluOpType as Op

F32 = mybir.dt.float32
BF16 = mybir.dt.bfloat16
AF = mybir.ActivationFunctionType
AX = mybir.AxisListType.X

T = 2048
D = 1024
H = 16
DH = 64
NCH = 8
CA = 512
CF = 1024
DEPTH = 6
NIC = 22
WIN = 256

TAP = None
LAYERS = DEPTH
DO_OUTER = True
SECOND = True

ENGS = ("tensor", "vector", "scalar", "sync", "gpsimd")
ARN = 36864


class Bld:
    def __init__(self, nc):
        self.nc = nc
        self.q = {e: [] for e in ENGS}
        self.cnt = {}
        self.sems = {}
        self.lastwait = {}
        self.prev_mm_row = None
        self.spacer = None
        self.nspacer = 0
        self.ninstr = 0

    def sem(self, name, h):
        self.sems[name] = h
        self.cnt[name] = 0

    def bump(self, name, k=1):
        self.cnt[name] += k
        return self.cnt[name]

    def val(self, name):
        return self.cnt[name]

    def op(self, eng, f, waits=(), incs=()):
        wl = []
        for s, v in waits:
            if v <= 0:
                continue
            key = (eng, s)
            if self.lastwait.get(key, -1) >= v:
                continue
            self.lastwait[key] = v
            wl.append((s, v))
        for s, k in incs:
            self.bump(s, k)
        sems = self.sems
        self.ninstr += 1 + len(wl)

        def body(e, f=f, wl=tuple(wl), incs=tuple(incs)):
            for s, v in wl:
                e.wait_ge(sems[s], v)
            inst = f(e)
            for s, k in incs:
                inst.then_inc(sems[s], k)
        self.q[eng].append(body)

    def mm(self, out, lhsT, rhs, start, stop, kbase=0, ksize=128,
           waits=(), incs=(), tp=None, sgc=False):
        rows = 128 if ksize > 64 else (64 if ksize > 32 else 32)
        cur = (rows, kbase) if rows < 128 else None
        if cur is not None and self.prev_mm_row is not None and self.prev_mm_row != cur:
            so, sl, sr = self.spacer
            self.nspacer += 1
            import traceback
            fr = traceback.extract_stack()[-3]
            print("SPACER", self.prev_mm_row, "->", cur, "at", fr.lineno)
            self.ninstr += 1
            self.q["tensor"].append(
                lambda t, so=so, sl=sl, sr=sr: t.matmul(
                    so, sl, sr, start=False, stop=True, skip_group_check=True))
        self.prev_mm_row = cur
        self.op("tensor",
                lambda t, out=out, lhsT=lhsT, rhs=rhs, start=start, stop=stop, tp=tp,
                sgc=sgc: t.matmul(out, lhsT, rhs, start=start, stop=stop,
                                  tile_position=tp, skip_group_check=sgc),
                waits=waits, incs=incs)

    def finalize(self, blk):
        for e in ENGS:
            ops = self.q[e]
            if not ops:
                continue

            def body(eng, ops=ops):
                for f in ops:
                    f(eng)
            getattr(blk, e)(body)


def _arena_offsets():
    AO = {}
    off = 0
    for name, n in (("qtok", 4096), ("ktok", 4096), ("vtok", 6144), ("qfm", 4096),
                    ("kfm", 6144), ("pnbuf", 4096), ("ptbuf", 2048),
                    ("sqs", 2048), ("rott", 2048), ("abuf", 2048)):
        AO[name] = off
        off += n
    assert off <= ARN, off
    AO["p"] = AO["qtok"]
    AO["gact"] = AO["pnbuf"]
    AO["aout"] = AO["qtok"]
    return AO


AO = _arena_offsets()


def host_tables():
    t = np.arange(T, dtype=np.float64)
    j = np.arange(32, dtype=np.float64)
    inv_freq = 1.0 / (10000.0 ** (2.0 * j / DH))
    freqs = t[:, None] * inv_freq[None, :]
    base_scale = (2.0 * j + 0.4 * DH) / (1.4 * DH)
    power = (t - T // 2) / (WIN // 2)
    sc = base_scale[None, :] ** power[:, None]
    cos = np.cos(freqs)
    sin = np.sin(freqs)

    def tok(arr):
        return np.ascontiguousarray(
            arr.reshape(16, 128, 64).transpose(1, 0, 2)).astype(np.float32)

    cq = np.concatenate([cos * sc, cos * sc], axis=1)
    sq = np.concatenate([-sin * sc, sin * sc], axis=1)
    inv = 1.0 / sc
    ck = np.concatenate([cos * inv, cos * inv], axis=1)
    sk = np.concatenate([-sin * inv, sin * inv], axis=1)
    tabs = np.stack([tok(x) for x in (cq, sq, ck, sk)])

    LT = (np.arange(128)[:, None] <= np.arange(128)[None, :]).astype(np.float32)
    BIG = 1e9
    Rs = []
    for w0 in (False, True):
        for isub in (0, 1):
            i = 128 * isub + np.arange(128)[:, None]
            jj = np.arange(512)[None, :]
            d = i + 256 - jj
            valid = (d >= 0) & (d <= 256)
            if w0:
                valid &= (jj >= 256)
            M = np.where(valid, 0.0, -BIG).astype(np.float32)
            R = np.empty_like(M)
            R[0] = M[0]
            R[1:] = M[1:] - M[:-1]
            Rs.append(R)
    maskR = np.stack(Rs)
    UT = (np.arange(128)[:, None] <= np.arange(128)[None, :]).astype(np.float32)
    return tabs, LT, maskR, UT


def apv(tensor, pbase, npart, freeoff, dims):
    epp = 1
    for s in tensor.shape[1:]:
        epp *= s
    return bass.AP(tensor, pbase * epp + freeoff, [[epp, npart]] + [list(d) for d in dims])


_AR = [None]


def aap(name, pbase, npart, extra, dims):
    return apv(_AR[0], pbase, npart, AO[name] + extra, dims)


def build_program(verbose=False):
    nc = bass.Bass("TRN2", target_bir_lowering=False)
    E = {"nc": nc}
    E["dx"] = nc.dram_tensor("x", [D, T], F32, kind="ExternalInput").ap()
    E["dwqkv"] = nc.dram_tensor("wqkv", [DEPTH, NCH, 128, 3 * D], F32, kind="ExternalInput").ap()
    E["dwo"] = nc.dram_tensor("wo", [DEPTH, NCH, 128, D], F32, kind="ExternalInput").ap()
    E["dw1"] = nc.dram_tensor("w1", [DEPTH, 2, NIC, NCH, 128, 128], F32, kind="ExternalInput").ap()
    E["dw2"] = nc.dram_tensor("w2", [DEPTH, NIC, 128, D], F32, kind="ExternalInput").ap()
    E["dqs"] = nc.dram_tensor("qsr", [DEPTH, 128], F32, kind="ExternalInput").ap()
    E["dks"] = nc.dram_tensor("ksr", [DEPTH, 128], F32, kind="ExternalInput").ap()
    E["dgb"] = nc.dram_tensor("gb", [DEPTH, 2, D], F32, kind="ExternalInput").ap()
    E["db1"] = nc.dram_tensor("b1p", [DEPTH, 128, 2, NIC], F32, kind="ExternalInput").ap()
    E["db2"] = nc.dram_tensor("b2c", [DEPTH, 128, NCH], F32, kind="ExternalInput").ap()
    E["dowq"] = nc.dram_tensor("owq", [NCH, 128, D], F32, kind="ExternalInput").ap()
    E["dowk"] = nc.dram_tensor("owk", [NCH, 128, D], F32, kind="ExternalInput").ap()
    E["dowv"] = nc.dram_tensor("owv", [NCH, 128, D], F32, kind="ExternalInput").ap()
    E["dowo"] = nc.dram_tensor("owo", [NCH, 128, D], F32, kind="ExternalInput").ap()
    E["dowb"] = nc.dram_tensor("owb", [NCH, 128, H], F32, kind="ExternalInput").ap()
    E["dobb"] = nc.dram_tensor("obb", [1, H], F32, kind="ExternalInput").ap()
    E["dtabs"] = nc.dram_tensor("tabs", [4, 128, 16, 64], F32, kind="ExternalInput").ap()
    E["dlt"] = nc.dram_tensor("mlt", [128, 128], F32, kind="ExternalInput").ap()
    E["dmr"] = nc.dram_tensor("maskR", [4, 128, 512], F32, kind="ExternalInput").ap()
    E["dut"] = nc.dram_tensor("mut", [128, 128], F32, kind="ExternalInput").ap()
    E["do"] = nc.dram_tensor("o", [D, T], F32, kind="ExternalOutput").ap()
    E["dpark"] = nc.dram_tensor("xpark", [NCH, 128, T], F32).ap()

    ctx = []

    def sb(name, shape, dt):
        h = nc.sbuf_tensor(name, shape, dt)
        ctx.append(h)
        E[name] = h.__enter__()

    sb("u", [128, NCH, T], F32)
    sb("xs", [128, NCH, CF], BF16)
    sb("arena", [128, ARN], BF16)
    sb("tabf", [128, 4, 1024], BF16)
    sb("wring", [128, 8192], BF16)
    sb("rowf", [128, 2048], F32)
    sb("rowbf", [128, 2048], BF16)
    sb("qsb", [128, 256], BF16)
    sb("onesN", [1, 512], BF16)
    sb("onesrow", [1, 128], BF16)
    sb("tmpf", [128, 2, 512], F32)
    sb("colb", [128, 256], F32)
    sb("onec", [128, 1], BF16)
    sb("zeroc", [128, 1], BF16)
    sb("epsc", [128, 1], F32)
    sb("bndc", [128, 16], F32)
    sb("eps18", [128, 1], F32)
    sb("ltc", [128, 128], BF16)
    sb("mrc", [128, 4, 512], BF16)
    sb("utc", [128, 128], BF16)
    sb("gbsb", [1, 2 * D], BF16)
    sb("qsrow", [1, 384], BF16)
    sb("b1sb", [128, 2 * NIC], F32)
    sb("b2sb", [128, NCH], F32)
    sb("stf", [128, 8, 64], F32)
    sb("stb", [128, 8, 64], BF16)
    sb("bbrow", [1, H], BF16)
    _AR[0] = E["arena"]

    pbh = [nc.psum_tensor(f"pb{i}", [128, 512], F32) for i in range(8)]
    for h in pbh:
        ctx.append(h)
    E["pb"] = [h.__enter__() for h in pbh]

    b = Bld(nc)
    with nc.Block() as blk:
        for s in ("W", "K", "O", "T", "V", "S", "G"):
            h = nc.semaphore(s)
            ctx.append(h)
            b.sem(s, h.__enter__())
        b.spacer = (E["pb"][6][0:1, 508:509], E["onec"][:], E["onec"][:])
        emit_all(b, E)
        b.finalize(blk)

    if verbose:
        print(f"instr~{b.ninstr} spacers={b.nspacer} " +
              " ".join(f"{k}={v}" for k, v in b.cnt.items()))
    assert b.nspacer <= 8, f"unexpected spacers: {b.nspacer}"
    for h in reversed(ctx):
        h.__exit__(None, None, None)
    return nc



def emit_all(b, E):
    pb = E["pb"]
    u = E["u"]
    for c in range(NCH):
        b.op("gpsimd", lambda g, c=c: g.dma_start(u[:, c, :], E["dx"][c * 128:(c + 1) * 128, :]),
             incs=[("W", 16)])
    b.op("gpsimd", lambda g: g.dma_start(E["ltc"][:], E["dlt"]), incs=[("W", 16)])
    b.op("gpsimd", lambda g: g.dma_start(
        E["mrc"][:], E["dmr"].rearrange("a p n -> p a n")), incs=[("W", 16)])
    b.op("gpsimd", lambda g: g.dma_start(E["utc"][:], E["dut"]), incs=[("W", 16)])
    b.op("gpsimd", lambda g: g.dma_start(E["bbrow"][:], E["dobb"]), incs=[("W", 16)])
    b.op("gpsimd", lambda g: g.memset(E["onec"][:], 1.0))
    b.op("gpsimd", lambda g: g.memset(E["zeroc"][:], 0.0))
    b.op("gpsimd", lambda g: g.memset(E["epsc"][:], 1e-5))
    b.op("gpsimd", lambda g: g.memset(E["eps18"][:], 1e-18))
    b.op("gpsimd", lambda g: g.memset(E["onesrow"][:], 1.0))
    b.op("gpsimd", lambda g: g.memset(E["onesN"][:], 1.0))
    b.op("gpsimd", lambda g: g.memset(E["stf"][:], 0.0))
    b.op("gpsimd", lambda g: g.memset(E["stb"][:], 0.0), incs=[("G", 1)])
    st = {"winit": b.val("W"), "ginit": b.val("G"), "wring_free_t": 0, "xs_free_t": 0}
    st["u_w"] = st["winit"]

    park_and_ln(b, E, st)
    if TAP == "ln1":
        emit_output(b, E, st)
        return
    if emit_ffn6(b, E, st):
        if TAP in ("l0a", "l0f"):
            emit_output(b, E, st)
        return
    if TAP == "ffn6":
        emit_output(b, E, st)
        return
    unpark_merge(b, E, st)
    if TAP == "x1":
        emit_output(b, E, st)
        return
    if DO_OUTER:
        emit_deltanet(b, E, st)
    if st.get("tapped"):
        return
    if TAP == "x2" or not SECOND:
        emit_output(b, E, st)
        return
    park_and_ln(b, E, st)
    if emit_ffn6(b, E, st):
        emit_output(b, E, st)
        return
    unpark_merge(b, E, st)
    emit_output(b, E, st)


def barrier_waits(b):
    return [("T", b.val("T")), ("V", b.val("V")), ("S", b.val("S")),
            ("W", b.val("W")), ("K", b.val("K")), ("O", b.val("O")),
            ("G", b.val("G"))]



def emit_boundary_snapshot(b, E, st):
    u = E["u"]
    for bi, t in enumerate((511, 1023, 1535)):
        for c in range(4, NCH):
            dst = E["bndc"][:, bi * 4 + c - 4:bi * 4 + c - 3]
            b.op("vector", lambda v, dst=dst, c=c, t=t: v.tensor_copy(
                dst, u[:, c, t:t + 1]), waits=[("W", st["u_w"])], incs=[("V", 1)])
    st["v_bnd"] = b.val("V")


def emit_ln_slice(b, E, st, s0, shifted, gb, out_mode, park_o=None):
    pb = E["pb"]
    u = E["u"]
    xs = E["xs"]
    x0 = s0 % CF
    par = (s0 // 512) % 2
    n = 512
    v_entry = b.val("V")

    casts = []
    for c in range(NCH):
        dst = xs[:, c, x0:x0 + n]
        sh = shifted and c >= 4
        xw = [("T", st["xs_free_t"]), ("W", st["u_w"])]
        if sh and s0 == 0:
            b.op("vector", lambda v, dst=dst, c=c: v.tensor_copy(
                dst[:, 1:n], u[:, c, 0:n - 1]), waits=xw, incs=[("V", 1)])
            b.op("vector", lambda v, dst=dst: v.tensor_copy(dst[:, 0:1], E["zeroc"][:]),
                 waits=[("G", st["ginit"])], incs=[("V", 1)])
        elif sh:
            b.op("vector", lambda v, dst=dst, c=c, s0=s0: v.tensor_copy(
                dst[:, 1:n], u[:, c, s0:s0 + n - 1]), waits=xw, incs=[("V", 1)])
            bcol = E["bndc"][:, (s0 // 512 - 1) * 4 + c - 4:(s0 // 512 - 1) * 4 + c - 3]
            b.op("vector", lambda v, dst=dst, bcol=bcol: v.tensor_copy(
                dst[:, 0:1], bcol), incs=[("V", 1)])
        else:
            b.op("vector", lambda v, dst=dst, c=c, s0=s0: v.tensor_copy(
                dst, u[:, c, s0:s0 + n]), waits=xw, incs=[("V", 1)])
        casts.append(b.val("V"))

    for c in range(NCH):
        b.mm(pb[6][0:1, 0:n], E["onec"][:], xs[:, c, x0:x0 + n],
             start=(c == 0), stop=(c == NCH - 1),
             waits=[("V", casts[c])] + ([("V", v_entry), ("G", st["ginit"])]
                                        if c == 0 else []),
             incs=([("T", 1)] if c == NCH - 1 else []))
    t_sx = b.val("T")
    sq_cons = st.get("lnsq_cons", [0, 0])
    for c in range(NCH):
        slot = c % 2
        sq = aap("sqs", 0, 128, slot * 1024, [[1, n]])
        b.op("vector", lambda v, sq=sq, c=c, x0=x0: v.tensor_tensor(
            sq, xs[:, c, x0:x0 + n], xs[:, c, x0:x0 + n], Op.mult),
            waits=[("T", sq_cons[slot])], incs=[("V", 1)])
        vv = b.val("V")
        b.mm(pb[7][0:1, 0:n], E["onec"][:], sq, start=(c == 0), stop=(c == NCH - 1),
             waits=[("V", vv)] + ([("V", v_entry)] if c == 0 else []),
             incs=[("T", 1)])
        sq_cons[slot] = b.val("T")
    st["lnsq_cons"] = sq_cons
    t_sx2 = b.val("T")

    rf = E["rowf"]
    ro = par * 512
    murow = rf[0:1, 0:n]
    mu2 = rf[0:1, 512:512 + n]
    var = rf[0:1, 1024:1024 + n]
    lnr = rf[0:1, 1536:1536 + n]
    inv = 1.0 / float(D)
    b.op("vector", lambda v: v.tensor_scalar(murow, pb[6][0:1, 0:n], inv, None, Op.mult),
         waits=[("T", t_sx), ("S", st.get("ln_rows_s", 0))], incs=[("V", 1)])
    b.op("vector", lambda v: v.tensor_tensor(mu2, murow, murow, Op.mult), incs=[("V", 1)])
    b.op("vector", lambda v: v.scalar_tensor_tensor(var, pb[7][0:1, 0:n], inv, mu2,
                                                    Op.mult, Op.subtract),
         waits=[("T", t_sx2)], incs=[("V", 1)])
    v_var = b.val("V")
    b.op("scalar", lambda sc: sc.activation(lnr, var, AF.Ln, bias=E["epsc"][0:1, :],
                                            scale=1.0),
         waits=[("V", v_var), ("G", st["ginit"])], incs=[("S", 1)])
    rstdf = mu2
    b.op("scalar", lambda sc: sc.activation(rstdf, lnr, AF.Exp, bias=0.0, scale=-0.5),
         incs=[("S", 1)])
    s_rstd = b.val("S")
    st["ln_rows_s"] = s_rstd
    rbf_cons = st.get("rowbf_cons", [0])
    rbf = E["rowbf"]
    rstd_hi = rbf[0:1, 0:n]
    rstd_lo = rbf[0:1, 512:512 + n]
    nm_hi = rbf[0:1, 1024:1024 + n]
    nm_lo = rbf[0:1, 1536:1536 + n]
    b.op("vector", lambda v: v.tensor_copy(rstd_hi, rstdf),
         waits=[("S", s_rstd), ("T", rbf_cons[0])], incs=[("V", 1)])
    b.op("vector", lambda v: v.tensor_tensor(rstd_lo, rstdf, rstd_hi, Op.subtract),
         incs=[("V", 1)])
    nmursf = rf[0:1, 1536:1536 + n]
    b.op("vector", lambda v: v.scalar_tensor_tensor(nmursf, murow, -1.0, rstdf,
                                                    Op.mult, Op.mult),
         waits=[("S", s_rstd)], incs=[("V", 1)])
    b.op("vector", lambda v: v.tensor_copy(nm_hi, nmursf), incs=[("V", 1)])
    b.op("vector", lambda v: v.tensor_tensor(nm_lo, nmursf, nm_hi, Op.subtract),
         incs=[("V", 1)])
    v_rows = b.val("V")

    if gb is None:
        b.mm(pb[4][:, 0:n], E["onesrow"][:], rstd_hi, start=True, stop=False,
             ksize=1, waits=[("V", v_rows), ("V", v_entry)])
        b.mm(pb[4][:, 0:n], E["onesrow"][:], rstd_lo, start=False, stop=True,
             ksize=1, incs=[("T", 1)])
        b.mm(pb[5][:, 0:n], E["onesrow"][:], nm_hi, start=True, stop=False, ksize=1)
        b.mm(pb[5][:, 0:n], E["onesrow"][:], nm_lo, start=False, stop=True,
             ksize=1, incs=[("T", 1)])

    applies = []
    for c in range(NCH):
        if gb is not None:
            glhs = E["gbsb"][0:1, c * 128:(c + 1) * 128]
            blhs = E["gbsb"][0:1, D + c * 128:D + (c + 1) * 128]
            bw = [("V", v_rows), ("W", st.get("w_params", 0)),
                  ("V", v_entry if c == 0 else 0)]
            if applies:
                bw.append(("V", applies[-1]))
            b.mm(pb[4][:, 0:n], glhs, rstd_hi, start=True, stop=False, ksize=1,
                 waits=bw)
            b.mm(pb[4][:, 0:n], glhs, rstd_lo, start=False, stop=True, ksize=1,
                 incs=[("T", 1)])
            b.mm(pb[5][:, 0:n], glhs, nm_hi, start=True, stop=False, ksize=1)
            b.mm(pb[5][:, 0:n], glhs, nm_lo, start=False, stop=False, ksize=1)
            b.mm(pb[5][:, 0:n], blhs, E["onesN"][:, 0:n], start=False, stop=True,
                 ksize=1, incs=[("T", 1)])
        t_bc = b.val("T")
        slot = c % 2
        tf = E["tmpf"][:, slot, :]
        sh = shifted and c >= 4
        src1 = xs[:, c, x0:x0 + n] if sh else u[:, c, s0:s0 + n]
        b.op("vector", lambda v, tf=tf, src1=src1: v.tensor_tensor(
            tf, src1, pb[4][:, 0:n], Op.mult),
            waits=[("T", t_bc), ("W", st["u_w"]), ("T", t_sx), ("T", t_sx2)],
            incs=[("V", 1)])
        if out_mode == "u":
            dst = u[:, c, s0:s0 + n]
            pw = [("O", park_o[c])] if park_o else []
        else:
            dst = xs[:, c, x0:x0 + n]
            pw = []
        b.op("vector", lambda v, dst=dst, tf=tf: v.tensor_tensor(
            dst, tf, pb[5][:, 0:n], Op.add),
            waits=[("T", t_sx), ("T", t_sx2)] + pw, incs=[("V", 1)])
        applies.append(b.val("V"))
    rbf_cons[0] = b.val("T")
    st["rowbf_cons"] = rbf_cons
    return b.val("V")


def park_and_ln(b, E, st):
    u = E["u"]
    park_o = []
    bw = barrier_waits(b)
    for c in range(NCH):
        b.op("gpsimd", lambda g, c=c: g.dma_start(E["dpark"][c], u[:, c, :]),
             waits=bw, incs=[("O", 16)])
    park_o = [b.val("O")] * NCH
    for sl in range(4):
        emit_ln_slice(b, E, st, sl * 512, shifted=False, gb=None, out_mode="u",
                      park_o=park_o)
    st["xs_free_t"] = 0


def unpark_merge(b, E, st):
    u = E["u"]
    bw = barrier_waits(b)
    for c in range(NCH):
        b.op("vector", lambda v, c=c: v.tensor_scalar(
            u[:, c, :], u[:, c, :], 0.5, None, Op.mult),
            waits=bw, incs=[("V", 1)])
        vv = b.val("V")
        b.op("gpsimd", lambda g, c=c: g.dma_start(
            u[:, c, :], E["dpark"][c], accum_op=Op.add),
            waits=bw + [("V", vv)], incs=[("W", 16)])
    st["u_w"] = b.val("W")


def emit_output(b, E, st):
    u = E["u"]
    bw = barrier_waits(b)
    for c in range(NCH):
        b.op("sync", lambda s, c=c: s.dma_start(
            E["do"][c * 128:(c + 1) * 128, :], u[:, c, :]), waits=bw, incs=[("K", 16)])
    fin = b.val("K")
    b.op("sync", lambda s: None, waits=[("K", fin)])



def dbg_dump(b, E, regions):
    bw = barrier_waits(b)
    for ap, r0, nr, ncol in regions:
        b.op("gpsimd", lambda g, ap=ap, r0=r0, nr=nr, ncol=ncol: g.dma_start(
            E["do"][r0:r0 + nr, 0:ncol], ap), waits=bw, incs=[("W", 16)])
    fin = b.val("W")
    b.op("sync", lambda s: None, waits=[("W", fin)])


def emit_ffn6(b, E, st):
    for li in range(LAYERS):
        emit_layer_params(b, E, st, li)
        if st.get("tapped"):
            return True
        if emit_attn_sublayer(b, E, st, li):
            return True
        if TAP == "l0a" and li == 0:
            return True
        emit_ffn_sublayer(b, E, st, li)
        if TAP == "l0f" and li == 0:
            return True
    return False


def emit_layer_params(b, E, st, li):
    pb = E["pb"]
    bw = [("T", b.val("T")), ("V", b.val("V")), ("S", b.val("S"))]
    b.op("gpsimd", lambda g: g.dma_start(
        E["gbsb"][:], E["dgb"][li:li + 1].rearrange("o s n -> o (s n)")),
        waits=bw, incs=[("W", 16)])
    b.op("gpsimd", lambda g: g.dma_start(
        E["b1sb"][:].rearrange("p (s i) -> p s i", s=2), E["db1"][li]), incs=[("W", 16)])
    b.op("gpsimd", lambda g: g.dma_start(E["b2sb"][:], E["db2"][li]), incs=[("W", 16)])
    b.op("gpsimd", lambda g: g.dma_start(E["qsrow"][0:1, 256:384], E["dqs"][li:li + 1, :]),
         incs=[("W", 16)])
    b.op("gpsimd", lambda g: g.dma_start(E["qsrow"][0:1, 128:256], E["dks"][li:li + 1, :]),
         incs=[("W", 16)])
    wv = b.val("W")
    st["w_params"] = wv
    qsr = E["qsrow"]
    b.op("vector", lambda v: v.tensor_scalar(qsr[0:1, 0:128], qsr[0:1, 256:384],
                                             8.0, None, Op.mult),
         waits=bw + [("W", wv)], incs=[("V", 1)])
    v_rows = b.val("V")
    for i, src0 in enumerate((0, 64, 128, 192)):
        b.mm(pb[6][:, 0:64], E["onesrow"][:], qsr[0:1, src0:src0 + 64],
             start=True, stop=True, ksize=1,
             waits=[("V", b.val("V"))], incs=[("T", 1)])
        tv = b.val("T")
        b.op("vector", lambda v, i=i: v.tensor_copy(E["qsb"][:, i * 64:(i + 1) * 64],
                                                    pb[6][:, 0:64]),
             waits=[("T", tv)], incs=[("V", 1)])
    v_qsb = b.val("V")
    tb = apv(E["wring"], 0, 128, 0, [[1024, 4], [1, 1024]])
    b.op("gpsimd", lambda g, tb=tb: g.dma_start(
        tb, E["dtabs"].rearrange("a p t d -> p a (t d)")),
        waits=[("T", st["wring_free_t"])], incs=[("W", 16)])
    w_tb = b.val("W")
    for i in range(4):
        bc = apv(E["qsb"], 0, 128, i * 64, [[0, 16], [1, 64]])
        tbi = apv(E["wring"], 0, 128, i * 1024, [[1, 1024]])
        b.op("vector", lambda v, i=i, bc=bc, tbi=tbi: v.tensor_tensor(
            E["tabf"][:, i, :], tbi, bc, Op.mult),
            waits=[("V", v_qsb), ("W", w_tb)], incs=[("V", 1)])
    st["v_tabs"] = b.val("V")
    st["tabs_wring_v"] = b.val("V")
    if TAP == "dtab" and li == 0:
        dbg_dump(b, E, [(E["qsrow"][0:1, 0:384], 0, 1, 384),
                        (E["qsb"][:, 0:256], 1, 128, 256),
                        (apv(E["tabf"], 0, 128, 0, [[1, 2048]]), 129, 128, 2048),
                        (apv(E["tabf"], 0, 128, 2048, [[1, 2048]]), 257, 128, 2048)])
        st["tapped"] = True


def emit_attn_sublayer(b, E, st, li):
    pb = E["pb"]
    u = E["u"]
    xs = E["xs"]
    wr = E["wring"]
    emit_boundary_snapshot(b, E, st)

    wslot_cons = [st["wring_free_t"], st["wring_free_t"]]
    tabv = st.get("tabs_wring_v", 0)
    pbank_free = st.get("pbank_free", [[] for _ in range(8)])
    ring_s = st.get("attn_ring_s", [0, 0, 0, 0])
    ring_v = st.get("attn_ring_v", [0, 0, 0, 0])
    pn_cons = st.get("pn_cons", [0, 0])
    pt_cons = st.get("pt_cons", [0, 0])
    sbank_cons = st.get("sbank_cons", [[], []])
    sidx0 = st.get("attn_sidx", 0)

    for ch in range(4):
        s0 = ch * CA
        casts = []
        for c in range(NCH):
            dst = xs[:, c, 0:CA]
            sh = c >= 4
            xw = [("T", st["xs_free_t"]), ("W", st["u_w"])]
            if sh and s0 == 0:
                b.op("vector", lambda v, dst=dst, c=c: v.tensor_copy(
                    dst[:, 1:CA], u[:, c, 0:CA - 1]), waits=xw, incs=[("V", 1)])
                b.op("vector", lambda v, dst=dst: v.tensor_copy(dst[:, 0:1],
                     E["zeroc"][:]), waits=[("G", st["ginit"])], incs=[("V", 1)])
            elif sh:
                b.op("vector", lambda v, dst=dst, c=c, s0=s0: v.tensor_copy(
                    dst[:, 1:CA], u[:, c, s0:s0 + CA - 1]), waits=xw, incs=[("V", 1)])
                bcol = E["bndc"][:, (s0 // 512 - 1) * 4 + c - 4:(s0 // 512 - 1) * 4 + c - 3]
                b.op("vector", lambda v, dst=dst, bcol=bcol: v.tensor_copy(
                    dst[:, 0:1], bcol), incs=[("V", 1)])
            else:
                src = u[:, c, s0:s0 + CA]
                b.op("vector", lambda v, dst=dst, src=src: v.tensor_copy(dst, src),
                     waits=xw, incs=[("V", 1)])
            casts.append(b.val("V"))

        hw = [("T", st.get("attn_prev_t", 0))]
        if ch > 0:
            b.op("vector", lambda v: v.tensor_copy(
                aap("kfm", 0, 128, 0, [[768, 8], [1, 256]]),
                aap("kfm", 0, 128, 512, [[768, 8], [1, 256]])),
                waits=hw, incs=[("V", 1)])
            b.op("vector", lambda v: v.tensor_copy(
                aap("vtok", 0, 128, 0, [[1, 2048]]),
                aap("vtok", 0, 128, 4096, [[1, 2048]])),
                waits=hw, incs=[("V", 1)])
        v_halo = b.val("V")

        evac_v = {}
        for cg in range(6):
            slot = cg % 2
            wsl = apv(wr, 0, 128, slot * 4096, [[512, 8], [1, 512]])
            src = E["dwqkv"][li].rearrange("kc k n -> k kc n")[:, :, cg * 512:(cg + 1) * 512]
            b.op("gpsimd", lambda g, wsl=wsl, src=src: g.dma_start(wsl, src),
                 waits=[("T", wslot_cons[slot]), ("V", tabv)], incs=[("W", 16)])
            wv = b.val("W")
            for tt in range(4):
                bank = pb[tt % 2]
                for kc in range(NCH):
                    b.mm(bank[:, 0:512],
                         xs[:, kc, tt * 128:(tt + 1) * 128],
                         apv(wr, 0, 128, slot * 4096 + kc * 512, [[1, 512]]),
                         start=(kc == 0), stop=(kc == NCH - 1),
                         waits=[("W", wv), ("V", casts[kc])] +
                               (pbank_free[tt % 2] if kc == 0 else []),
                         incs=([("T", 1)] if kc == NCH - 1 else []))
                tv = b.val("T")
                if cg < 2:
                    dst = aap("qtok", 0, 128, tt * 1024 + cg * 512, [[1, 512]])
                elif cg < 4:
                    dst = aap("ktok", 0, 128, tt * 1024 + (cg - 2) * 512, [[1, 512]])
                else:
                    dst = aap("vtok", 0, 128, (2 + tt) * 1024 + (cg - 4) * 512, [[1, 512]])
                wt = [("T", tv), ("V", v_halo)]
                if cg < 2:
                    wt += [("T", st.get("aout_free_t", 0)), ("K", st.get("qtok_free_k", 0))]
                elif cg < 4:
                    wt += [("K", st.get("ktok_free_k", 0))]
                else:
                    wt += [("T", st.get("attn_prev_t", 0))]
                b.op("vector", lambda v, dst=dst, bank=bank: v.tensor_copy(
                    dst, bank[:, 0:512]), waits=wt, incs=[("V", 1)])
                pbank_free[tt % 2] = [("V", b.val("V"))]
                evac_v[(cg, tt)] = b.val("V")
            wslot_cons[slot] = b.val("T")
        v_vall = max(evac_v[(cg, tt)] for cg in (4, 5) for tt in range(4))
        if ch == 0 and TAP == "dq":
            dbg_dump(b, E, [(aap("qtok", 0, 128, tt * 1024, [[1, 1024]]),
                             tt * 128, 128, 1024) for tt in range(4)])
            return True
        if ch == 0 and TAP == "dk0":
            dbg_dump(b, E, [(aap("ktok", 0, 128, tt * 1024, [[1, 1024]]),
                             tt * 128, 128, 1024) for tt in range(4)])
            return True
        if ch == 0 and TAP == "dxs":
            dbg_dump(b, E, [(xs[:, c, 0:CA], c * 128, 128, 512) for c in range(NCH)])
            return True

        v_tabs = st.get("v_tabs", 0)
        sq_cons = st.get("asq_cons", [0, 0])
        rot_cons = st.get("rot_cons", [0, 0])
        q_done = {}
        k_done = {}
        for which, reg, ctab, stab, rncol0 in (("q", "qtok", 0, 1, 0),
                                               ("k", "ktok", 2, 3, 64)):
            for tt in range(4):
                ttg = ch * 4 + tt
                g0 = 0 if which == "q" else 2
                ready = [("V", max(evac_v[(g0, tt)], evac_v[(g0 + 1, tt)]))]
                slot = tt % 2
                sq = aap("sqs", 0, 128, slot * 1024, [[1, 1024]])
                tile = aap(reg, 0, 128, tt * 1024, [[1, 1024]])
                b.op("vector", lambda v, sq=sq, tile=tile: v.tensor_tensor(
                    sq, tile, tile, Op.mult),
                    waits=ready + [("V", sq_cons[slot])], incs=[("V", 1)])
                nrm = apv(E["colb"], 0, 128, rncol0 + tt * 16, [[1, 16]])
                sq3 = aap("sqs", 0, 128, slot * 1024, [[64, 16], [1, 64]])
                b.op("vector", lambda v, nrm=nrm, sq3=sq3: v.reduce_sum(nrm, sq3, axis=AX),
                     incs=[("V", 1)])
                v_n = b.val("V")
                b.op("scalar", lambda sc, nrm=nrm: sc.activation(
                    nrm, nrm, AF.Ln, bias=E["eps18"][:, 0:1], scale=1.0),
                    waits=[("V", v_n), ("G", st["ginit"])], incs=[("S", 1)])
                b.op("scalar", lambda sc, nrm=nrm: sc.activation(
                    nrm, nrm, AF.Exp, bias=0.0, scale=-0.5), incs=[("S", 1)])
                s_rn = b.val("S")
                rsl = tt % 2
                tmp = aap("rott", 0, 128, rsl * 1024, [[1, 1024]])
                rotv = aap(reg, 0, 128, tt * 1024 + 32, [[64, 16], [-32, 2], [1, 32]])
                stv = apv(E["tabf"], 0, 128, stab * 1024 + ttg * 64,
                          [[0, 16], [32, 2], [1, 32]])
                ctv = apv(E["tabf"], 0, 128, ctab * 1024 + ttg * 64, [[0, 16], [1, 64]])
                b.op("vector", lambda v, tmp=tmp, rotv=rotv, stv=stv: v.tensor_tensor(
                    tmp, rotv, stv, Op.mult),
                    waits=[("V", v_tabs), ("V", rot_cons[rsl])], incs=[("V", 1)])
                xr = sq
                b.op("vector", lambda v, xr=xr, tile=tile, ctv=ctv: v.tensor_tensor(
                    xr, tile, ctv, Op.mult), incs=[("V", 1)])
                kf0 = aap("rott", 0, 128, (rsl ^ 1) * 1024, [[1, 1024]])
                b.op("vector", lambda v, kf0=kf0, xr=xr, tmp=tmp: v.tensor_tensor(
                    kf0, xr, tmp, Op.add), incs=[("V", 1)])
                rnb = apv(E["colb"], 0, 128, rncol0 + tt * 16, [[1, 16], [0, 64]])
                b.op("vector", lambda v, tile=tile, kf0=kf0, rnb=rnb: v.tensor_tensor(
                    tile, kf0, rnb, Op.mult),
                    waits=[("S", s_rn)], incs=[("V", 1)])
                if which == "q":
                    q_done[tt] = b.val("V")
                else:
                    k_done[tt] = b.val("V")
                sq_cons[slot] = b.val("V")
                rot_cons[rsl] = b.val("V")
        st["asq_cons"] = sq_cons
        st["rot_cons"] = rot_cons
        if ch == 0 and TAP == "dnrm":
            dbg_dump(b, E, [(apv(E["colb"], 0, 128, 0, [[1, 256]]), 0, 128, 256),
                            (aap("sqs", 0, 128, 0, [[1, 1024]]), 128, 128, 1024),
                            (aap("sqs", 0, 128, 1024, [[1, 1024]]), 256, 128, 1024)])
            return True
        if ch == 0 and TAP == "dqr":
            dbg_dump(b, E, [(aap("qtok", 0, 128, tt * 1024, [[1, 1024]]),
                             tt * 128, 128, 1024) for tt in range(4)])
            return True
        if ch == 0 and TAP == "dkr":
            dbg_dump(b, E, [(aap("ktok", 0, 128, tt * 1024, [[1, 1024]]),
                             tt * 128, 128, 1024) for tt in range(4)] +
                     [(apv(E["colb"], 0, 128, 0, [[1, 256]]), 512, 128, 256)])
            return True

        fmw = [("T", st.get("attn_prev_t", 0))]
        qxp = {}
        kxp = {}
        for tt in range(4):
            src = aap("qtok", 0, 128, tt * 1024, [[1, 1024]])
            dst = aap("qfm", 0, 128, tt * 128, [[512, 8], [1, 128]])
            b.op("sync", lambda s, dst=dst, src=src: s.dma_start(dst, src, transpose=True),
                 waits=[("V", q_done[tt])] + fmw, incs=[("K", 16)])
            qxp[tt] = b.val("K")
        for tt in range(4):
            src = aap("ktok", 0, 128, tt * 1024, [[1, 1024]])
            dst = aap("kfm", 0, 128, 256 + tt * 128, [[768, 8], [1, 128]])
            b.op("sync", lambda s, dst=dst, src=src: s.dma_start(dst, src, transpose=True),
                 waits=[("V", k_done[tt]), ("V", v_halo)] + fmw, incs=[("K", 16)])
            kxp[tt] = b.val("K")
        k_allxp = b.val("K")
        st["qtok_free_k"] = max(qxp.values())
        st["ktok_free_k"] = k_allxp
        if ch == 0 and TAP == "dqfm":
            dbg_dump(b, E, [(aap("qfm", 0, 128, 0, [[1, 2048]]), 0, 128, 2048),
                            (aap("qfm", 0, 128, 2048, [[1, 2048]]), 128, 128, 2048),
                            (aap("kfm", 0, 128, 0, [[1, 2048]]), 256, 128, 2048),
                            (aap("kfm", 0, 128, 2048, [[1, 2048]]), 384, 128, 2048)])
            return True

        sidx = sidx0
        aout_v = []
        obank_cons = st.get("obank_cons", [[], []])
        for wl in range(2):
            wg = 2 * ch + wl
            w0 = (wg == 0)
            kc0 = 256 if w0 else wl * 256
            N = 256 if w0 else 512
            nm = 2 if w0 else 4
            for p in range(8):
                obank = pb[4 + (p % 2)]
                for hh in range(2):
                    h = 2 * p + hh
                    pvw = []
                    for isub in range(2):
                        sb_i = sidx % 2
                        sbank = pb[2 + sb_i]
                        lhs = aap("qfm", hh * 64, 64,
                                  (h // 2) * 512 + wl * 256 + isub * 128, [[1, 128]])
                        rhs = aap("kfm", hh * 64, 64, (h // 2) * 768 + kc0, [[1, N]])
                        b.mm(sbank[:, 0:N], lhs, rhs, start=True, stop=False,
                             kbase=hh * 64, ksize=64,
                             waits=[("K", k_allxp)] + sbank_cons[sb_i])
                        vi = (2 + isub) if w0 else isub
                        mrhs = E["mrc"][:, vi, 256:512] if w0 else E["mrc"][:, vi, 0:512]
                        b.mm(sbank[:, 0:N], E["ltc"][:], mrhs, start=False, stop=True,
                             waits=[("W", st["winit"])], incs=[("T", 1)])
                        t_s = b.val("T")
                        ri = sidx % 4
                        negm = apv(E["colb"], 0, 128, 128 + ri * 4, [[1, 1]])
                        b.op("vector", lambda v, negm=negm, sbank=sbank, N=N: v.reduce_max(
                            negm, sbank[:, 0:N], axis=AX, negate=True),
                            waits=[("T", t_s), ("S", ring_s[ri])],
                            incs=[("V", 1)])
                        v_m = b.val("V")
                        lcol = apv(E["colb"], 0, 128, 130 + ri * 4, [[1, 1]])
                        pdst = aap("pnbuf", 0, 128, hh * 2048 + isub * 512, [[1, N]])
                        b.op("scalar", lambda sc, pdst=pdst, sbank=sbank, negm=negm,
                             lcol=lcol, N=N: sc.activation(
                                 pdst, sbank[:, 0:N], AF.Exp, bias=negm, scale=1.0,
                                 accum_out=lcol),
                             waits=[("V", v_m)], incs=[("S", 1)])
                        s_exp = b.val("S")
                        sbank_cons[sb_i] = [("S", s_exp)]
                        rl = apv(E["colb"], 0, 128, 131 + ri * 4, [[1, 1]])
                        b.op("vector", lambda v, rl=rl, lcol=lcol: v.reciprocal(rl, lcol),
                             waits=[("S", s_exp)], incs=[("V", 1)])
                        v_rl = b.val("V")
                        pnd = aap("abuf", 0, 128, hh * 1024 + isub * 512, [[1, N]])
                        b.op("scalar", lambda sc, pnd=pnd, pdst=pdst, rl=rl: sc.activation(
                            pnd, pdst, AF.Copy, bias=0.0, scale=rl),
                            waits=[("V", v_rl), ("K", pn_cons[hh])], incs=[("S", 1)])
                        pvw.append(("S", b.val("S")))
                        ring_s[ri] = b.val("S")
                        ring_v[ri] = v_rl
                        sidx += 1
                    psrc = aap("abuf", 0, 128, hh * 1024, [[1, 1024]])
                    ptd = aap("ptbuf", 0, 128, hh * 1024, [[128, 8], [1, 128]])
                    b.op("sync", lambda s, ptd=ptd, psrc=psrc: s.dma_start(
                        ptd, psrc, transpose=True),
                        waits=pvw + [("T", pt_cons[hh])], incs=[("K", 16)])
                    k_pt = b.val("K")
                    pn_cons[hh] = k_pt
                    for jm in range(nm):
                        vti = wl * 2 + jm + (2 if w0 else 0)
                        vsl = aap("vtok", 0, 128, vti * 1024 + h * 64, [[1, 64]])
                        prhs = aap("ptbuf", 0, 128, hh * 1024 + jm * 128,
                                   [[512, 2], [1, 128]])
                        ow = [("K", k_pt), ("V", v_vall)]
                        if jm == 0:
                            ow += obank_cons[p % 2]
                        b.mm(obank[hh * 64:hh * 64 + 64, 0:256], vsl, prhs,
                             start=(jm == 0), stop=(jm == nm - 1), tp=(0, hh * 64),
                             waits=ow, incs=([("T", 1)] if jm == nm - 1 else []),
                             sgc=True)
                    pt_cons[hh] = b.val("T")
                t_pv = b.val("T")
                adst = aap("aout", 0, 128, p * 512 + wl * 256, [[1, 256]])
                b.op("vector", lambda v, adst=adst, obank=obank: v.tensor_copy(
                    adst, obank[:, 0:256]),
                    waits=[("T", t_pv), ("K", st["qtok_free_k"])], incs=[("V", 1)])
                obank_cons[p % 2] = [("V", b.val("V"))]
                aout_v.append(b.val("V"))
        st["obank_cons"] = obank_cons
        st["attn_prev_t"] = b.val("T")
        if ch == 0 and TAP == "dao":
            dbg_dump(b, E, [(aap("aout", 0, 128, 0, [[1, 2048]]), 0, 128, 2048),
                            (aap("aout", 0, 128, 2048, [[1, 2048]]), 128, 128, 2048)])
            return True
        if ch == 0 and TAP == "dpn":
            dbg_dump(b, E, [(aap("abuf", 0, 128, 0, [[1, 2048]]), 0, 128, 2048),
                            (aap("ptbuf", 0, 128, 0, [[1, 2048]]), 128, 128, 2048)])
            return True

        v_aout = max(aout_v)
        for oc in range(NCH):
            slot = oc % 2
            wsl = apv(wr, 0, 128, slot * 4096, [[128, 8], [1, 128]])
            src = E["dwo"][li].rearrange("kc k n -> k kc n")[:, :, oc * 128:(oc + 1) * 128]
            b.op("gpsimd", lambda g, wsl=wsl, src=src: g.dma_start(wsl, src),
                 waits=[("T", wslot_cons[slot])], incs=[("W", 16)])
            wv = b.val("W")
            bank = pb[oc % 2]
            for kc in range(NCH):
                b.mm(bank[:, 0:512],
                     apv(wr, 0, 128, slot * 4096 + kc * 128, [[1, 128]]),
                     aap("aout", 0, 128, kc * 512, [[1, 512]]),
                     start=(kc == 0), stop=(kc == NCH - 1),
                     waits=[("W", wv), ("V", v_aout)] +
                           (pbank_free[oc % 2] if kc == 0 else []),
                     incs=([("T", 1)] if kc == NCH - 1 else []))
            tv = b.val("T")
            b.op("vector", lambda v, oc=oc, bank=bank, s0=s0: v.tensor_tensor(
                u[:, oc, s0:s0 + CA], u[:, oc, s0:s0 + CA], bank[:, 0:512], Op.add),
                waits=[("T", tv), ("W", st["u_w"])], incs=[("V", 1)])
            pbank_free[oc % 2] = [("V", b.val("V"))]
            wslot_cons[slot] = b.val("T")
        st["aout_free_t"] = b.val("T")
        st["xs_free_t"] = b.val("T")
        st["attn_prev_t"] = b.val("T")
    st["pbank_free"] = pbank_free
    st["attn_ring_s"] = ring_s
    st["attn_ring_v"] = ring_v
    st["pn_cons"] = pn_cons
    st["pt_cons"] = pt_cons
    st["sbank_cons"] = sbank_cons
    st["attn_sidx"] = sidx
    st["wring_free_t"] = b.val("T")
    return False


def emit_ffn_sublayer(b, E, st, li):
    pb = E["pb"]
    u = E["u"]
    xs = E["xs"]
    wr = E["wring"]
    emit_boundary_snapshot(b, E, st)
    pbank_free = st.get("pbank_free", [[] for _ in range(8)])
    w1slot_cons = [st["wring_free_t"]] * 4
    w2slot_cons = [st["wring_free_t"]] * 4
    gact_cons = st.get("gact_cons", [0, 0])

    for ch in range(2):
        c0 = ch * CF
        xn_v = []
        for tsl in range(2):
            s0 = c0 + tsl * 512
            vd = emit_ln_slice(b, E, st, s0, shifted=True, gb=True, out_mode="xs")
            xn_v.append(vd)
        v_xn = max(xn_v)
        if ch == 1 and TAP == "dxn" and li == 0:
            dbg_dump(b, E, [(xs[:, c, 0:CF], c * 128, 128, 1024) for c in range(NCH)])
            st["tapped"] = True
            return

        p_t = {}
        for ic in range(NIC):
            for s in range(2):
                slot = s * 2 + ic % 2
                wsl = apv(wr, 0, 128, slot * 1024, [[128, 8], [1, 128]])
                src = E["dw1"][li, s, ic].rearrange("kc k m -> k kc m")
                b.op("gpsimd", lambda g, wsl=wsl, src=src: g.dma_start(wsl, src),
                     waits=[("T", w1slot_cons[slot])], incs=[("W", 16)])
            wv = b.val("W")
            for tsl in range(2):
                bkA = pb[tsl * 2]
                bkG = pb[tsl * 2 + 1]
                for s in range(2):
                    bank = bkA if s == 0 else bkG
                    slot = s * 2 + ic % 2
                    for kc in range(NCH):
                        b.mm(bank[:, 0:512],
                             apv(wr, 0, 128, slot * 1024 + kc * 128, [[1, 128]]),
                             xs[:, kc, tsl * 512:(tsl + 1) * 512],
                             start=(kc == 0), stop=(kc == NCH - 1),
                             waits=[("W", wv), ("V", v_xn)] +
                                   (pbank_free[tsl * 2 + s] if kc == 0 else []),
                             incs=([("T", 1)] if kc == NCH - 1 else []))
                t_ag = b.val("T")
                if tsl == 1:
                    w1slot_cons[ic % 2] = t_ag
                    w1slot_cons[2 + ic % 2] = t_ag
                gsl = (ic * 2 + tsl) % 2
                gact = aap("gact", 0, 128, gsl * 512, [[1, 512]])
                b1g = apv(E["b1sb"], 0, 128, NIC + ic, [[1, 1]])
                b1a = apv(E["b1sb"], 0, 128, ic, [[1, 1]])
                b.op("scalar", lambda sc, gact=gact, bkG=bkG, b1g=b1g: sc.activation(
                    gact, bkG[:, 0:512], AF.Gelu, bias=b1g, scale=1.0),
                    waits=[("T", t_ag), ("S", gact_cons[gsl]),
                           ("W", st.get("w_params", 0)),
                           ("K", st.get("pn_cons", [0, 0])[gsl])], incs=[("S", 1)])
                s_g = b.val("S")
                gact_cons[gsl] = s_g
                pdst = aap("p", 0, 128, ic * 1024 + tsl * 512, [[1, 512]])
                b.op("vector", lambda v, pdst=pdst, bkA=bkA, b1a=b1a, gact=gact:
                     v.scalar_tensor_tensor(pdst, bkA[:, 0:512], b1a, gact,
                                            Op.add, Op.mult),
                     waits=[("S", s_g), ("T", st.get("p_free_t", 0)),
                            ("K", st.get("qtok_free_k", 0))], incs=[("V", 1)])
                pv = b.val("V")
                p_t[(ic, tsl)] = pv
                pbank_free[tsl * 2] = [("V", pv)]
                pbank_free[tsl * 2 + 1] = [("S", s_g)]
        st["gact_cons"] = gact_cons

        for tsl in range(2):
            v_pall = max(p_t[(ic, tsl)] for ic in range(NIC))
            evw = []
            for ic in range(NIC):
                slot = ic % 4
                wsl = apv(wr, 0, 128, 4096 + slot * 1024, [[1, 1024]])
                src = E["dw2"][li, ic]
                b.op("gpsimd", lambda g, wsl=wsl, src=src: g.dma_start(wsl, src),
                     waits=[("T", w2slot_cons[slot])], incs=[("W", 16)])
                wv = b.val("W")
                for oc in range(NCH):
                    ww = [("W", wv), ("V", p_t[(ic, tsl)])]
                    if ic == 0:
                        ww += [("V", v_pall)] + pbank_free[oc]
                    b.mm(pb[oc][:, 0:512],
                         apv(wr, 0, 128, 4096 + slot * 1024 + oc * 128, [[1, 128]]),
                         aap("p", 0, 128, ic * 1024 + tsl * 512, [[1, 512]]),
                         start=(ic == 0), stop=(ic == NIC - 1),
                         waits=ww,
                         incs=([("T", 1)] if oc == NCH - 1 else []))
                w2slot_cons[slot] = b.val("T")
            t_w2 = b.val("T")
            for oc in range(NCH):
                b2c = apv(E["b2sb"], 0, 128, oc, [[1, 1]])
                dst = u[:, oc, c0 + tsl * 512:c0 + (tsl + 1) * 512]
                b.op("vector", lambda v, dst=dst, oc=oc, b2c=b2c: v.scalar_tensor_tensor(
                    dst, pb[oc][:, 0:512], b2c, dst, Op.add, Op.add),
                    waits=[("T", t_w2), ("W", st["u_w"])], incs=[("V", 1)])
                pbank_free[oc] = [("V", b.val("V"))]
        st["p_free_t"] = b.val("T")
        st["xs_free_t"] = b.val("T")
    st["pbank_free"] = pbank_free
    st["wring_free_t"] = b.val("T")


def emit_deltanet(b, E, st):
    pb = E["pb"]
    old_spacer = b.spacer
    b.spacer = (pb[2][0:1, 400:401], E["onec"][:], E["onec"][:])
    u = E["u"]
    xs = E["xs"]
    wr = E["wring"]
    wslot_cons = [st["wring_free_t"]] * 2
    pbank_free = st.get("pbank_free", [[] for _ in range(8)])
    abuf_cons = [0, 0, 0, 0]
    dn_sq = [0, 0]
    st["xs_free_t"] = st.get("xs_free_t", 0)

    for ch in range(4):
        s0 = ch * 512
        x0 = (ch % 2) * 512
        v_xo = emit_ln_slice(b, E, st, s0, shifted=False, gb=None, out_mode="xs")
        obank_cons = [[("V", v_xo)], [("V", v_xo)]]

        evac_v = {}
        for pi, (dw, reg) in enumerate((("dowq", "qtok"), ("dowk", "ktok"),
                                        ("dowv", "vtok"))):
            for cg in range(2):
                slot = (pi * 2 + cg) % 2
                wsl = apv(wr, 0, 128, slot * 4096, [[512, 8], [1, 512]])
                src = E[dw].rearrange("kc k n -> k kc n")[:, :, cg * 512:(cg + 1) * 512]
                b.op("gpsimd", lambda g, wsl=wsl, src=src: g.dma_start(wsl, src),
                     waits=[("T", wslot_cons[slot])], incs=[("W", 16)])
                wv = b.val("W")
                for tt in range(4):
                    bank = pb[tt % 2]
                    for kc in range(NCH):
                        b.mm(bank[:, 0:512],
                             xs[:, kc, x0 + tt * 128:x0 + (tt + 1) * 128],
                             apv(wr, 0, 128, slot * 4096 + kc * 512, [[1, 512]]),
                             start=(kc == 0), stop=(kc == NCH - 1),
                             waits=[("W", wv), ("V", v_xo)] +
                                   (pbank_free[tt % 2] if kc == 0 else []),
                             incs=([("T", 1)] if kc == NCH - 1 else []))
                    tv = b.val("T")
                    dst = aap(reg, 0, 128, tt * 1024 + cg * 512, [[1, 512]])
                    b.op("vector", lambda v, dst=dst, bank=bank: v.tensor_copy(
                        dst, bank[:, 0:512]),
                        waits=[("T", tv), ("K", st.get("dn_fm_k", 0)),
                               ("T", st.get("dn_scan_t", 0))], incs=[("V", 1)])
                    pbank_free[tt % 2] = [("V", b.val("V"))]
                    evac_v[(pi, cg, tt)] = b.val("V")
                wslot_cons[slot] = b.val("T")
        wslb = apv(E["qsb"], 0, 128, 0, [[16, 8], [1, 16]])
        b.op("gpsimd", lambda g, wslb=wslb: g.dma_start(
            wslb, E["dowb"].rearrange("kc k n -> k kc n")),
            waits=[("T", st["wring_free_t"]), ("V", st.get("dn_qsb_v", 0))],
            incs=[("W", 16)])
        wvb = b.val("W")
        s_betas = {}
        for tt in range(4):
            bank = pb[2]
            for kc in range(NCH):
                b.mm(bank[:, 0:16], xs[:, kc, x0 + tt * 128:x0 + (tt + 1) * 128],
                     apv(E["qsb"], 0, 128, kc * 16, [[1, 16]]),
                     start=(kc == 0), stop=False,
                     waits=[("W", wvb), ("V", v_xo),
                            ("S", s_betas.get(tt - 1, 0))] +
                           (st.get("betabank_free", []) if kc == 0 else []))
            b.mm(bank[:, 0:16], E["onesrow"][:], E["bbrow"][:], start=False, stop=True,
                 ksize=1, waits=[("W", st["winit"])], incs=[("T", 1)])
            tv = b.val("T")
            bcol = apv(E["colb"], 0, 128, 64 + tt * 16, [[1, 16]])
            b.op("scalar", lambda sc, bcol=bcol, bank=bank: sc.activation(
                bcol, bank[:, 0:16], AF.Sigmoid, bias=0.0, scale=1.0),
                waits=[("T", tv)], incs=[("S", 1)])
            s_betas[tt] = b.val("S")
        st["betabank_free"] = [("S", b.val("S"))]

        qk_v = {}
        for which, reg, rn0, pi in (("q", "qtok", 0, 0), ("k", "ktok", 128, 1)):
            for tt in range(4):
                ready = [("V", max(evac_v[(pi, 0, tt)], evac_v[(pi, 1, tt)]))]
                slot = tt % 2
                sq = aap("sqs", 0, 128, slot * 1024, [[1, 1024]])
                tile = aap(reg, 0, 128, tt * 1024, [[1, 1024]])
                b.op("vector", lambda v, sq=sq, tile=tile: v.tensor_tensor(
                    sq, tile, tile, Op.mult),
                    waits=ready + [("V", dn_sq[slot])], incs=[("V", 1)])
                nrm = apv(E["colb"], 0, 128, rn0 + tt * 16, [[1, 16]])
                sq3 = aap("sqs", 0, 128, slot * 1024, [[64, 16], [1, 64]])
                b.op("vector", lambda v, nrm=nrm, sq3=sq3: v.reduce_sum(nrm, sq3, axis=AX),
                     incs=[("V", 1)])
                v_n = b.val("V")
                b.op("scalar", lambda sc, nrm=nrm: sc.activation(
                    nrm, nrm, AF.Ln, bias=E["eps18"][:, 0:1], scale=1.0),
                    waits=[("V", v_n), ("G", st["ginit"])], incs=[("S", 1)])
                b.op("scalar", lambda sc, nrm=nrm: sc.activation(
                    nrm, nrm, AF.Exp, bias=0.0, scale=-0.5), incs=[("S", 1)])
                s_rn = b.val("S")
                if which == "q":
                    rnb = apv(E["colb"], 0, 128, tt * 16, [[1, 16], [0, 64]])
                    b.op("vector", lambda v, tile=tile, rnb=rnb: v.tensor_tensor(
                        tile, tile, rnb, Op.mult),
                        waits=[("S", s_rn)], incs=[("V", 1)])
                dn_sq[slot] = b.val("V")
                qk_v[(which, tt)] = (b.val("V"), s_rn)
        v_brn = {}
        for tt in range(4):
            bc = apv(E["colb"], 0, 128, 64 + tt * 16, [[1, 16]])
            rc = apv(E["colb"], 0, 128, 128 + tt * 16, [[1, 16]])
            dc = apv(E["colb"], 0, 128, 192 + tt * 16, [[1, 16]])
            b.op("vector", lambda v, dc=dc, bc=bc, rc=rc: v.tensor_tensor(
                dc, bc, rc, Op.mult),
                waits=[("S", s_betas[tt]), ("S", qk_v[("k", tt)][1])], incs=[("V", 1)])
            v_brn[tt] = b.val("V")

        qxp = {}
        kxp = {}
        for which, reg, dreg, stp in (("q", "qtok", "qfm", 512), ("k", "ktok", "kfm", 768)):
            for tt in range(4):
                src = aap(reg, 0, 128, tt * 1024, [[1, 1024]])
                dst = aap(dreg, 0, 128, tt * 128, [[stp, 8], [1, 128]])
                b.op("sync", lambda s, dst=dst, src=src: s.dma_start(
                    dst, src, transpose=True),
                    waits=[("V", qk_v[(which, tt)][0]), ("T", st.get("dn_scan_t", 0))],
                    incs=[("K", 16)])
                (qxp if which == "q" else kxp)[tt] = b.val("K")
        kb_v = {}
        for tt in range(4):
            tile = aap("ktok", 0, 128, tt * 1024, [[1, 1024]])
            brnb = apv(E["colb"], 0, 128, 192 + tt * 16, [[1, 16], [0, 64]])
            b.op("vector", lambda v, tile=tile, brnb=brnb: v.tensor_tensor(
                tile, tile, brnb, Op.mult),
                waits=[("K", kxp[tt]), ("V", v_brn[tt])], incs=[("V", 1)])
            kb_v[tt] = b.val("V")

        dn_sb = st.get("dn_sb", [[], []])
        dfree = st.get("dn_dfree", [0, 0])
        for blk_i in range(4):
            dbank = pb[5 + (blk_i % 2)]
            for p in range(8):
                obank = pb[4] if p % 2 == 0 else pb[7]
                for hh in range(2):
                    h = 2 * p + hh
                    sbank = pb[2] if hh == 0 else pb[3]
                    lhs = aap("kfm", hh * 64, 64, (h // 2) * 768 + blk_i * 128, [[1, 128]])
                    rhs = aap("qfm", hh * 64, 64, (h // 2) * 512 + blk_i * 128, [[1, 128]])
                    b.mm(sbank[:, 0:128], lhs, rhs, start=True, stop=True,
                         kbase=hh * 64, ksize=64,
                         waits=[("K", kxp[blk_i]), ("K", qxp[blk_i]),
                                ("S", s_betas[3])] + dn_sb[hh],
                         incs=[("T", 1)])
                    t_sa = b.val("T")
                    asl = (p * 2 + hh) % 4
                    A = aap("abuf", 0, 128, asl * 512, [[1, 128]])
                    brncol = apv(E["colb"], 0, 128, 192 + blk_i * 16 + h, [[1, 1]])
                    b.op("vector", lambda v, A=A, sbank=sbank, brncol=brncol:
                         v.scalar_tensor_tensor(A, sbank[:, 0:128], brncol, E["utc"][:],
                                                Op.mult, Op.mult),
                         waits=[("T", t_sa), ("V", v_brn[blk_i]),
                                ("T", abuf_cons[asl]), ("W", st["winit"])],
                         incs=[("V", 1)])
                    v_a = b.val("V")
                    dn_sb[hh] = [("V", v_a)]
                    vsl = aap("vtok", 0, 128, blk_i * 1024 + h * 64, [[1, 64]])
                    ow = [("V", v_a), ("V", evac_v[(2, 0, blk_i)]),
                          ("V", evac_v[(2, 1, blk_i)])]
                    if hh == 0:
                        ow += obank_cons[p % 2]
                    b.mm(obank[hh * 64:hh * 64 + 64, 0:128], vsl, A,
                         start=True, stop=False, tp=(0, hh * 64), waits=ow, sgc=True)
                    stsl = apv(E["stb"], hh * 64, 64, p * 64, [[1, 64]])
                    b.mm(obank[hh * 64:hh * 64 + 64, 0:128], stsl,
                         aap("qfm", hh * 64, 64, (h // 2) * 512 + blk_i * 128, [[1, 128]]),
                         start=False, stop=True, kbase=hh * 64, ksize=64,
                         tp=(hh * 64, hh * 64),
                         waits=[("V", st.get("dn_stb_v", 0)), ("G", st["ginit"])],
                         incs=[("T", 1)], sgc=True)
                    abuf_cons[asl] = b.val("T")
                    b.mm(dbank[hh * 64:hh * 64 + 64, p * 64:(p + 1) * 64],
                         aap("ktok", 0, 128, blk_i * 1024 + h * 64, [[1, 64]]),
                         vsl, start=True, stop=True, tp=(0, hh * 64),
                         waits=[("V", kb_v[blk_i]), ("V", dfree[blk_i % 2])],
                         incs=[("T", 1)], sgc=True)
                t_pair = b.val("T")
                adst = aap("aout", 0, 128, p * 512 + blk_i * 128, [[1, 128]])
                b.op("vector", lambda v, adst=adst, obank=obank: v.tensor_copy(
                    adst, obank[:, 0:128]),
                    waits=[("T", t_pair), ("K", max(qxp.values()))], incs=[("V", 1)])
                obank_cons[p % 2] = [("V", b.val("V"))]
            t_deltas = b.val("T")
            for p in range(8):
                for hh in range(2):
                    dsl = dbank[hh * 64:hh * 64 + 64, p * 64:(p + 1) * 64]
                    fsl = apv(E["stf"], hh * 64, 64, p * 64, [[1, 64]])
                    b.op("vector", lambda v, fsl=fsl, dsl=dsl: v.tensor_tensor(
                        fsl, fsl, dsl, Op.add),
                        waits=[("T", t_deltas), ("G", st["ginit"])], incs=[("V", 1)])
            for p in range(8):
                fsl = apv(E["stf"], 0, 128, p * 64, [[1, 64]])
                bsl = apv(E["stb"], 0, 128, p * 64, [[1, 64]])
                b.op("vector", lambda v, fsl=fsl, bsl=bsl: v.tensor_copy(bsl, fsl),
                     incs=[("V", 1)])
            st["dn_stb_v"] = b.val("V")
            dfree[blk_i % 2] = b.val("V")
        st["dn_sb"] = dn_sb
        st["dn_dfree"] = dfree
        st["dn_scan_t"] = b.val("T")
        v_aout = b.val("V")

        for oc in range(NCH):
            slot = oc % 2
            wsl = apv(wr, 0, 128, slot * 4096, [[128, 8], [1, 128]])
            src = E["dowo"].rearrange("kc k n -> k kc n")[:, :, oc * 128:(oc + 1) * 128]
            b.op("gpsimd", lambda g, wsl=wsl, src=src: g.dma_start(wsl, src),
                 waits=[("T", wslot_cons[slot])], incs=[("W", 16)])
            wv = b.val("W")
            bank = pb[oc % 2]
            for kc in range(NCH):
                b.mm(bank[:, 0:512],
                     apv(wr, 0, 128, slot * 4096 + kc * 128, [[1, 128]]),
                     aap("aout", 0, 128, kc * 512, [[1, 512]]),
                     start=(kc == 0), stop=(kc == NCH - 1),
                     waits=[("W", wv), ("V", v_aout)] +
                           (pbank_free[oc % 2] if kc == 0 else []),
                     incs=([("T", 1)] if kc == NCH - 1 else []))
            tv = b.val("T")
            b.op("vector", lambda v, oc=oc, bank=bank, s0=s0: v.tensor_tensor(
                u[:, oc, s0:s0 + 512], u[:, oc, s0:s0 + 512], bank[:, 0:512], Op.add),
                waits=[("T", tv), ("W", st["u_w"])], incs=[("V", 1)])
            pbank_free[oc % 2] = [("V", b.val("V"))]
            wslot_cons[slot] = b.val("T")
        st["xs_free_t"] = b.val("T")
        st["dn_fm_k"] = b.val("K")
        if ch == 0 and TAP == "ddn2":
            dbg_dump(b, E, [(xs[:, c, 0:512], c * 128, 128, 512) for c in range(NCH)])
            st["tapped"] = True
            return
        if ch == 0 and TAP == "ddn":
            dbg_dump(b, E, [(aap("aout", 0, 128, 0, [[1, 2048]]), 0, 128, 2048),
                            (aap("aout", 0, 128, 2048, [[1, 2048]]), 128, 128, 2048),
                            (apv(E["stf"], 0, 128, 0, [[1, 512]]), 256, 128, 512),
                            (apv(E["colb"], 0, 128, 0, [[1, 256]]), 384, 128, 256),
                            (apv(E["qsb"], 0, 128, 0, [[1, 128]]), 512, 128, 128),
                            (xs[:, 0, 0:512], 640, 128, 512),
                            (xs[:, 7, 0:512], 768, 128, 512)])
            st["tapped"] = True
            return
    st["pbank_free"] = pbank_free
    st["wring_free_t"] = b.val("T")
    b.spacer = old_spacer



def _stage_weights(inputs):
    w = {}
    w["wqkv"] = np.ascontiguousarray(
        np.asarray(inputs["L0_ffn_Wqkv"], np.float32).reshape(DEPTH, NCH, 128, 3 * D))
    w["wo"] = np.ascontiguousarray(
        np.asarray(inputs["L0_ffn_Wo"], np.float32).reshape(DEPTH, NCH, 128, D))
    W1 = np.asarray(inputs["L0_ffn_W1"], np.float32)
    W1p = np.zeros((DEPTH, 2, NIC * 128, D), np.float32)
    W1p[:, 0, :2730] = W1[:, :, :2730].transpose(0, 2, 1)
    W1p[:, 1, :2730] = W1[:, :, 2730:].transpose(0, 2, 1)
    W1p = W1p.reshape(DEPTH, 2, NIC, 128, NCH, 128).transpose(0, 1, 2, 4, 5, 3)
    w["w1"] = np.ascontiguousarray(W1p)
    W2 = np.asarray(inputs["L0_ffn_W2"], np.float32)
    W2p = np.zeros((DEPTH, NIC * 128, D), np.float32)
    W2p[:, :2730] = W2
    w["w2"] = np.ascontiguousarray(W2p.reshape(DEPTH, NIC, 128, D))
    qsr = np.asarray(inputs["L0_ffn_q_scale"], np.float32)
    ksr = np.asarray(inputs["L0_ffn_k_scale"], np.float32)
    w["qsr"] = np.ascontiguousarray(
        np.concatenate([qsr, qsr[:, 32:], qsr[:, :32]], axis=1))
    w["ksr"] = np.ascontiguousarray(
        np.concatenate([ksr, ksr[:, 32:], ksr[:, :32]], axis=1))
    w["gb"] = np.ascontiguousarray(
        np.stack([inputs["L0_ffn_ln_g"], inputs["L0_ffn_ln_b"]], axis=1).astype(np.float32))
    b1 = np.asarray(inputs["L0_ffn_b1"], np.float32)
    b1p = np.zeros((DEPTH, 2, NIC * 128), np.float32)
    b1p[:, 0, :2730] = b1[:, :2730]
    b1p[:, 1, :2730] = b1[:, 2730:]
    w["b1p"] = np.ascontiguousarray(
        b1p.reshape(DEPTH, 2, NIC, 128).transpose(0, 3, 1, 2))
    w["b2c"] = np.ascontiguousarray(
        np.asarray(inputs["L0_ffn_b2"], np.float32).reshape(DEPTH, NCH, 128)
        .transpose(0, 2, 1))
    for nm, key in (("owq", "L0_attn_Wq"), ("owk", "L0_attn_Wk"),
                    ("owv", "L0_attn_Wv"), ("owo", "L0_attn_Wo")):
        w[nm] = np.ascontiguousarray(
            np.asarray(inputs[key], np.float32).reshape(NCH, 128, D))
    w["owb"] = np.ascontiguousarray(
        np.asarray(inputs["L0_attn_Wb"], np.float32).reshape(NCH, 128, H))
    w["obb"] = np.asarray(inputs["L0_attn_bb"], np.float32).reshape(1, H)
    tabs, LT, maskR, UT = host_tables()
    w["tabs"] = tabs
    w["mlt"] = LT
    w["maskR"] = maskR
    w["mut"] = UT
    return w


def kernel(**inputs):
    x = np.asarray(inputs["L0_attn_x"], np.float32)
    ncores = 8
    w = _stage_weights(inputs)
    nc = build_program()
    in_maps = []
    for c in range(ncores):
        m = dict(w)
        m["x"] = np.ascontiguousarray(x[c].T)
        in_maps.append(m)
    res = run_bass_kernel_spmd(nc, in_maps, core_ids=list(range(ncores)))
    out = np.stack([np.asarray(r["o"]).T for r in res.results], axis=0)
    return out.astype(np.float32)


if __name__ == "__main__":
    build_program(verbose=True)
    print("built ok")
